# version 01
# baseline (speedup 1.0000x reference)
import numpy as np
import concourse.bass as bass
import concourse.mybir as mybir
from concourse.bass_utils import run_bass_kernel_spmd
from contextlib import ExitStack

F32 = mybir.dt.float32
BF16 = mybir.dt.bfloat16
I32 = mybir.dt.int32
AF = mybir.ActivationFunctionType
AX = mybir.AxisListType
ALU = mybir.AluOpType

B, T, D = 4, 4096, 1024
FCONSTS = [1e-5, 1.0, -0.5, -1.0, 0.5, 0.1, 0.9, 1.9, 1.0 / 32.0, 0.01, 1e-9,
           1.0 / 1024.0, 1.0 / 2048.0, 0.0, 2.0, 3.0]
TPC = 2048
NT = 4
TT = 512
DI = 2048
DH = 4096
NS = 16
EPS = 1e-5

ALL_PHASES = ["p1", "p2", "p3", "p4", "p5", "p6"]
DEBUG_P2 = False
STRICT_SYNC = True
RACE_DETECT = True
GROUPS = [ALL_PHASES]

NDMASEM = 8
ENG_OF_KIND = {"mm": "tensor", "dve": "vector", "act": "scalar", "pool": "gpsimd",
               "gdma": "gpsimd", "sdma": "sync", "adma": "scalar", "pdma": "gpsimd"}
DMA_KINDS = ("gdma", "sdma", "adma")


class Ctx:
    def __init__(self, nc, stack):
        self.nc = nc
        self.stack = stack
        self.sems = {}
        self.counts = {}
        self.rr = {k: 0 for k in DMA_KINDS}
        self.highwater = {}
        for k in ("mm", "dve", "act", "pool", "pdma"):
            self.sems[k] = stack.enter_context(nc.semaphore("S_" + k))
            self.counts[k] = 0
        for k in DMA_KINDS:
            for j in range(NDMASEM):
                n = f"{k}{j}"
                self.sems[n] = stack.enter_context(nc.semaphore("S_" + n))
                self.counts[n] = 0
        self.t = {}

    def sb(self, stack, name, shape, dtype):
        h = stack.enter_context(self.nc.sbuf_tensor(name, shape, dtype))
        self.t[name] = h
        return h

    def ps(self, stack, name, shape, dtype=F32):
        h = stack.enter_context(self.nc.psum_tensor(name, shape, dtype))
        self.t[name] = h
        return h


_REGN = [0]


def _rn(base):
    _REGN[0] += 1
    return f"{base}{_REGN[0]}"


def eng_of(sem_name):
    for k, e in ENG_OF_KIND.items():
        if sem_name.startswith(k):
            return e
    raise KeyError(sem_name)


class Plan:

    def __init__(self, C):
        self.C = C
        self.ops = []

    def op(self, kind, fn, deps=(), inc=False, manual=False):
        C = self.C
        if kind in DMA_KINDS:
            j = C.rr[kind]
            C.rr[kind] = (j + 1) % NDMASEM
            sem_name = f"{kind}{j}"
            amt = 16
            inc = True
            prev = (sem_name, C.counts[sem_name]) if C.counts[sem_name] else None
            deps = tuple(deps) + (prev,)
        else:
            sem_name = kind
            amt = 1
        ev = None
        if inc:
            C.counts[sem_name] += amt
            ev = (sem_name, C.counts[sem_name])
        self.ops.append((sem_name, fn, tuple(d for d in deps if d is not None), ev,
                         0 if (not inc or manual) else amt))
        return ev

    def run(self):
        C = self.C
        by_eng = {}
        for rec in self.ops:
            by_eng.setdefault(eng_of(rec[0]), []).append(rec)
        with C.nc.Block() as block:
            for eng_name in ("gpsimd", "sync", "tensor", "scalar", "vector"):
                recs = by_eng.get(eng_name)
                if not recs:
                    continue

                def body(eng, recs=recs, eng_name=eng_name):
                    hw = C.highwater.setdefault(eng_name, {})
                    for (sem_name, fn, deps, ev, amt) in recs:
                        for (dk, dv) in deps:
                            is_dma = any(dk.startswith(p) for p in DMA_KINDS) or dk == "pdma"
                            if (not STRICT_SYNC and eng_of(dk) == eng_name
                                    and not is_dma):
                                continue
                            if hw.get(dk, 0) >= dv:
                                continue
                            eng.wait_ge(C.sems[dk], dv)
                            hw[dk] = dv
                        inst = fn(eng)
                        if amt:
                            inst.then_inc(C.sems[sem_name], amt)
                getattr(block, eng_name)(body)


W_IMAJOR = {
    "L0_attn_W_in": (1024, 4096),
    "L0_attn_Wd": (2048, 2048),
    "L0_attn_W_out": (2048, 1024),
}
W_OMAJOR = {
    "L0_ffn_w1_w": (4096, 1024),
    "L0_ffn_w2_w": (4096, 1024),
    "L0_ffn_h1_w": (4096, 4096),
    "L0_ffn_h2_w": (4096, 4096),
    "L0_ffn_w3_w": (1024, 4096),
    "L1_ffn_W_fc": (4096, 1024),
    "L1_ffn_W_proj": (1024, 4096),
}
PHASE_WEIGHTS = {
    "p1": ["L0_ffn_w1_w", "L0_ffn_w2_w", "L0_ffn_h1_w", "L0_ffn_h2_w", "L0_ffn_w3_w"],
    "p2": ["L0_attn_W_in", "L0_attn_Wd", "L0_attn_W_out"],
    "p3": [],
    "p4": ["L1_ffn_W_fc", "L1_ffn_W_proj"],
    "p5": [],
    "p6": [],
}


def emit_prepass(C, pl, names, tscr, tev):
    nc = C.nc
    out = {}
    for name in names:
        ext = C.t["ext_" + name]
        if name in W_IMAJOR:
            A, Bb = W_IMAJOR[name]
            stash = nc.dram_tensor("st_" + name, [A, Bb], BF16)
            step = max(1, min(A, (8 << 20) // (Bb * 4)))
            ev = None
            r = 0
            while r < A:
                rr = min(step, A - r)
                ev = pl.op("gdma", lambda g, s=stash, e=ext, r=r, rr=rr:
                           g.dma_start(s[r:r + rr, :], e[r:r + rr, :]), deps=[ev])
                r += rr
            out[name] = (stash, (ev,))
        else:
            O, I = W_OMAJOR[name]
            stash_n = nc.dram_tensor("sn_" + name, [O, I], BF16)
            stash_t = nc.dram_tensor("st_" + name, [I, O], BF16)
            step = max(1, min(O, (8 << 20) // (I * 4)))
            cast_ev = None
            r = 0
            while r < O:
                rr = min(step, O - r)
                cast_ev = pl.op("gdma", lambda g, s=stash_n, e=ext, r=r, rr=rr:
                                g.dma_start(s[r:r + rr, :], e[r:r + rr, :]),
                                deps=[cast_ev])
                r += rr
            last = None
            for ic in range(I // 128):
                bi = ic % 2
                ld = pl.op("sdma", lambda s, sn=stash_n, ic=ic, bi=bi, O=O:
                           s.dma_start(tscr[bi][:, 0:O], sn[:, ic * 128:(ic + 1) * 128],
                                       transpose=True),
                           deps=[cast_ev, tev[bi]])
                st = pl.op("gdma", lambda g, stt=stash_t, ic=ic, bi=bi, O=O:
                           g.dma_start(stt[ic * 128:(ic + 1) * 128, :], tscr[bi][:, 0:O]),
                           deps=[ld])
                tev[bi] = st
                last = st
            out[name] = (stash_t, (tev[0], tev[1]))
    return out


def emit_ln(C, pl, P, src_fn, n_dc, tw, dst_fn, src_is_f32, deps=(),
            gamma=None, beta=None, dst_war=()):
    xb = C.t["XB"]
    sq = C.t["XSQ"]
    rows = C.t["ROWS"]
    mu = rows[0:1, 0 * TT:0 * TT + tw]
    m2 = rows[0:1, 1 * TT:1 * TT + tw]
    var = m2
    rstd = rows[0:1, 2 * TT:2 * TT + tw]
    bro = rows[0:1, 3 * TT:3 * TT + tw]

    deps = tuple(deps)
    mm_ev = None
    mms = []
    for dc in range(n_dc):
        bi = dc % 2
        war = (mms[-2],) if len(mms) >= 2 else ()
        if src_is_f32:
            e1 = pl.op("dve", lambda v, dc=dc, bi=bi: v.tensor_copy(
                xb[:, bi, 0:tw], src_fn(dc)), deps=deps + war, inc=True)
            sb_fn = (lambda bi=bi: xb[:, bi, 0:tw])
        else:
            e1 = None
            sb_fn = (lambda dc=dc: src_fn(dc))
        e2 = pl.op("dve", lambda v, bi=bi, f=sb_fn: v.tensor_tensor(
            sq[:, bi, 0:tw], f(), f(), ALU.mult),
            deps=deps + war + ((e1,) if e1 else ()), inc=True)
        mm_ev = pl.op("mm", lambda t, dc=dc, bi=bi, f=sb_fn, n=n_dc, tw=tw: (
            t.matmul(P["st0"][0:1, 0:tw], C.t["ONES_BF"][:, 0:1], f(),
                     start=(dc == 0), stop=(dc == n - 1)),
            t.matmul(P["st1"][0:1, 0:tw], C.t["ONES_BF"][:, 0:1], sq[:, bi, 0:tw],
                     start=(dc == 0), stop=(dc == n - 1)))[-1],
            deps=((e1,) if e1 else ()) + (e2, P.get("st_free")), inc=True)
        mms.append(mm_ev)

    r1 = pl.op("dve", lambda v: v.tensor_scalar(mu, P["st0"][0:1, 0:tw], 1.0 / D, None,
                                                ALU.mult), deps=[mm_ev], inc=True)
    r2 = pl.op("dve", lambda v: v.tensor_scalar(m2, P["st1"][0:1, 0:tw], 1.0 / D, None,
                                                ALU.mult), deps=[mm_ev], inc=True)
    r3 = pl.op("dve", lambda v: v.tensor_tensor(bro, mu, mu, ALU.mult), deps=[r1], inc=True)
    r4 = pl.op("dve", lambda v: v.tensor_tensor(var, m2, bro, ALU.subtract),
               deps=[r2, r3], inc=True)
    r5 = pl.op("act", lambda s: s.activation(var, var, AF.Ln, bias=EPS), deps=[r4], inc=True)
    r6 = pl.op("act", lambda s: s.activation(rstd, var, AF.Exp, scale=-0.5),
               deps=[r5], inc=True)
    r7 = pl.op("dve", lambda v: v.tensor_tensor(bro, mu, rstd, ALU.mult), deps=[r6], inc=True)
    if gamma is None:
        r7 = pl.op("dve", lambda v: v.tensor_scalar(bro, bro, -1.0, None, ALU.mult),
                   deps=[r7], inc=True)
    b1 = pl.op("mm", lambda t: t.matmul(P["bc0"][:, 0:tw], C.t["ONE1_F"][0:1, :], rstd),
               deps=[r6, P.get("bc_free")], inc=True)
    b2 = pl.op("mm", lambda t: t.matmul(P["bc1"][:, 0:tw], C.t["ONE1_F"][0:1, :], bro),
               deps=[r7, P.get("bc_free")], inc=True)

    scr = C.t["SCR"]
    apply_evs = []
    prev = None
    for dc in range(n_dc):
        a1 = pl.op("dve", lambda v, dc=dc: v.tensor_tensor(
            scr[:, 0:tw], src_fn(dc), P["bc0"][:, 0:tw], ALU.mult),
            deps=[b1, prev], inc=True)
        if gamma is None:
            a2 = pl.op("dve", lambda v, dc=dc: v.tensor_tensor(
                dst_fn(dc), scr[:, 0:tw], P["bc1"][:, 0:tw], ALU.add),
                deps=[a1, b2] + list(dst_war), inc=True)
        else:
            a15 = pl.op("dve", lambda v, dc=dc: v.tensor_tensor(
                scr[:, 0:tw], scr[:, 0:tw], P["bc1"][:, 0:tw], ALU.subtract),
                deps=[a1, b2], inc=True)
            a2 = pl.op("dve", lambda v, dc=dc: v.tensor_scalar(
                dst_fn(dc), scr[:, 0:tw], gamma(dc), beta(dc), ALU.mult, ALU.add),
                deps=[a15] + list(dst_war), inc=True)
        prev = a2
        apply_evs.append(a2)
    P["st_free"] = r2
    P["bc_free"] = apply_evs[-1]
    return apply_evs


def emit_wmm(C, pl, stashT_ev, I, O, og, wbuf, wev, rhs_fn, rhs_deps, bk,
             bank_ids, consume, tw, oc_range=None, dma_kind="adma"):
    stashT, stash_ready = stashT_ev
    ICn = I // 128
    oc0, oc1 = (0, O // 128) if oc_range is None else oc_range
    og_oc = og // 128
    rview = stashT.rearrange("(ic p) o -> p ic o", p=128)
    evs = []
    for gi in range(oc0 // og_oc, (oc1 + og_oc - 1) // og_oc):
        bi = gi % 2
        ld = pl.op(dma_kind, lambda e, gi=gi, bi=bi: e.dma_start(
            wbuf[bi][:, :, 0:og], rview[:, :, gi * og:(gi + 1) * og]),
            deps=[wev[bi]] + list(stash_ready))
        last = None
        for oj in range(og_oc):
            oc = gi * og_oc + oj
            if oc < oc0 or oc >= oc1:
                continue
            nb = bank_ids[(oc - oc0) % len(bank_ids)]
            bview, bfree = bk[nb]
            mm_ev = pl.op("mm", lambda t, bi=bi, oj=oj, bview=bview: [
                t.matmul(bview[:, 0:tw], wbuf[bi][:, ic, oj * 128:(oj + 1) * 128],
                         rhs_fn(ic), start=(ic == 0), stop=(ic == ICn - 1))
                for ic in range(ICn)][-1],
                deps=tuple(rhs_deps) + (ld, bfree), inc=True)
            cev = consume(pl, oc, bview, mm_ev)
            bk[nb][1] = cev
            evs.append(cev)
            last = cev
        if last is not None:
            wev[bi] = last
    return evs


def emit_p_l0ffn(C, stash, tag):
    with ExitStack() as sk:
        lnx = C.sb(sk, f"{tag}_lnx", [128, 8, TT], BF16)
        hbuf = C.sb(sk, f"{tag}_h", [128, 32, TT], BF16)
        gbuf = C.sb(sk, f"{tag}_g", [128, 32, TT], BF16)
        scr2 = C.sb(sk, f"{tag}_scr2", [128, TT], F32)
        wb_a = [C.sb(sk, f"{tag}_wa{i}", [128, 8, 256], BF16) for i in range(2)]
        wb_b = [C.sb(sk, f"{tag}_wb{i}", [128, 8, 256], BF16) for i in range(2)]
        wb_h = [C.sb(sk, f"{tag}_wh{i}", [128, 32, 128], BF16) for i in range(2)]
        P = {"st0": C.ps(sk, f"{tag}_st0", [128, 512]),
             "st1": C.ps(sk, f"{tag}_st1", [128, 512]),
             "bc0": C.ps(sk, f"{tag}_bc0", [128, 512]),
             "bc1": C.ps(sk, f"{tag}_bc1", [128, 512]),
             "st_free": None, "bc_free": None}
        bk = {i: [C.ps(sk, f"{tag}_mm{i}", [128, 512]), None] for i in range(4)}

        stream = C.t["STREAM"]
        s1c, s2c = C.t["w1s_c"], C.t["w2s_c"]
        h1c, h2c = C.t["h1s_c"], C.t["h2s_c"]
        s3h = C.t["w3sh_c"]
        st_w1 = stash["L0_ffn_w1_w"]
        st_w2 = stash["L0_ffn_w2_w"]
        st_h1 = stash["L0_ffn_h1_w"]
        st_h2 = stash["L0_ffn_h2_w"]
        st_w3 = stash["L0_ffn_w3_w"]

        pl = Plan(C)
        wev_a = [None, None]
        wev_b = [None, None]
        wev_h = [None, None]
        rv1 = st_w1[0].rearrange("(ic p) o -> p ic o", p=128)
        rv2 = st_w2[0].rearrange("(ic p) o -> p ic o", p=128)

        for tile in range(NT):
            t0 = tile * TT
            ln_evs = emit_ln(C, pl, P,
                             src_fn=lambda dc, t0=t0: stream[:, dc, t0:t0 + TT],
                             n_dc=8, tw=TT,
                             dst_fn=lambda dc: lnx[:, dc, :], src_is_f32=True)

            h_evs = []
            for gi in range(16):
                bi = gi % 2
                ld1 = pl.op("adma", lambda e, gi=gi, bi=bi: e.dma_start(
                    wb_a[bi][:, :, :], rv1[:, :, gi * 256:(gi + 1) * 256]),
                    deps=[wev_a[bi]] + list(st_w1[1]))
                ld2 = pl.op("sdma", lambda e, gi=gi, bi=bi: e.dma_start(
                    wb_b[bi][:, :, :], rv2[:, :, gi * 256:(gi + 1) * 256]),
                    deps=[wev_b[bi]] + list(st_w2[1]))
                for oj in range(2):
                    oc = gi * 2 + oj
                    nb1, nb2 = (oc % 2), 2 + (oc % 2)
                    m1 = pl.op("mm", lambda t, bi=bi, oj=oj, nb1=nb1: [
                        t.matmul(bk[nb1][0][:, :], wb_a[bi][:, ic, oj * 128:(oj + 1) * 128],
                                 lnx[:, ic, :], start=(ic == 0), stop=(ic == 7))
                        for ic in range(8)][-1],
                        deps=[ld1, bk[nb1][1]] + ln_evs, inc=True)
                    m2 = pl.op("mm", lambda t, bi=bi, oj=oj, nb2=nb2: [
                        t.matmul(bk[nb2][0][:, :], wb_b[bi][:, ic, oj * 128:(oj + 1) * 128],
                                 lnx[:, ic, :], start=(ic == 0), stop=(ic == 7))
                        for ic in range(8)][-1],
                        deps=[ld2, bk[nb2][1]] + ln_evs, inc=True)
                    a1 = pl.op("act", lambda s, oc=oc, nb1=nb1: s.activation(
                        scr2[:, :], bk[nb1][0][:, :], AF.Silu, scale=s1c[:, oc:oc + 1]),
                        deps=[m1], inc=True)
                    d1 = pl.op("dve", lambda v, oc=oc, nb2=nb2: v.scalar_tensor_tensor(
                        hbuf[:, oc, :], bk[nb2][0][:, :], s2c[:, oc:oc + 1], scr2[:, :],
                        ALU.mult, ALU.mult), deps=[m2, a1], inc=True)
                    bk[nb1][1] = a1
                    bk[nb2][1] = d1
                    h_evs.append(d1)
                    wev_a[bi], wev_b[bi] = a1, d1

            def mk_gelu(dst, scol):
                def consume(pl_, oc, bankv, mm_ev, dst=dst, scol=scol):
                    return pl_.op("act", lambda s, oc=oc, bankv=bankv: s.activation(
                        dst[:, oc, :], bankv[:, :], AF.Gelu, scale=scol[:, oc:oc + 1]),
                        deps=[mm_ev], inc=True)
                return consume

            g_evs = emit_wmm(C, pl, st_h1, DH, DH, 128, wb_h, wev_h,
                             rhs_fn=lambda ic: hbuf[:, ic, :], rhs_deps=h_evs,
                             bk=bk, bank_ids=[0, 1],
                             consume=mk_gelu(gbuf, h1c), tw=TT)
            h2_evs = emit_wmm(C, pl, st_h2, DH, DH, 128, wb_h, wev_h,
                              rhs_fn=lambda ic: gbuf[:, ic, :], rhs_deps=g_evs,
                              bk=bk, bank_ids=[2, 3],
                              consume=mk_gelu(hbuf, h2c), tw=TT)

            def consume_w3(pl_, oc, bankv, mm_ev):
                return pl_.op("dve", lambda v, oc=oc, bankv=bankv, t0=t0:
                              v.scalar_tensor_tensor(
                                  stream[:, oc, t0:t0 + TT], bankv[:, :],
                                  s3h[:, oc:oc + 1], stream[:, oc, t0:t0 + TT],
                                  ALU.mult, ALU.add), deps=[mm_ev], inc=True)
            emit_wmm(C, pl, st_w3, DH, D, 128, wb_h, wev_h,
                     rhs_fn=lambda ic: hbuf[:, ic, :], rhs_deps=h2_evs,
                     bk=bk, bank_ids=[0, 1], consume=consume_w3, tw=TT)
        pl.run()


def emit_spike_scan(C, pl, y_fn, out_fn, dc, deps, mem, spk, acc):
    e = pl.op("dve", lambda v: v.tensor_scalar(acc[:, :], y_fn(), 1.0, None, ALU.is_gt),
              deps=deps, inc=True)
    e = pl.op("dve", lambda v: v.scalar_tensor_tensor(
        mem[:, :], y_fn(), 1.9, acc[:, :], ALU.mult, ALU.subtract), deps=[e], inc=True)
    e = pl.op("dve", lambda v: v.tensor_scalar(spk[:, :], mem[:, :], 1.0, None, ALU.is_gt),
              deps=[e], inc=True)
    e = pl.op("dve", lambda v: v.tensor_tensor(acc[:, :], acc[:, :], spk[:, :], ALU.add),
              deps=[e], inc=True)
    for _ in range(3):
        e = pl.op("dve", lambda v: v.scalar_tensor_tensor(
            mem[:, :], mem[:, :], 0.9, y_fn(), ALU.mult, ALU.add), deps=[e], inc=True)
        e = pl.op("dve", lambda v: v.tensor_tensor(mem[:, :], mem[:, :], spk[:, :],
                                                   ALU.subtract), deps=[e], inc=True)
        e = pl.op("dve", lambda v: v.tensor_scalar(spk[:, :], mem[:, :], 1.0, None,
                                                   ALU.is_gt), deps=[e], inc=True)
        e = pl.op("dve", lambda v: v.tensor_tensor(acc[:, :], acc[:, :], spk[:, :],
                                                   ALU.add), deps=[e], inc=True)
    return out_fn(pl, e)


def emit_p_l1ffn(C, stash, tag, out_ext):
    with ExitStack() as sk:
        lnx = C.sb(sk, f"{tag}_lnx", [128, 8, TT], BF16)
        hb = C.sb(sk, f"{tag}_h", [128, 32, TT], BF16)
        yf = C.sb(sk, f"{tag}_y", [128, 8, TT], F32)
        rsc = C.sb(sk, f"{tag}_rsc", [128, TT], F32)
        mem = C.sb(sk, f"{tag}_mem", [128, TT], F32)
        spk = C.sb(sk, f"{tag}_spk", [128, TT], F32)
        acc = C.sb(sk, f"{tag}_acc", [128, TT], F32)
        wb_f = [C.sb(sk, f"{tag}_wf{i}", [128, 8, 512], BF16) for i in range(2)]
        wb_p = [C.sb(sk, f"{tag}_wp{i}", [128, 32, 256], BF16) for i in range(2)]
        P = {"st0": C.ps(sk, f"{tag}_st0", [128, 512]),
             "st1": C.ps(sk, f"{tag}_st1", [128, 512]),
             "bc0": C.ps(sk, f"{tag}_bc0", [128, 512]),
             "bc1": C.ps(sk, f"{tag}_bc1", [128, 512]),
             "st_free": None, "bc_free": None}
        bk = {i: [C.ps(sk, f"{tag}_mm{i}", [128, 512]), None] for i in range(2)}
        is_p6 = out_ext is not None
        if is_p6:
            stile = C.sb(sk, f"{tag}_stile", [128, 8, TT], F32)
            obuf = [C.sb(sk, f"{tag}_ob{i}", [128, 1024], F32) for i in range(2)]
            ps_tr = [C.ps(sk, f"{tag}_tr{i}", [128, 4, 128]) for i in range(2)]
        stream = C.t["STREAM"] if not is_p6 else None
        st_fc = stash["L1_ffn_W_fc"]
        st_pj = stash["L1_ffn_W_proj"]

        pl = Plan(C)
        wev_f = [None, None]
        wev_p = [None, None]
        out_dma_evs = []
        stile_ev = None
        for tile in range(NT):
            t0 = tile * TT
            if is_p6:
                stile_ev = pl.op("gdma", lambda g, t0=t0: g.dma_start(
                    stile[:, :, :],
                    C.t["STREAMD"].rearrange("d p t -> p d t")[:, :, t0:t0 + TT]),
                    deps=[stile_ev, C.t.get("_p5_done"),
                          out_dma_evs[-1] if out_dma_evs else None])
                src = lambda dc: stile[:, dc, :]
                ln_deps = (stile_ev,)
            else:
                src = lambda dc, t0=t0: stream[:, dc, t0:t0 + TT]
                ln_deps = ()
            ln_evs = emit_ln(C, pl, P, src_fn=src, n_dc=8, tw=TT,
                             dst_fn=lambda dc: lnx[:, dc, :], src_is_f32=True,
                             deps=ln_deps)

            def consume_fc(pl_, oc, bankv, mm_ev):
                a = pl_.op("act", lambda s, bankv=bankv: s.activation(
                    rsc[:, :], bankv[:, :], AF.Relu), deps=[mm_ev], inc=True)
                return pl_.op("dve", lambda v, oc=oc: v.tensor_tensor(
                    hb[:, oc, :], rsc[:, :], rsc[:, :], ALU.mult), deps=[a], inc=True)
            h_evs = emit_wmm(C, pl, st_fc, D, DH, 512, wb_f, wev_f,
                             rhs_fn=lambda ic: lnx[:, ic, :], rhs_deps=ln_evs,
                             bk=bk, bank_ids=[0, 1], consume=consume_fc, tw=TT)

            def consume_pj(pl_, oc, bankv, mm_ev):
                return pl_.op("act", lambda s, oc=oc, bankv=bankv: s.activation(
                    yf[:, oc, :], bankv[:, :], AF.Copy), deps=[mm_ev], inc=True)
            y_evs = emit_wmm(C, pl, st_pj, DH, D, 256, wb_p, wev_p,
                             rhs_fn=lambda ic: hb[:, ic, :], rhs_deps=h_evs,
                             bk=bk, bank_ids=[0, 1], consume=consume_pj, tw=TT)

            fin_evs = []
            for dc in range(8):
                if is_p6:
                    def out_fn(pl_, e, dc=dc):
                        return pl_.op("dve", lambda v, dc=dc: v.scalar_tensor_tensor(
                            stile[:, dc, :], acc[:, :], 0.1, stile[:, dc, :],
                            ALU.mult, ALU.add), deps=[e], inc=True)
                else:
                    def out_fn(pl_, e, dc=dc, t0=t0):
                        return pl_.op("dve", lambda v, dc=dc, t0=t0:
                                      v.scalar_tensor_tensor(
                                          stream[:, dc, t0:t0 + TT], acc[:, :], 0.1,
                                          stream[:, dc, t0:t0 + TT], ALU.mult, ALU.add),
                                      deps=[e], inc=True)
                fin_evs.append(emit_spike_scan(
                    C, pl, y_fn=lambda dc=dc: yf[:, dc, :], out_fn=out_fn, dc=dc,
                    deps=y_evs, mem=mem, spk=spk, acc=acc))
            if is_p6:
                ev_evac = None
                for tb in range(4):
                    bi = tb % 2
                    evac_evs = []
                    for dc in range(8):
                        slot = dc % 4
                        pb = ps_tr[dc // 4 % 2]
                        tr = pl.op("mm", lambda t, tb=tb, dc=dc, slot=slot, pb=pb:
                                   t.transpose(pb[:, slot, :],
                                               stile[:, dc, tb * 128:(tb + 1) * 128],
                                               C.t["ident_f"][:, :]),
                                   deps=fin_evs + [ev_evac, out_dma_evs[-2] if
                                                   len(out_dma_evs) >= 2 else None],
                                   inc=True)
                        ev_evac = pl.op("dve", lambda v, bi=bi, dc=dc, slot=slot, pb=pb:
                                        v.tensor_copy(obuf[bi][:, dc * 128:(dc + 1) * 128],
                                                      pb[:, slot, :]),
                                        deps=[tr], inc=True)
                        evac_evs.append(ev_evac)
                    out_dma_evs.append(pl.op("gdma", lambda g, t0=t0, tb=tb, bi=bi:
                                             g.dma_start(out_ext[t0 + tb * 128:
                                                                 t0 + (tb + 1) * 128, :],
                                                         obuf[bi][:, :]),
                                             deps=evac_evs))
        if is_p6:
            pl.op("pool", lambda g: g.nop(), deps=out_dma_evs)
        pl.run()


def emit_pair_signal(C, pl, k, data_evs):
    def fn(g, k=k):
        with g.register(_rn("foff")) as foff:
            g.reg_load(foff, C.t["auxi"][0:1, 6:7])
            g.reg_add(foff, foff, k)
            return g.dma_start(bass.AP(C.t["SH_FLAG"], foff, [[1, 1], [1, 1]]),
                               C.t["syncv"][0:1, k:k + 1])
    return pl.op("gdma", fn, deps=list(data_evs) + [C.t["_consts_ev"]])


def emit_pair_wait(C, pl, k, deps=()):
    def fn(g, k=k):
        fl = C.t["FLAG_SB"]
        with (g.register(_rn("pf")) as pf, g.register(_rn("want")) as want,
              g.register(_rn("got")) as got, g.register(_rn("nd")) as nd,
              g.register(_rn("cnt")) as cnt):
            g.reg_load(pf, C.t["auxi"][0:1, 7:8])
            g.reg_add(pf, pf, k)
            g.reg_load(want, C.t["syncv"][0:1, k:k + 1])
            g.reg_mov(nd, 1)
            g.reg_mov(cnt, C.counts["pdma"])
            with g.While(nd):
                g.reg_add(cnt, cnt, 16)
                g.dma_start(fl[0:1, 0:1],
                            bass.AP(C.t["SH_FLAG"], pf, [[1, 1], [1, 1]])
                            ).then_inc(C.sems["pdma"], 16)
                g.wait_ge(C.sems["pdma"], cnt)
                g.reg_load(got, fl[0:1, 0:1])
                g.reg_mov(nd, 1)
                with g.If_eq(got, want):
                    g.reg_mov(nd, 0)
        return g.sem_inc(C.sems["pool"], 1)
    return pl.op("pool", fn, deps=tuple(deps) + (C.t["_consts_ev"],),
                 inc=True, manual=True)


def emit_p2(C, stash):
    nc = C.nc
    with ExitStack() as sk:
        lnx1 = C.sb(sk, "p2_lnx1", [128, 8, TT], BF16)
        lnx2 = C.sb(sk, "p2_lnx2", [128, 8, TT], BF16)
        xsp = C.sb(sk, "p2_xsp", [128, 16, TT + 3], BF16)
        xs = C.sb(sk, "p2_xs", [128, 16, TT], BF16)
        gatb = C.sb(sk, "p2_gat", [128, 16, TT], BF16)
        spf = C.sb(sk, "p2_spf", [128, TT], F32)
        spp = C.sb(sk, "p2_spp", [128, 2, TT], BF16)
        convs = C.sb(sk, "p2_convs", [128, TT], F32)
        wb_in = [C.sb(sk, f"p2_wi{i}", [128, 8, 256], BF16) for i in range(2)]
        wb_d = [C.sb(sk, f"p2_wd{i}", [128, 16, 128], BF16) for i in range(2)]
        wb_o = [C.sb(sk, f"p2_wo{i}", [128, 16, 128], BF16) for i in range(2)]
        dat = C.sb(sk, "p2_dat", [16, TT], F32)
        bxs = C.sb(sk, "p2_bx", [16, TT], F32)
        bxd = C.sb(sk, "p2_bxd", [16, TT], F32)
        z16 = C.sb(sk, "p2_z16", [16, TT], F32)
        hs = C.sb(sk, "p2_hs", [16, TPC], F32)
        acum = C.sb(sk, "p2_ac", [16, TPC], F32)
        h0e = C.sb(sk, "p2_h0e", [16, 1], F32)
        h0r = C.sb(sk, "p2_h0r", [16, 1], F32)
        hscb = C.sb(sk, "p2_hscb", [16, TT], BF16)
        halo3 = C.sb(sk, "p2_halo3", [128, 16, 3], BF16)
        hxs = C.sb(sk, "p2_hxs", [128, 16, 3], BF16)
        hln1 = C.sb(sk, "p2_hln1", [128, 8, 3], BF16)
        hln2 = C.sb(sk, "p2_hln2", [128, 8, 3], BF16)
        P = {"st0": C.ps(sk, "p2_st0", [128, 512]),
             "st1": C.ps(sk, "p2_st1", [128, 512]),
             "bc0": C.ps(sk, "p2_bc0", [128, 512]),
             "bc1": C.ps(sk, "p2_bc1", [128, 512]),
             "st_free": None, "bc_free": None}
        bk = {i: [C.ps(sk, f"p2_mm{i}", [128, 512]), None] for i in range(2)}
        deltarow = C.sb(sk, "p2_delta", [1, TT], F32)
        ms0 = C.ps(sk, "p2_ms0", [16, 512])
        ms1 = C.ps(sk, "p2_ms1", [16, 512])

        stream = C.t["STREAM"]
        st_in = stash["L0_attn_W_in"]
        st_wd = stash["L0_attn_Wd"]
        st_wo = stash["L0_attn_W_out"]
        rv_in = st_in[0].rearrange("(ic p) o -> p ic o", p=128)
        lng = C.t["lng_c"]
        lnb = C.t["lnb_c"]
        cw = C.t["convw_c"]
        cb = C.t["convb_c"]
        dvc = C.t["dvec_c"]
        bdc = C.t["bd_c"]
        bmat = C.t["BMAT_BF"]
        ctbf = C.t["CT_BF"]
        auxf = C.t["auxf"]

        if DEBUG_P2:
            C.t["DBG_YG"] = C.nc.dram_tensor("dbg_yg", [NT, 128, 16 * TT], BF16,
                                             kind="ExternalOutput")
        pl = Plan(C)
        z16_ev = pl.op("dve", lambda v: v.memset(z16[:, :], 0.0), inc=True)

        hl1 = emit_ln(C, pl, P, src_fn=lambda dc: stream[:, dc, TPC - 3:TPC],
                      n_dc=8, tw=3, dst_fn=lambda dc: hln1[:, dc, :], src_is_f32=True)
        hl2 = emit_ln(C, pl, P, src_fn=lambda dc: hln1[:, dc, :],
                      n_dc=8, tw=3, dst_fn=lambda dc: hln2[:, dc, :], src_is_f32=False,
                      gamma=lambda dc: lng[:, dc:dc + 1], beta=lambda dc: lnb[:, dc:dc + 1])
        wev_in = [None, None]
        hxs_evs = []
        for gi in range(8):
            bi = gi % 2
            ld = pl.op("adma", lambda e, gi=gi, bi=bi: e.dma_start(
                wb_in[bi][:, :, :], rv_in[:, :, gi * 256:(gi + 1) * 256]),
                deps=[wev_in[bi]] + list(st_in[1]))
            for oj in range(2):
                oc = gi * 2 + oj
                nb = oc % 2
                mm_ev = pl.op("mm", lambda t, bi=bi, oj=oj, nb=nb: [
                    t.matmul(bk[nb][0][:, 0:3], wb_in[bi][:, ic, oj * 128:(oj + 1) * 128],
                             hln2[:, ic, :], start=(ic == 0), stop=(ic == 7))
                    for ic in range(8)][-1],
                    deps=[ld, bk[nb][1]] + hl2, inc=True)
                cev = pl.op("dve", lambda v, oc=oc, nb=nb: v.tensor_copy(
                    hxs[:, oc, :], bk[nb][0][:, 0:3]), deps=[mm_ev], inc=True)
                bk[nb][1] = cev
                hxs_evs.append(cev)
                wev_in[bi] = cev
        def fn_send_halo(g):
            with g.register(_rn("hoff")) as hoff:
                g.reg_load(hoff, C.t["auxi"][0:1, 2:3])
                return g.dma_start(bass.AP(C.t["SH_HALO"], hoff, [[48, 128], [1, 48]]),
                                   hxs.rearrange("p a b -> p (a b)")[:, :])
        e_send = pl.op("gdma", fn_send_halo, deps=hxs_evs)
        emit_pair_signal(C, pl, 0, [e_send])
        w0 = emit_pair_wait(C, pl, 0)

        def fn_recv_halo(g):
            with g.register(_rn("hoff")) as hoff:
                g.reg_load(hoff, C.t["auxi"][0:1, 3:4])
                return g.dma_start(halo3.rearrange("p a b -> p (a b)")[:, :],
                                   bass.AP(C.t["SH_HALO"], hoff, [[48, 128], [1, 48]]))
        e_recv = pl.op("gdma", fn_recv_halo, deps=[w0])
        e_mask = pl.op("dve", lambda v: v.tensor_scalar(
            halo3.rearrange("p a b -> p (a b)")[:, :],
            halo3.rearrange("p a b -> p (a b)")[:, :],
            auxf[:, 0:1], None, ALU.mult), deps=[e_recv], inc=True)

        wev_d = [None, None]
        hs_last = None
        ac_last = None
        scan_evs = []
        xsp_tail_ev = None
        spill_evs = []
        sp_l2 = sp_gate = sp_xs = None
        for tile in range(NT):
            t0 = tile * TT
            l1 = emit_ln(C, pl, P, src_fn=lambda dc, t0=t0: stream[:, dc, t0:t0 + TT],
                         n_dc=8, tw=TT, dst_fn=lambda dc: lnx1[:, dc, :], src_is_f32=True,
                         dst_war=[sp_l2])
            l2 = emit_ln(C, pl, P, src_fn=lambda dc: lnx1[:, dc, :],
                         n_dc=8, tw=TT, dst_fn=lambda dc: lnx2[:, dc, :], src_is_f32=False,
                         gamma=lambda dc: lng[:, dc:dc + 1],
                         beta=lambda dc: lnb[:, dc:dc + 1])
            sp_l2 = pl.op("gdma", lambda g, tile=tile: g.dma_start(
                C.t["SPILL_LNX2"][tile, :, :],
                lnx1.rearrange("p a b -> p (a b)")[:, :]),
                deps=l2 + l1)

            if tile == 0:
                e_h = pl.op("dve", lambda v: v.tensor_copy(
                    xsp[:, :, 0:3], halo3[:, :, :]),
                    deps=[e_mask, xsp_tail_ev], inc=True)
            else:
                e_h = pl.op("dve", lambda v: v.tensor_copy(
                    xsp[:, :, 0:3], xsp[:, :, TT:TT + 3]),
                    deps=[xsp_tail_ev], inc=True)

            xs_evs = []
            gate_evs = []
            for gi in range(16):
                bi = gi % 2
                ld = pl.op("adma", lambda e, gi=gi, bi=bi: e.dma_start(
                    wb_in[bi][:, :, :], rv_in[:, :, gi * 256:(gi + 1) * 256]),
                    deps=[wev_in[bi]] + list(st_in[1]))
                for oj in range(2):
                    oc = gi * 2 + oj
                    nb = oc % 2
                    mm_ev = pl.op("mm", lambda t, bi=bi, oj=oj, nb=nb: [
                        t.matmul(bk[nb][0][:, :], wb_in[bi][:, ic, oj * 128:(oj + 1) * 128],
                                 lnx2[:, ic, :], start=(ic == 0), stop=(ic == 7))
                        for ic in range(8)][-1],
                        deps=[ld, bk[nb][1]] + l2, inc=True)
                    if oc < 16:
                        cev = pl.op("act", lambda s, oc=oc, nb=nb: s.activation(
                            xsp[:, oc, 3:3 + TT], bk[nb][0][:, :], AF.Copy),
                            deps=[mm_ev, e_h], inc=True)
                        xs_evs.append(cev)
                    else:
                        cev = pl.op("act", lambda s, oc=oc, nb=nb: s.activation(
                            gatb[:, oc - 16, :], bk[nb][0][:, :], AF.Copy),
                            deps=[mm_ev, sp_gate], inc=True)
                        gate_evs.append(cev)
                    bk[nb][1] = cev
                    wev_in[bi] = cev
            sp_gate = pl.op("gdma", lambda g, tile=tile: g.dma_start(
                C.t["SPILL_GATE"][tile, :, :],
                gatb.rearrange("p a b -> p (a b)")[:, :]), deps=gate_evs)

            conv_evs = []
            prev_cc = None
            for cc in range(16):
                e = pl.op("dve", lambda v, cc=cc: v.tensor_scalar(
                    convs[:, :], xsp[:, cc, 0:TT], cw[:, cc * 4:cc * 4 + 1], None,
                    ALU.mult), deps=xs_evs + [e_h, prev_cc], inc=True)
                for k in (1, 2, 3):
                    e = pl.op("dve", lambda v, cc=cc, k=k: v.scalar_tensor_tensor(
                        convs[:, :], xsp[:, cc, k:k + TT], cw[:, cc * 4 + k:cc * 4 + k + 1],
                        convs[:, :], ALU.mult, ALU.add), deps=[e], inc=True)
                e = pl.op("act", lambda s, cc=cc: s.activation(
                    xs[:, cc, :], convs[:, :], AF.Silu, bias=cb[:, cc:cc + 1]),
                    deps=[e, sp_xs], inc=True)
                prev_cc = e
                conv_evs.append(e)
            xsp_tail_ev = conv_evs[-1]
            sp_xs = pl.op("gdma", lambda g, tile=tile: g.dma_start(
                C.t["SPILL_XS"][tile, :, :],
                xs.rearrange("p a b -> p (a b)")[:, :]), deps=conv_evs)
            spill_evs += [sp_l2, sp_gate, sp_xs]

            rv_d = st_wd[0].rearrange("(ic p) o -> p ic o", p=128)
            dsum_ev = None
            sp_prev = [None, None]
            for gi in range(16):
                bi = gi % 2
                ld = pl.op("adma", lambda e, gi=gi, bi=bi: e.dma_start(
                    wb_d[bi][:, :, :], rv_d[:, :, gi * 128:(gi + 1) * 128]),
                    deps=[wev_d[bi]] + list(st_wd[1]))
                for oj in range(1):
                    oc = gi
                    nb = oc % 2
                    mm_ev = pl.op("mm", lambda t, bi=bi, oj=oj, nb=nb: [
                        t.matmul(bk[nb][0][:, :], wb_d[bi][:, ic, oj * 128:(oj + 1) * 128],
                                 xs[:, ic, :], start=(ic == 0), stop=(ic == 15))
                        for ic in range(16)][-1],
                        deps=[ld, bk[nb][1]] + conv_evs, inc=True)
                    e1 = pl.op("act", lambda s, oc=oc, nb=nb: s.activation(
                        spf[:, :], bk[nb][0][:, :], AF.Exp, bias=bdc[:, oc:oc + 1]),
                        deps=[mm_ev], inc=True)
                    si = oc % 2
                    e2 = pl.op("act", lambda s, si=si: s.activation(
                        spp[:, si, :], spf[:, :], AF.Ln, bias=1.0),
                        deps=[e1, sp_prev[si]], inc=True)
                    dsum_ev = pl.op("mm", lambda t, si=si, oc=oc: t.matmul(
                        P["st0"][0:1, :], C.t["ONES_BF"][:, 0:1], spp[:, si, :],
                        start=(oc == 0), stop=(oc == 15)),
                        deps=[e2, P["st_free"] if oc == 0 else None], inc=True)
                    sp_prev[si] = dsum_ev
                    bk[nb][1] = e1
                    wev_d[bi] = e1
            delta = deltarow[0:1, :]
            d_ev = pl.op("dve", lambda v: v.tensor_scalar(
                delta, P["st0"][0:1, :], 1.0 / DI, None, ALU.mult),
                deps=[dsum_ev], inc=True)
            P["st_free"] = d_ev

            m_da = pl.op("mm", lambda t: t.matmul(ms0[:, :], C.t["A_ROW"][0:1, :], delta),
                         deps=[d_ev, C.t.get("_ms0_free")], inc=True)
            e_da = pl.op("act", lambda s: s.activation(dat[:, :], ms0[:, :], AF.Exp),
                         deps=[m_da], inc=True)
            m_db = pl.op("mm", lambda t: t.matmul(ms1[:, :], C.t["ONE1_F"][0:1, 0:16],
                                                  delta),
                         deps=[d_ev, C.t.get("_ms1_free")], inc=True)
            m_bx = pl.op("mm", lambda t: [
                t.matmul(ms0[:, :], bmat[:, cc * 16:(cc + 1) * 16], xs[:, cc, :],
                         start=(cc == 0), stop=(cc == 15)) for cc in range(16)][-1],
                deps=conv_evs + [e_da], inc=True)
            e_bx = pl.op("act", lambda s: s.activation(bxs[:, :], ms0[:, :], AF.Copy),
                         deps=[m_bx], inc=True)
            C.t["_ms0_free"] = e_bx
            e_bxd = pl.op("dve", lambda v: v.tensor_tensor(bxd[:, :], bxs[:, :],
                                                           ms1[:, :], ALU.mult),
                          deps=[e_bx, m_db], inc=True)
            C.t["_ms1_free"] = e_bxd

            hs_last = pl.op("dve", lambda v, t0=t0, tile=tile: v.tensor_tensor_scan(
                hs[:, t0:t0 + TT], dat[:, :], bxd[:, :],
                (0.0 if tile == 0 else hs[:, t0 - 1:t0]), ALU.mult, ALU.add),
                deps=[e_da, e_bxd, hs_last], inc=True)
            ac_last = pl.op("dve", lambda v, t0=t0, tile=tile: v.tensor_tensor_scan(
                acum[:, t0:t0 + TT], dat[:, :], z16[:, :],
                (1.0 if tile == 0 else acum[:, t0 - 1:t0]), ALU.mult, ALU.add),
                deps=[e_da, z16_ev, ac_last], inc=True)
            scan_evs += [hs_last, ac_last]

        def fn_send_h(g):
            with g.register(_rn("ho")) as ho:
                g.reg_load(ho, C.t["auxi"][0:1, 4:5])
                return g.dma_start(bass.AP(C.t["SH_HEND"], ho, [[1, 16], [1, 1]]),
                                   hs[:, TPC - 1:TPC])
        e_sh = pl.op("gdma", fn_send_h, deps=[hs_last])
        emit_pair_signal(C, pl, 1, [e_sh])
        w1 = emit_pair_wait(C, pl, 1)

        def fn_recv_h(g):
            with g.register(_rn("ho")) as ho:
                g.reg_load(ho, C.t["auxi"][0:1, 5:6])
                return g.dma_start(h0r[:, :],
                                   bass.AP(C.t["SH_HEND"], ho, [[1, 16], [1, 1]]))
        e_rh = pl.op("gdma", fn_recv_h, deps=[w1])
        e_h0 = pl.op("dve", lambda v: v.tensor_tensor(
            h0e[:, :], h0r[:, :], auxf[0:16, 0:1], ALU.mult), deps=[e_rh], inc=True)

        wev_o = [None, None]
        rv_o = st_wo[0].rearrange("(ic p) o -> p ic o", p=128)
        tile_done = None
        for tile in range(NT):
            t0 = tile * TT
            rl_xs = pl.op("gdma", lambda g, tile=tile: g.dma_start(
                xs.rearrange("p a b -> p (a b)")[:, :],
                C.t["SPILL_XS"][tile, :, :]), deps=spill_evs + [tile_done])
            rl_gat = pl.op("gdma", lambda g, tile=tile: g.dma_start(
                gatb.rearrange("p a b -> p (a b)")[:, :],
                C.t["SPILL_GATE"][tile, :, :]), deps=spill_evs + [tile_done])
            rl_ln = pl.op("gdma", lambda g, tile=tile: g.dma_start(
                lnx2.rearrange("p a b -> p (a b)")[:, :],
                C.t["SPILL_LNX2"][tile, :, :]), deps=spill_evs + [tile_done])
            e_hsc = pl.op("dve", lambda v, t0=t0: v.scalar_tensor_tensor(
                hscb[:, :], acum[:, t0:t0 + TT], h0e[:, 0:1], hs[:, t0:t0 + TT],
                ALU.mult, ALU.add), deps=[e_h0] + scan_evs, inc=True)

            yg_evs = []
            prev_y = None
            for cc in range(16):
                nb = cc % 2
                m = pl.op("mm", lambda t, cc=cc, nb=nb: t.matmul(
                    bk[nb][0][:, :], ctbf[:, cc * 128:(cc + 1) * 128], hscb[:, :]),
                    deps=[e_hsc, bk[nb][1], C.t["_consts_ev"]], inc=True)
                e_y = pl.op("dve", lambda v, cc=cc, nb=nb: v.scalar_tensor_tensor(
                    convs[:, :], xs[:, cc, :], dvc[:, cc:cc + 1], bk[nb][0][:, :],
                    ALU.mult, ALU.add), deps=[m, rl_xs, prev_y], inc=True)
                e_sg = pl.op("act", lambda s, cc=cc: s.activation(
                    spf[:, :], gatb[:, cc, :], AF.Silu), deps=[rl_gat, prev_y], inc=True)
                e_yg = pl.op("dve", lambda v, cc=cc: v.tensor_tensor(
                    gatb[:, cc, :], convs[:, :], spf[:, :], ALU.mult),
                    deps=[e_y, e_sg], inc=True)
                bk[nb][1] = e_y
                prev_y = e_yg
                yg_evs.append(e_yg)

            if DEBUG_P2:
                pl.op("gdma", lambda g, tile=tile: g.dma_start(
                    C.t["DBG_YG"][tile, :, :],
                    gatb.rearrange("p a b -> p (a b)")[:, :]), deps=yg_evs)
            for gi in range(8):
                bi = gi % 2
                ld = pl.op("adma", lambda e, gi=gi, bi=bi: e.dma_start(
                    wb_o[bi][:, :, :], rv_o[:, :, gi * 128:(gi + 1) * 128]),
                    deps=[wev_o[bi]] + list(st_wo[1]))
                for oj in range(1):
                    oc = gi
                    nb = oc % 2
                    m = pl.op("mm", lambda t, bi=bi, oj=oj, nb=nb: [
                        t.matmul(bk[nb][0][:, :], wb_o[bi][:, ic, oj * 128:(oj + 1) * 128],
                                 gatb[:, ic, :], start=(ic == 0), stop=(ic == 15))
                        for ic in range(16)][-1],
                        deps=[ld, bk[nb][1]] + yg_evs, inc=True)
                    e1 = pl.op("dve", lambda v, oc=oc, nb=nb: v.tensor_tensor(
                        convs[:, :], bk[nb][0][:, :], lnx2[:, oc, :], ALU.add),
                        deps=[m, rl_ln], inc=True)
                    e2 = pl.op("dve", lambda v, oc=oc, t0=t0: v.tensor_tensor(
                        stream[:, oc, t0:t0 + TT], stream[:, oc, t0:t0 + TT],
                        convs[:, :], ALU.add), deps=[e1], inc=True)
                    bk[nb][1] = e1
                    wev_o[bi] = e2
                    tile_done = e2
        if DEBUG_P2:
            nc = C.nc
            d_hs = nc.dram_tensor("dbg_hs", [16, TPC], F32, kind="ExternalOutput")
            d_ac = nc.dram_tensor("dbg_ac", [16, TPC], F32, kind="ExternalOutput")
            d_xs = nc.dram_tensor("dbg_xs", [NT, 128, 16 * TT], BF16, kind="ExternalOutput")
            d_gt = nc.dram_tensor("dbg_gt", [NT, 128, 16 * TT], BF16, kind="ExternalOutput")
            d_ln = nc.dram_tensor("dbg_ln", [NT, 128, 8 * TT], BF16, kind="ExternalOutput")
            d_h0 = nc.dram_tensor("dbg_h0", [16, 1], F32, kind="ExternalOutput")
            pl.op("gdma", lambda g: g.dma_start(d_hs[:, :], hs[:, :]), deps=scan_evs)
            pl.op("gdma", lambda g: g.dma_start(d_ac[:, :], acum[:, :]), deps=scan_evs)
            pl.op("gdma", lambda g: g.dma_start(d_xs[:, :, :], C.t["SPILL_XS"][:, :, :]),
                  deps=spill_evs)
            pl.op("gdma", lambda g: g.dma_start(d_gt[:, :, :], C.t["SPILL_GATE"][:, :, :]),
                  deps=spill_evs)
            pl.op("gdma", lambda g: g.dma_start(d_ln[:, :, :], C.t["SPILL_LNX2"][:, :, :]),
                  deps=spill_evs)
            pl.op("gdma", lambda g: g.dma_start(d_h0[:, :], h0e[:, :]), deps=[e_h0])
        pl.run()


def emit_p5_prep(C):
    with ExitStack() as sk:
        lnk = C.sb(sk, "p5_lnk", [128, 8, TT], BF16)
        tmb = [C.sb(sk, f"p5_tmb{i}", [128, 1024], BF16) for i in range(2)]
        P = {"st0": C.ps(sk, "p5p_st0", [128, 512]),
             "st1": C.ps(sk, "p5p_st1", [128, 512]),
             "bc0": C.ps(sk, "p5p_bc0", [128, 512]),
             "bc1": C.ps(sk, "p5p_bc1", [128, 512]),
             "st_free": None, "bc_free": None}
        ps_tr = [C.ps(sk, f"p5p_tr{i}", [128, 4, 128], BF16) for i in range(2)]
        stream = C.t["STREAM"]
        kstage = C.nc.dram_tensor("KSTAGE", [TPC, D], BF16)

        pl = Plan(C)
        key_evs = []
        for tile in range(NT):
            t0 = tile * TT
            ln_evs = emit_ln(C, pl, P,
                             src_fn=lambda dc, t0=t0: stream[:, dc, t0:t0 + TT],
                             n_dc=8, tw=TT, dst_fn=lambda dc: lnk[:, dc, :],
                             src_is_f32=True)
            for tb in range(4):
                bi = tb % 2
                evac = []
                ev_e = None
                for dc in range(8):
                    pb = ps_tr[dc % 2]
                    slot = (dc // 2) % 4
                    tr = pl.op("mm", lambda t, tb=tb, dc=dc, pb=pb, slot=slot:
                               t.transpose(pb[:, slot, :],
                                           lnk[:, dc, tb * 128:(tb + 1) * 128],
                                           C.t["ident_bf"][:, :]),
                               deps=ln_evs + [ev_e, key_evs[-2] if len(key_evs) >= 2
                                              else None], inc=True)
                    ev_e = pl.op("dve", lambda v, bi=bi, dc=dc, pb=pb, slot=slot:
                                 v.tensor_copy(tmb[bi][:, dc * 128:(dc + 1) * 128],
                                               pb[:, slot, :]), deps=[tr], inc=True)
                    evac.append(ev_e)

                key_evs.append(pl.op("gdma", lambda g, t0=t0, tb=tb, bi=bi:
                                     g.dma_start(kstage[t0 + tb * 128:
                                                        t0 + (tb + 1) * 128, :],
                                                 tmb[bi][:, :]), deps=evac))
        def fn_pub(g):
            with g.register(_rn("ko")) as ko:
                g.reg_load(ko, C.t["auxi"][0:1, 0:1])
                return g.dma_start(bass.AP(C.t["SH_KEYS"], ko, [[D, TPC], [1, D]]),
                                   kstage[:, :])
        key_evs = [pl.op("gdma", fn_pub, deps=key_evs)]
        sp_evs = []
        for dc in range(8):
            sp_evs.append(pl.op("gdma", lambda g, dc=dc: g.dma_start(
                C.t["STREAMD"][dc, :, :], stream[:, dc, :]), inc=True))
        C.t["_keys_sig"] = emit_pair_signal(C, pl, 2, key_evs + sp_evs)
        C.t["_keys_evs"] = tuple(key_evs) + tuple(sp_evs)

        pl.run()


def emit_p5_main(C):
    with ExitStack() as sk:
        kfm = C.sb(sk, "p5_kfm", [128, 8, 2 * TPC], BF16)
        ktm = C.sb(sk, "p5_ktm", [128, 32, 1024], BF16)
        ebuf = C.sb(sk, "p5_e", [128, 4096], F32)
        ek = C.sb(sk, "p5_ek", [128, 4096], BF16)
        ekt = C.sb(sk, "p5_ekt", [128, 32, 128], BF16)
        att = C.sb(sk, "p5_att", [128, 1024], F32)
        stile = C.sb(sk, "p5_stile", [128, 8, 128], F32)
        zsp = C.sb(sk, "p5_zsp", [128, 8], F32)
        rc = C.sb(sk, "p5_rc", [128, 8], F32)
        wrows = C.sb(sk, "p5_wrows", [128, 2, 1024], F32)
        wrsc = C.sb(sk, "p5_wrsc", [128, 8], F32)
        wrs8 = C.sb(sk, "p5_wrs8", [8, 128], F32)
        wrow = C.sb(sk, "p5_wrow", [1, 1024], F32)
        wrsbc = C.sb(sk, "p5_wrsbc", [128, 1024], BF16)
        sc = [C.ps(sk, f"p5_sc{i}", [128, 512]) for i in range(2)]
        at = [C.ps(sk, f"p5_at{i}", [128, 512]) for i in range(2)]
        trb = [C.ps(sk, f"p5_trb{i}", [128, 4, 128], BF16) for i in range(2)]
        tra = [C.ps(sk, f"p5_tra{i}", [128, 4, 128]) for i in range(2)]

        wrsd = C.nc.dram_tensor("WRS_D", [8, 128], F32)
        pl = Plan(C)
        rs_evs = []
        lw = None
        for half in range(4):
            lw = pl.op("gdma", lambda g, half=half: g.dma_start(
                wrows[:, :, :],
                C.t["ext_L1_attn_W"].rearrange("(a p) b -> p a b", p=128)
                [:, half * 2:half * 2 + 2, :]), deps=[rs_evs[-1] if rs_evs else None])
            for j in range(2):
                rs_evs.append(pl.op("dve", lambda v, half=half, j=j: v.reduce_sum(
                    wrsc[:, half * 2 + j:half * 2 + j + 1], wrows[:, j, :], axis=AX.X),
                    deps=[lw], inc=True))
        trw = pl.op("mm", lambda t: t.transpose(sc[0][0:8, 0:128], wrsc[:, :],
                                                C.t["ident_f"][:, :]),
                    deps=rs_evs, inc=True)
        evw = pl.op("dve", lambda v: v.tensor_copy(wrs8[:, :], sc[0][0:8, 0:128]),
                    deps=[trw], inc=True)
        stw = pl.op("gdma", lambda g: g.dma_start(wrsd[:, :], wrs8[:, :]), deps=[evw])
        ldw = pl.op("gdma", lambda g: g.dma_start(
            wrow[0:1, :], wrsd.rearrange("a b -> (a b)")[None, :]), deps=[stw])
        bcw = pl.op("mm", lambda t: [
            t.matmul(sc[0][:, 0:512], C.t["ONE1_F"][0:1, :], wrow[0:1, 0:512]),
            t.matmul(sc[1][:, 0:512], C.t["ONE1_F"][0:1, :], wrow[0:1, 512:1024])][-1],
            deps=[ldw, evw], inc=True)
        wv1 = pl.op("dve", lambda v: v.tensor_copy(wrsbc[:, 0:512], sc[0][:, 0:512]),
                    deps=[bcw], inc=True)
        wv2 = pl.op("dve", lambda v: v.tensor_copy(wrsbc[:, 512:1024], sc[1][:, 0:512]),
                    deps=[bcw], inc=True)
        C.t["_wrs_ev"] = wv2
        C.t["_wrs_ev1"] = wv1
        w2 = emit_pair_wait(C, pl, 2)

        kfm_evs = []
        ktm_evs = []
        for half in range(2):
            acol = 0 if half == 0 else 1
            for dc in range(8):
                def fn_fm(s, half=half, dc=dc, acol=acol):
                    with s.register(_rn("ko")) as ko:
                        s.reg_load(ko, C.t["auxi"][0:1, acol:acol + 1])
                        s.reg_add(ko, ko, dc * 128)
                        return s.dma_start(
                            kfm[:, dc, half * TPC:(half + 1) * TPC],
                            bass.AP(C.t["SH_KEYS"], ko, [[D, TPC], [1, 128]]),
                            transpose=True)
                kfm_evs.append(pl.op("sdma", fn_fm,
                                      deps=[w2] + list(C.t["_keys_evs"])))

            def fn_tm(g, half=half, acol=acol):
                with g.register(_rn("ko")) as ko:
                    g.reg_load(ko, C.t["auxi"][0:1, acol:acol + 1])
                    return g.dma_start(
                        ktm[:, half * 16:(half + 1) * 16, :],
                        bass.AP(C.t["SH_KEYS"], ko, [[D, 128], [128 * D, 16], [1, D]]))
            ktm_evs.append(pl.op("gdma", fn_tm,
                                  deps=[w2] + list(C.t["_keys_evs"])))

        st_prev = [None] * 4
        all_stores = []
        for qt in range(16):
            q0 = qt * 128
            e_evs = []
            for ks in range(8):
                nb = ks % 2
                m = pl.op("mm", lambda t, q0=q0, ks=ks, nb=nb: [
                    t.matmul(sc[nb][:, :], kfm[:, dc, q0:q0 + 128],
                             kfm[:, dc, ks * 512:(ks + 1) * 512],
                             start=(dc == 0), stop=(dc == 7)) for dc in range(8)][-1],
                    deps=kfm_evs + [e_evs[-2] if len(e_evs) >= 2 else None,
                                    C.t["_wrs_ev"], C.t["_wrs_ev1"]], inc=True)
                e_evs.append(pl.op("act", lambda s, ks=ks, nb=nb: s.activation(
                    ebuf[:, ks * 512:(ks + 1) * 512], sc[nb][:, :], AF.Exp,
                    scale=1.0 / 32.0, accum_out=zsp[:, ks:ks + 1]), deps=[m], inc=True))
            rz = pl.op("dve", lambda v: v.reduce_sum(rc[:, 0:1], zsp[:, :], axis=AX.X),
                       deps=e_evs, inc=True)
            rt = pl.op("dve", lambda v: v.tensor_scalar(rc[:, 1:2], rc[:, 0:1], 0.01,
                                                        None, ALU.mult),
                       deps=[rz], inc=True)
            rek = pl.op("dve", lambda v: v.scalar_tensor_tensor(
                ek[:, :], ebuf[:, :], rc[:, 1:2], ebuf[:, :], ALU.is_ge, ALU.mult,
                accum_out=rc[:, 2:3]), deps=[rt] + e_evs, inc=True)
            r1 = pl.op("dve", lambda v: v.tensor_scalar(rc[:, 3:4], rc[:, 0:1], 1e-9,
                                                        None, ALU.mult), deps=[rz], inc=True)
            r2 = pl.op("dve", lambda v: v.tensor_tensor(rc[:, 3:4], rc[:, 3:4],
                                                        rc[:, 2:3], ALU.add),
                       deps=[r1, rek], inc=True)
            r3 = pl.op("dve", lambda v: v.reciprocal(rc[:, 4:5], rc[:, 3:4]),
                       deps=[r2], inc=True)
            r4 = pl.op("dve", lambda v: v.tensor_tensor(rc[:, 5:6], rc[:, 0:1],
                                                        rc[:, 2:3], ALU.subtract),
                       deps=[rek, rz], inc=True)
            r5 = pl.op("dve", lambda v: v.reciprocal(rc[:, 6:7], rc[:, 0:1]),
                       deps=[rz], inc=True)
            r6 = pl.op("dve", lambda v: v.tensor_tensor(rc[:, 5:6], rc[:, 5:6],
                                                        rc[:, 6:7], ALU.mult),
                       deps=[r4, r5], inc=True)
            r7 = pl.op("dve", lambda v: v.tensor_scalar(rc[:, 5:6], rc[:, 5:6], 0.01,
                                                        None, ALU.mult), deps=[r6], inc=True)

            ekt_evs = []
            ev_e = None
            for kg in range(8):
                pb = trb[kg % 2]
                for j in range(4):
                    kc = kg * 4 + j
                    tr = pl.op("mm", lambda t, kc=kc, j=j, pb=pb: t.transpose(
                        pb[:, j, :], ek[:, kc * 128:(kc + 1) * 128],
                        C.t["ident_bf"][:, :]),
                        deps=[rek, ekt_evs[-2] if len(ekt_evs) >= 2 else None], inc=True)
                ev_e = pl.op("dve", lambda v, kg=kg, pb=pb: v.tensor_copy(
                    ekt[:, kg * 4:(kg + 1) * 4, :].rearrange("p a b -> p (a b)"),
                    pb.rearrange("p a b -> p (a b)")[:, :]), deps=[tr], inc=True)
                ekt_evs.append(ev_e)

            att_evs = []
            for ds in range(2):
                m = pl.op("mm", lambda t, ds=ds: [
                    t.matmul(at[ds][:, :], ekt[:, kc, :],
                             ktm[:, kc, ds * 512:(ds + 1) * 512],
                             start=(kc == 0), stop=(kc == 31)) for kc in range(32)][-1],
                    deps=ekt_evs + ktm_evs + [st_prev[2 + ds]], inc=True)
                att_evs.append(pl.op("act", lambda s, ds=ds: s.activation(
                    att[:, ds * 512:(ds + 1) * 512], at[ds][:, :], AF.Copy,
                    scale=rc[:, 4:5]), deps=[m, r3], inc=True))
                st_prev[2 + ds] = att_evs[-1]
            cas = pl.op("dve", lambda v: v.scalar_tensor_tensor(
                att[:, :], wrsbc[:, :], rc[:, 5:6], att[:, :],
                ALU.mult, ALU.add),
                deps=att_evs + [r7, C.t["_wrs_ev"], C.t["_wrs_ev1"]], inc=True)

            ld_st = pl.op("gdma", lambda g, q0=q0: g.dma_start(
                stile.rearrange("p a b -> p (a b)")[:, :],
                C.t["STREAMD"].rearrange("d p t -> p d t")[:, :, q0:q0 + 128]),
                deps=[st_prev[0]] + list(C.t["_keys_evs"]))
            add_evs = []
            ev_e2 = None
            for dc in range(8):
                pb = tra[dc % 2]
                slot = (dc // 2) % 4
                tr = pl.op("mm", lambda t, dc=dc, pb=pb, slot=slot: t.transpose(
                    pb[:, slot, :], att[:, dc * 128:(dc + 1) * 128],
                    C.t["ident_f"][:, :]),
                    deps=[cas, ev_e2, add_evs[-2] if len(add_evs) >= 2 else None],
                    inc=True)
                ev_e2 = pl.op("dve", lambda v, dc=dc, pb=pb, slot=slot: v.tensor_tensor(
                    stile[:, dc, :], stile[:, dc, :], pb[:, slot, :], ALU.add),
                    deps=[tr, ld_st], inc=True)
                add_evs.append(ev_e2)
            st_st = pl.op("gdma", lambda g, q0=q0: g.dma_start(
                C.t["STREAMD"].rearrange("d p t -> p d t")[:, :, q0:q0 + 128],
                stile.rearrange("p a b -> p (a b)")[:, :]), deps=add_evs)
            st_prev[0] = st_st
            all_stores.append(st_st)
        C.t["_p5_done"] = pl.op("pool", lambda g: g.nop(), deps=all_stores, inc=True)
        pl.run()


COL_INPUTS = [
    ("lng_c", [128, 8]), ("lnb_c", [128, 8]), ("convw_c", [128, 64]),
    ("convb_c", [128, 16]), ("dvec_c", [128, 16]), ("bd_c", [128, 16]),
    ("w1s_c", [128, 32]), ("w2s_c", [128, 32]), ("h1s_c", [128, 32]),
    ("h2s_c", [128, 32]), ("w3s_c", [128, 8]), ("bmat_c", [128, 256]),
    ("auxf", [128, 2]),
]


def emit_consts(C, pl):
    evs = []
    for name, shape in COL_INPUTS:
        evs.append(pl.op("gdma", lambda g, n=name: g.dma_start(
            C.t[n][:, :], C.t["ext_" + n][:, :])))
    for name in ["ident_f", "cmat_t", "alog_r", "auxi", "syncv", "fconst"]:
        evs.append(pl.op("gdma", lambda g, n=name: g.dma_start(
            C.t[n][:, :], C.t["ext_" + n][:, :])))
    e1 = pl.op("dve", lambda v: v.memset(C.t["ONES_BF"][:, :], 1.0), inc=True)
    e2 = pl.op("dve", lambda v: v.memset(C.t["ONE1_F"][:, :], 1.0), inc=True)
    e3 = pl.op("dve", lambda v: v.tensor_copy(C.t["ident_bf"][:, :],
                                              C.t["ident_f"][:, :]),
               deps=[evs[-6]], inc=True)
    e4 = pl.op("dve", lambda v: v.tensor_scalar(C.t["w3sh_c"][:, :], C.t["w3s_c"][:, :],
                                                0.5, None, ALU.mult),
               deps=[evs[10]], inc=True)
    e5 = pl.op("act", lambda s: s.activation(C.t["A_ROW"][0:1, :], C.t["alog_r"][0:1, :],
                                             AF.Exp), deps=[evs[-4], evs[-1]], inc=True)
    e6 = pl.op("dve", lambda v: v.tensor_scalar(C.t["A_ROW"][0:1, :],
                                                C.t["A_ROW"][0:1, :], -1.0, None,
                                                ALU.mult), deps=[e5], inc=True)
    e7 = pl.op("dve", lambda v: v.tensor_copy(C.t["CT_BF"][:, :], C.t["cmat_t"][:, :]),
               deps=[evs[-5]], inc=True)
    e8 = pl.op("dve", lambda v: v.tensor_copy(C.t["BMAT_BF"][:, :], C.t["bmat_c"][:, :]),
               deps=[evs[11]], inc=True)
    allc = evs + [e1, e2, e3, e4, e6, e7, e8]
    C.t["_consts_ev"] = pl.op("pool", lambda g: g.nop(), deps=allc, inc=True)
    return allc


def emit_load_stream(C, pl, ps_a, ps_b, src_ext, deps=()):
    xt = C.t["XT"]
    stream = C.t["STREAM"]
    last = []
    ld_prev = [None, None]
    ev_evac = None
    for ti in range(16):
        bi = ti % 2
        ld = pl.op("gdma", lambda g, ti=ti, bi=bi: g.dma_start(
            xt[:, bi, :], src_ext[ti * 128:(ti + 1) * 128, :]),
            deps=tuple(deps) + (ld_prev[bi],))
        for dc in range(8):
            pb = ps_a if dc % 2 == 0 else ps_b
            tr = pl.op("mm", lambda t, ti=ti, bi=bi, dc=dc, pb=pb: t.transpose(
                pb[:, :], xt[:, bi, dc * 128:(dc + 1) * 128], C.t["ident_f"][:, :]),
                deps=[ld, ev_evac], inc=True)
            ev_evac = pl.op("dve", lambda v, ti=ti, dc=dc, pb=pb: v.tensor_copy(
                stream[:, dc, ti * 128:(ti + 1) * 128], pb[:, :]),
                deps=[tr], inc=True)
            ld_prev[bi] = ev_evac
        last.append(ev_evac)
    return last


def emit_store_stream(C, pl, ps_a, ps_b, dst_ext, deps=()):
    ob = C.t["OBUF"]
    stream = C.t["STREAM"]
    st_prev = [None, None]
    ev_evac = None
    outs = []
    for ti in range(16):
        bi = ti % 2
        evs = []
        for dc in range(8):
            pb = ps_a if dc % 2 == 0 else ps_b
            tr = pl.op("mm", lambda t, ti=ti, dc=dc, pb=pb: t.transpose(
                pb[:, :], stream[:, dc, ti * 128:(ti + 1) * 128], C.t["ident_f"][:, :]),
                deps=tuple(deps) + (ev_evac, st_prev[bi]), inc=True)
            ev_evac = pl.op("dve", lambda v, ti=ti, bi=bi, dc=dc, pb=pb: v.tensor_copy(
                ob[:, bi, dc * 128:(dc + 1) * 128], pb[:, :]), deps=[tr], inc=True)
            evs.append(ev_evac)
        st = pl.op("gdma", lambda g, ti=ti, bi=bi: g.dma_start(
            dst_ext[ti * 128:(ti + 1) * 128, :], ob[:, bi, :]), deps=evs)
        st_prev[bi] = st
        outs.append(st)
    pl.op("pool", lambda g: g.nop(), deps=outs)
    return outs


def _cols(v):
    v = np.asarray(v, np.float32).reshape(-1)
    n = v.size // 128
    return np.ascontiguousarray(v.reshape(n, 128).T)


def stage_inputs(inputs):
    nonce = int(np.random.randint(1, 1 << 30))
    x = np.asarray(inputs["L0_attn_x"], np.float32)
    conv_w = np.asarray(inputs["L0_attn_conv_w"], np.float32)
    convw_c = np.ascontiguousarray(
        conv_w[:, 0, :].reshape(16, 128, 4).transpose(1, 0, 2).reshape(128, 64))
    common = {
        "ext_lng_c": _cols(inputs["L0_attn_ln_gamma"]),
        "ext_lnb_c": _cols(inputs["L0_attn_ln_beta"]),
        "ext_convw_c": convw_c,
        "ext_convb_c": _cols(inputs["L0_attn_conv_b"]),
        "ext_dvec_c": _cols(inputs["L0_attn_D_vec"]),
        "ext_bd_c": _cols(inputs["L0_attn_bd"]),
        "ext_w1s_c": _cols(inputs["L0_ffn_w1_s"]),
        "ext_w2s_c": _cols(inputs["L0_ffn_w2_s"]),
        "ext_h1s_c": _cols(inputs["L0_ffn_h1_s"]),
        "ext_h2s_c": _cols(inputs["L0_ffn_h2_s"]),
        "ext_w3s_c": _cols(inputs["L0_ffn_w3_s"]),
        "ext_bmat_c": np.ascontiguousarray(
            np.asarray(inputs["L0_attn_B_mat"], np.float32).T
            .reshape(16, 128, 16).transpose(1, 0, 2).reshape(128, 256)),
        "ext_cmat_t": np.ascontiguousarray(np.asarray(inputs["L0_attn_C_mat"],
                                                      np.float32).T),
        "ext_alog_r": np.asarray(inputs["L0_attn_A_log"], np.float32).reshape(1, 16),
        "ext_ident_f": np.eye(128, dtype=np.float32),
        "ext_syncv": (nonce + np.arange(8, dtype=np.int64)).astype(np.int32).reshape(1, 8),
        "ext_fconst": np.broadcast_to(np.asarray(FCONSTS, np.float32), (128, 16)).copy(),
    }
    for name in list(W_IMAJOR) + list(W_OMAJOR) + ["L1_attn_W"]:
        common["ext_" + name] = np.asarray(inputs[name], np.float32)
    in_maps = []
    for c in range(8):
        b, h = c >> 1, c & 1
        m = dict(common)
        m["xin"] = np.ascontiguousarray(x[b, h * TPC:(h + 1) * TPC, :])
        m["ext_auxf"] = np.ascontiguousarray(
            np.stack([np.full(128, float(h), np.float32),
                      np.zeros(128, np.float32)], axis=1))
        auxi = np.zeros((1, 16), np.int32)
        auxi[0, 0] = h * TPC * D
        auxi[0, 1] = (1 - h) * TPC * D
        auxi[0, 2] = h * 128 * 48
        auxi[0, 3] = (1 - h) * 128 * 48
        auxi[0, 4] = h * 16
        auxi[0, 5] = (1 - h) * 16
        auxi[0, 6] = h * 32
        auxi[0, 7] = (1 - h) * 32
        m["ext_auxi"] = auxi
        in_maps.append(m)
    return in_maps


def build_program(phases):
    nc = bass.Bass("TRN2", target_bir_lowering=False,
                   detect_race_conditions=RACE_DETECT)
    first, last = phases[0], phases[-1]
    xin = nc.dram_tensor("xin", [TPC, D], F32, kind="ExternalInput")
    xout = nc.dram_tensor("xout", [TPC, D], F32, kind="ExternalOutput")

    wnames = sorted({w for p in phases for w in PHASE_WEIGHTS[p]})
    if "p6" in phases and "p4" not in phases:
        wnames = sorted(set(wnames) | set(PHASE_WEIGHTS["p4"]))

    with ExitStack() as prog:
        C = Ctx(nc, prog)
        C.t["ext_xin"] = xin
        C.t["ext_xout"] = xout
        for name, shape in COL_INPUTS:
            C.t["ext_" + name] = nc.dram_tensor("ext_" + name, shape, F32,
                                                kind="ExternalInput")
            C.sb(prog, name, shape, F32)
        C.t["ext_ident_f"] = nc.dram_tensor("ext_ident_f", [128, 128], F32,
                                            kind="ExternalInput")
        C.sb(prog, "ident_f", [128, 128], F32)
        C.sb(prog, "ident_bf", [128, 128], BF16)
        C.t["ext_cmat_t"] = nc.dram_tensor("ext_cmat_t", [16, 2048], F32,
                                           kind="ExternalInput")
        C.sb(prog, "CT_BF", [16, 2048], BF16)
        C.sb(prog, "BMAT_BF", [128, 256], BF16)
        C.t["ext_alog_r"] = nc.dram_tensor("ext_alog_r", [1, 16], F32,
                                           kind="ExternalInput")
        C.sb(prog, "alog_r", [1, 16], F32)
        C.sb(prog, "A_ROW", [1, 16], F32)
        C.t["ext_auxi"] = nc.dram_tensor("ext_auxi", [1, 16], I32, kind="ExternalInput")
        C.sb(prog, "auxi", [1, 16], I32)
        C.t["ext_syncv"] = nc.dram_tensor("ext_syncv", [1, 8], I32, kind="ExternalInput")
        C.sb(prog, "syncv", [1, 8], I32)
        C.t["ext_fconst"] = nc.dram_tensor("ext_fconst", [128, 16], F32, kind="ExternalInput")
        fc = C.sb(prog, "fconst", [128, 16], F32)
        for i, v in enumerate(FCONSTS):
            nc.const_aps.register(F32, v, fc[:, i:i + 1])
        C.sb(prog, "FLAG_SB", [1, 4], I32)
        C.sb(prog, "ONES_BF", [128, 1], BF16)
        C.sb(prog, "ONE1_F", [1, 128], F32)
        C.sb(prog, "XB", [128, 2, TT], BF16)
        C.sb(prog, "XSQ", [128, 2, TT], BF16)
        C.sb(prog, "ROWS", [1, 4 * TT], F32)
        C.sb(prog, "SCR", [128, TT], F32)
        C.sb(prog, "w3sh_c", [128, 8], F32)
        for name in wnames:
            shape = (W_IMAJOR.get(name) or W_OMAJOR.get(name))
            C.t["ext_" + name] = nc.dram_tensor("ext_" + name, list(shape), F32,
                                                kind="ExternalInput")
        if "p5" in phases:
            C.t["ext_L1_attn_W"] = nc.dram_tensor("ext_L1_attn_W", [D, D], F32,
                                                  kind="ExternalInput")
        C.t["SH_FLAG"] = nc.dram_tensor("SH_FLAG", [2, 32], I32, addr_space="Shared")
        if "p2" in phases:
            C.t["SH_HALO"] = nc.dram_tensor("SH_HALO", [2, 128, 48], BF16,
                                            addr_space="Shared")
            C.t["SH_HEND"] = nc.dram_tensor("SH_HEND", [2, 16], F32,
                                            addr_space="Shared")
            C.t["SPILL_XS"] = nc.dram_tensor("SPILL_XS", [NT, 128, 16 * TT], BF16)
            C.t["SPILL_GATE"] = nc.dram_tensor("SPILL_GATE", [NT, 128, 16 * TT], BF16)
            C.t["SPILL_LNX2"] = nc.dram_tensor("SPILL_LNX2", [NT, 128, 8 * TT], BF16)
        if "p5" in phases:
            C.t["SH_KEYS"] = nc.dram_tensor("SH_KEYS", [2, TPC, D], BF16,
                                            addr_space="Shared")
        if "p5" in phases or "p6" in phases:
            C.t["STREAMD"] = nc.dram_tensor("STREAMD", [8, 128, TPC], F32)

        stash = {}
        sbuf_stream_needed = any(p in phases for p in ("p1", "p2", "p3", "p4", "p5"))
        with ExitStack() as smid:
            if sbuf_stream_needed or first == "p6":
                C.sb(smid, "STREAM", [128, 8, TPC], F32)
            with ExitStack() as s0:
                C.sb(s0, "cmat_t", [16, 2048], F32)
                tscr = [C.sb(s0, f"tscr{i}", [128, 4096], BF16) for i in range(2)]
                if sbuf_stream_needed or first == "p6":
                    C.sb(s0, "XT", [128, 2, 1024], F32)
                ps_a = C.ps(s0, "ps_la", [128, 128])
                ps_b = C.ps(s0, "ps_lb", [128, 128])
                pl0 = Plan(C)
                emit_consts(C, pl0)
                stash = emit_prepass(C, pl0, wnames, tscr, [None, None])
                if sbuf_stream_needed or first == "p6":
                    ld_evs = emit_load_stream(C, pl0, ps_a, ps_b, xin)
                    if first == "p6":
                        sde = []
                        for dc in range(8):
                            sde.append(pl0.op("gdma", lambda g, dc=dc: g.dma_start(
                                C.t["STREAMD"][dc, :, :], C.t["STREAM"][:, dc, :]),
                                deps=ld_evs))
                        C.t["_p5_done"] = sde[-1]
                pl0.run()

            for p in phases:
                if p in ("p1", "p3"):
                    emit_p_l0ffn(C, stash, p)
                elif p == "p2":
                    emit_p2(C, stash)
                elif p == "p4":
                    emit_p_l1ffn(C, stash, "p4", None)
                elif p == "p5":
                    emit_p5_prep(C)
                elif p == "p6":
                    break

            if last in ("p1", "p2", "p3", "p4"):
                with ExitStack() as s9:
                    C.sb(s9, "OBUF", [128, 2, 1024], F32)
                    ps_a = C.ps(s9, "ps_sa", [128, 128])
                    ps_b = C.ps(s9, "ps_sb", [128, 128])
                    pl9 = Plan(C)
                    emit_store_stream(C, pl9, ps_a, ps_b, xout)
                    pl9.run()

        if "p5" in phases:
            emit_p5_main(C)
        if "p6" in phases:
            emit_p_l1ffn(C, stash, "p6", xout)
        if last == "p5":
            with ExitStack() as sl:
                C.t["STREAM"] = C.sb(sl, "STREAM5", [128, 8, TPC], F32)
                C.t["OBUF"] = C.sb(sl, "OBUF5", [128, 2, 1024], F32)
                ps_a = C.ps(sl, "ps_s5a", [128, 128])
                ps_b = C.ps(sl, "ps_s5b", [128, 128])
                pl = Plan(C)
                evs = []
                for dc in range(8):
                    evs.append(pl.op("gdma", lambda g, dc=dc: g.dma_start(
                        C.t["STREAM"][:, dc, :], C.t["STREAMD"][dc, :, :]),
                        deps=[C.t.get("_p5_done")]))
                emit_store_stream(C, pl, ps_a, ps_b, xout, deps=evs)
                pl.run()
    return nc


def kernel(**inputs):
    in_maps = stage_inputs(inputs)
    cur = [m["xin"] for m in in_maps]
    for phases in GROUPS:
        nc = build_program(phases)
        maps = []
        for c in range(8):
            m = dict(in_maps[c])
            m["xin"] = np.ascontiguousarray(cur[c])
            maps.append(m)
        res = run_bass_kernel_spmd(nc, maps, core_ids=list(range(8)))
        cur = [res.results[c]["xout"] for c in range(8)]
    out = np.empty((B, T, D), np.float32)
    for c in range(8):
        b, h = c >> 1, c & 1
        out[b, h * TPC:(h + 1) * TPC, :] = cur[c]
    return out


# version 02
# speedup vs baseline: 1.5820x; 1.5820x over previous
import numpy as np
import concourse.bass as bass
import concourse.mybir as mybir
from concourse.bass_utils import run_bass_kernel_spmd
from contextlib import ExitStack

F32 = mybir.dt.float32
BF16 = mybir.dt.bfloat16
I32 = mybir.dt.int32
AF = mybir.ActivationFunctionType
AX = mybir.AxisListType
ALU = mybir.AluOpType

B, T, D = 4, 4096, 1024
FCONSTS = [1e-5, 1.0, -0.5, -1.0, 0.5, 0.1, 0.9, 1.9, 1.0 / 32.0, 0.01, 1e-9,
           1.0 / 1024.0, 1.0 / 2048.0, 0.0, 2.0, 3.0]
TPC = 2048
NT = 4
TT = 512
DI = 2048
DH = 4096
NS = 16
EPS = 1e-5

ALL_PHASES = ["p1", "p2", "p3", "p4", "p5", "p6"]
DEBUG_P2 = False
STRICT_SYNC = True
RACE_DETECT = True
GROUPS = [ALL_PHASES]

NDMASEM = 8
ENG_OF_KIND = {"mm": "tensor", "dve": "vector", "act": "scalar", "pool": "gpsimd",
               "gdma": "gpsimd", "sdma": "sync", "adma": "scalar", "pdma": "gpsimd"}
DMA_KINDS = ("gdma", "sdma", "adma")


class Ctx:
    def __init__(self, nc, stack):
        self.nc = nc
        self.stack = stack
        self.sems = {}
        self.counts = {}
        self.rr = {k: 0 for k in DMA_KINDS}
        self.highwater = {}
        for k in ("mm", "dve", "act", "pool", "pdma"):
            self.sems[k] = stack.enter_context(nc.semaphore("S_" + k))
            self.counts[k] = 0
        for k in DMA_KINDS:
            for j in range(NDMASEM):
                n = f"{k}{j}"
                self.sems[n] = stack.enter_context(nc.semaphore("S_" + n))
                self.counts[n] = 0
        self.t = {}

    def sb(self, stack, name, shape, dtype):
        h = stack.enter_context(self.nc.sbuf_tensor(name, shape, dtype))
        self.t[name] = h
        return h

    def ps(self, stack, name, shape, dtype=F32):
        h = stack.enter_context(self.nc.psum_tensor(name, shape, dtype))
        self.t[name] = h
        return h


_REGN = [0]


def _rn(base):
    _REGN[0] += 1
    return f"{base}{_REGN[0]}"


def eng_of(sem_name):
    for k, e in ENG_OF_KIND.items():
        if sem_name.startswith(k):
            return e
    raise KeyError(sem_name)


class Plan:

    def __init__(self, C):
        self.C = C
        self.ops = []

    def op(self, kind, fn, deps=(), inc=False, manual=False):
        C = self.C
        if kind in DMA_KINDS:
            j = C.rr[kind]
            C.rr[kind] = (j + 1) % NDMASEM
            sem_name = f"{kind}{j}"
            amt = 16
            inc = True
            prev = (sem_name, C.counts[sem_name]) if C.counts[sem_name] else None
            deps = tuple(deps) + (prev,)
        else:
            sem_name = kind
            amt = 1
        ev = None
        if inc:
            C.counts[sem_name] += amt
            ev = (sem_name, C.counts[sem_name])
        self.ops.append((sem_name, fn, tuple(d for d in deps if d is not None), ev,
                         0 if (not inc or manual) else amt))
        return ev

    def run(self):
        C = self.C
        by_eng = {}
        for rec in self.ops:
            by_eng.setdefault(eng_of(rec[0]), []).append(rec)
        with C.nc.Block() as block:
            for eng_name in ("gpsimd", "sync", "tensor", "scalar", "vector"):
                recs = by_eng.get(eng_name)
                if not recs:
                    continue

                def body(eng, recs=recs, eng_name=eng_name):
                    hw = C.highwater.setdefault(eng_name, {})
                    for (sem_name, fn, deps, ev, amt) in recs:
                        for (dk, dv) in deps:
                            is_dma = any(dk.startswith(p) for p in DMA_KINDS) or dk == "pdma"
                            same = eng_of(dk) == eng_name and not is_dma
                            if same and (not STRICT_SYNC or eng_name == "tensor"):
                                continue
                            if hw.get(dk, 0) >= dv:
                                continue
                            eng.wait_ge(C.sems[dk], dv)
                            hw[dk] = dv
                        inst = fn(eng)
                        if amt:
                            inst.then_inc(C.sems[sem_name], amt)
                getattr(block, eng_name)(body)


W_IMAJOR = {
    "L0_attn_W_in": (1024, 4096),
    "L0_attn_Wd": (2048, 2048),
    "L0_attn_W_out": (2048, 1024),
}
W_OMAJOR = {
    "L0_ffn_w1_w": (4096, 1024),
    "L0_ffn_w2_w": (4096, 1024),
    "L0_ffn_h1_w": (4096, 4096),
    "L0_ffn_h2_w": (4096, 4096),
    "L0_ffn_w3_w": (1024, 4096),
    "L1_ffn_W_fc": (4096, 1024),
    "L1_ffn_W_proj": (1024, 4096),
}
PHASE_WEIGHTS = {
    "p1": ["L0_ffn_w1_w", "L0_ffn_w2_w", "L0_ffn_h1_w", "L0_ffn_h2_w", "L0_ffn_w3_w"],
    "p2": ["L0_attn_W_in", "L0_attn_Wd", "L0_attn_W_out"],
    "p3": [],
    "p4": ["L1_ffn_W_fc", "L1_ffn_W_proj"],
    "p5": [],
    "p6": [],
}


def emit_prepass(C, pl, names, tscr, tev):
    nc = C.nc
    out = {}
    for name in names:
        ext = C.t["ext_" + name]
        if name in W_IMAJOR:
            A, Bb = W_IMAJOR[name]
            stash = nc.dram_tensor("st_" + name, [A, Bb], BF16)
            step = max(1, min(A, (8 << 20) // (Bb * 4)))
            ev = None
            r = 0
            while r < A:
                rr = min(step, A - r)
                ev = pl.op("gdma", lambda g, s=stash, e=ext, r=r, rr=rr:
                           g.dma_start(s[r:r + rr, :], e[r:r + rr, :]), deps=[ev])
                r += rr
            out[name] = (stash, (ev,))
        else:
            O, I = W_OMAJOR[name]
            stash_n = nc.dram_tensor("sn_" + name, [O, I], BF16)
            stash_t = nc.dram_tensor("st_" + name, [I, O], BF16)
            step = max(1, min(O, (8 << 20) // (I * 4)))
            cast_ev = None
            r = 0
            while r < O:
                rr = min(step, O - r)
                cast_ev = pl.op("gdma", lambda g, s=stash_n, e=ext, r=r, rr=rr:
                                g.dma_start(s[r:r + rr, :], e[r:r + rr, :]),
                                deps=[cast_ev])
                r += rr
            last = None
            nh = max(1, O // 2048)
            oh = O // nh
            for ic in range(I // 128):
                for hh in range(nh):
                    bi = (ic * nh + hh) % 2
                    ld = pl.op("sdma", lambda s, sn=stash_n, ic=ic, bi=bi, hh=hh, oh=oh:
                               s.dma_start(tscr[bi][:, 0:oh],
                                           sn[hh * oh:(hh + 1) * oh,
                                              ic * 128:(ic + 1) * 128],
                                           transpose=True),
                               deps=[cast_ev, tev[bi]])
                    st = pl.op("gdma", lambda g, stt=stash_t, ic=ic, bi=bi, hh=hh, oh=oh:
                               g.dma_start(stt[ic * 128:(ic + 1) * 128,
                                               hh * oh:(hh + 1) * oh],
                                           tscr[bi][:, 0:oh]),
                               deps=[ld])
                    tev[bi] = st
                    last = st
            out[name] = (stash_t, (tev[0], tev[1]))
    return out


def emit_ln(C, pl, P, src_fn, n_dc, tw, dst_fn, src_is_f32, deps=(),
            gamma=None, beta=None, dst_war=()):
    xb = C.t["XB"]
    sq = C.t["XSQ"]
    rows = C.t["ROWS"]
    mu = rows[0:1, 0 * TT:0 * TT + tw]
    m2 = rows[0:1, 1 * TT:1 * TT + tw]
    var = m2
    rstd = rows[0:1, 2 * TT:2 * TT + tw]
    bro = rows[0:1, 3 * TT:3 * TT + tw]

    deps = tuple(deps)
    mm_ev = None
    mms = []
    for dc in range(n_dc):
        bi = dc % 2
        war = (mms[-2],) if len(mms) >= 2 else ()
        if src_is_f32:
            e1 = pl.op("dve", lambda v, dc=dc, bi=bi: v.tensor_copy(
                xb[:, bi, 0:tw], src_fn(dc)), deps=deps + war, inc=True)
            sb_fn = (lambda bi=bi: xb[:, bi, 0:tw])
        else:
            e1 = None
            sb_fn = (lambda dc=dc: src_fn(dc))
        e2 = pl.op("dve", lambda v, bi=bi, f=sb_fn: v.tensor_tensor(
            sq[:, bi, 0:tw], f(), f(), ALU.mult),
            deps=deps + war + ((e1,) if e1 else ()), inc=True)
        mm_ev = pl.op("mm", lambda t, dc=dc, bi=bi, f=sb_fn, n=n_dc, tw=tw: (
            t.matmul(P["st0"][0:1, 0:tw], C.t["ONES_BF"][:, 0:1], f(),
                     start=(dc == 0), stop=(dc == n - 1)),
            t.matmul(P["st1"][0:1, 0:tw], C.t["ONES_BF"][:, 0:1], sq[:, bi, 0:tw],
                     start=(dc == 0), stop=(dc == n - 1)))[-1],
            deps=((e1,) if e1 else ()) + (e2, P.get("st_free")), inc=True)
        mms.append(mm_ev)

    r1 = pl.op("dve", lambda v: v.tensor_scalar(mu, P["st0"][0:1, 0:tw], 1.0 / D, None,
                                                ALU.mult), deps=[mm_ev], inc=True)
    r2 = pl.op("dve", lambda v: v.tensor_scalar(m2, P["st1"][0:1, 0:tw], 1.0 / D, None,
                                                ALU.mult), deps=[mm_ev], inc=True)
    r3 = pl.op("dve", lambda v: v.tensor_tensor(bro, mu, mu, ALU.mult), deps=[r1], inc=True)
    r4 = pl.op("dve", lambda v: v.tensor_tensor(var, m2, bro, ALU.subtract),
               deps=[r2, r3], inc=True)
    r5 = pl.op("act", lambda s: s.activation(var, var, AF.Ln, bias=EPS), deps=[r4], inc=True)
    r6 = pl.op("act", lambda s: s.activation(rstd, var, AF.Exp, scale=-0.5),
               deps=[r5], inc=True)
    r7 = pl.op("dve", lambda v: v.tensor_tensor(bro, mu, rstd, ALU.mult), deps=[r6], inc=True)
    if gamma is None:
        r7 = pl.op("dve", lambda v: v.tensor_scalar(bro, bro, -1.0, None, ALU.mult),
                   deps=[r7], inc=True)
    b1 = pl.op("mm", lambda t: t.matmul(P["bc0"][:, 0:tw], C.t["ONE1_F"][0:1, :], rstd),
               deps=[r6, P.get("bc_free")], inc=True)
    b2 = pl.op("mm", lambda t: t.matmul(P["bc1"][:, 0:tw], C.t["ONE1_F"][0:1, :], bro),
               deps=[r7, P.get("bc_free")], inc=True)

    scr = C.t["SCR"]
    apply_evs = []
    prev = None
    for dc in range(n_dc):
        a1 = pl.op("dve", lambda v, dc=dc: v.tensor_tensor(
            scr[:, 0:tw], src_fn(dc), P["bc0"][:, 0:tw], ALU.mult),
            deps=[b1, prev], inc=True)
        if gamma is None:
            a2 = pl.op("dve", lambda v, dc=dc: v.tensor_tensor(
                dst_fn(dc), scr[:, 0:tw], P["bc1"][:, 0:tw], ALU.add),
                deps=[a1, b2] + list(dst_war), inc=True)
        else:
            a15 = pl.op("dve", lambda v, dc=dc: v.tensor_tensor(
                scr[:, 0:tw], scr[:, 0:tw], P["bc1"][:, 0:tw], ALU.subtract),
                deps=[a1, b2], inc=True)
            a2 = pl.op("dve", lambda v, dc=dc: v.tensor_scalar(
                dst_fn(dc), scr[:, 0:tw], gamma(dc), beta(dc), ALU.mult, ALU.add),
                deps=[a15] + list(dst_war), inc=True)
        prev = a2
        apply_evs.append(a2)
    P["st_free"] = r2
    P["bc_free"] = apply_evs[-1]
    return apply_evs


def emit_wmm(C, pl, stashT_ev, I, O, og, wbuf, wev, rhs_fn, rhs_deps, bk,
             bank_ids, consume, tw, oc_range=None, dma_kind="sdma"):
    stashT, stash_ready = stashT_ev
    ICn = I // 128
    oc0, oc1 = (0, O // 128) if oc_range is None else oc_range
    og_oc = og // 128
    rview = stashT.rearrange("(ic p) o -> p ic o", p=128)
    evs = []
    for gi in range(oc0 // og_oc, (oc1 + og_oc - 1) // og_oc):
        bi = gi % 2
        ld = pl.op(dma_kind, lambda e, gi=gi, bi=bi: e.dma_start(
            wbuf[bi][:, :, 0:og], rview[:, :, gi * og:(gi + 1) * og]),
            deps=[wev[bi]] + list(stash_ready))
        last = None
        for oj in range(og_oc):
            oc = gi * og_oc + oj
            if oc < oc0 or oc >= oc1:
                continue
            nb = bank_ids[(oc - oc0) % len(bank_ids)]
            bview, bfree = bk[nb]
            mm_ev = pl.op("mm", lambda t, bi=bi, oj=oj, bview=bview: [
                t.matmul(bview[:, 0:tw], wbuf[bi][:, ic, oj * 128:(oj + 1) * 128],
                         rhs_fn(ic), start=(ic == 0), stop=(ic == ICn - 1))
                for ic in range(ICn)][-1],
                deps=tuple(rhs_deps) + (ld, bfree), inc=True)
            cev = consume(pl, oc, bview, mm_ev)
            bk[nb][1] = cev
            evs.append(cev)
            last = cev
        if last is not None:
            wev[bi] = last
    return evs


def emit_p_l0ffn(C, stash, tag, hooks=None):
    hooks = hooks or {}
    with ExitStack() as sk:
        lnx = C.sb(sk, f"{tag}_lnx", [128, 8, TT], BF16)
        hbuf = C.sb(sk, f"{tag}_h", [128, 32, TT], BF16)
        gbuf = C.sb(sk, f"{tag}_g", [128, 32, TT], BF16)
        scr2 = C.sb(sk, f"{tag}_scr2", [128, TT], F32)
        wb_a = [C.sb(sk, f"{tag}_wa{i}", [128, 8, 256], BF16) for i in range(2)]
        wb_b = [C.sb(sk, f"{tag}_wb{i}", [128, 8, 256], BF16) for i in range(2)]
        wb_h = [C.sb(sk, f"{tag}_wh{i}", [128, 32, 128], BF16) for i in range(2)]
        P = {"st0": C.ps(sk, f"{tag}_st0", [128, 512]),
             "st1": C.ps(sk, f"{tag}_st1", [128, 512]),
             "bc0": C.ps(sk, f"{tag}_bc0", [128, 512]),
             "bc1": C.ps(sk, f"{tag}_bc1", [128, 512]),
             "st_free": None, "bc_free": None}
        bk = {i: [C.ps(sk, f"{tag}_mm{i}", [128, 512]), None] for i in range(4)}

        stream = C.t["STREAM"]
        s1c, s2c = C.t["w1s_c"], C.t["w2s_c"]
        h1c, h2c = C.t["h1s_c"], C.t["h2s_c"]
        s3h = C.t["w3sh_c"]

        pl = Plan(C)
        if "pre" in hooks:
            hooks["pre"](pl, sk, P, bk)
        st_w1 = stash["L0_ffn_w1_w"]
        st_w2 = stash["L0_ffn_w2_w"]
        wev_a = [None, None]
        wev_b = [None, None]
        wev_h = [None, None]
        rv1 = st_w1[0].rearrange("(ic p) o -> p ic o", p=128)
        rv2 = st_w2[0].rearrange("(ic p) o -> p ic o", p=128)

        for tile in range(NT):
            t0 = tile * TT
            ln_evs = emit_ln(C, pl, P,
                             src_fn=lambda dc, t0=t0: stream[:, dc, t0:t0 + TT],
                             n_dc=8, tw=TT,
                             dst_fn=lambda dc: lnx[:, dc, :], src_is_f32=True)

            h_evs = []
            for gi in range(16):
                bi = gi % 2
                ld1 = pl.op("sdma", lambda e, gi=gi, bi=bi: e.dma_start(
                    wb_a[bi][:, :, :], rv1[:, :, gi * 256:(gi + 1) * 256]),
                    deps=[wev_a[bi]] + list(st_w1[1]))
                ld2 = pl.op("sdma", lambda e, gi=gi, bi=bi: e.dma_start(
                    wb_b[bi][:, :, :], rv2[:, :, gi * 256:(gi + 1) * 256]),
                    deps=[wev_b[bi]] + list(st_w2[1]))
                for oj in range(2):
                    oc = gi * 2 + oj
                    nb1, nb2 = (oc % 2), 2 + (oc % 2)
                    m1 = pl.op("mm", lambda t, bi=bi, oj=oj, nb1=nb1: [
                        t.matmul(bk[nb1][0][:, :], wb_a[bi][:, ic, oj * 128:(oj + 1) * 128],
                                 lnx[:, ic, :], start=(ic == 0), stop=(ic == 7))
                        for ic in range(8)][-1],
                        deps=[ld1, bk[nb1][1]] + ln_evs, inc=True)
                    m2 = pl.op("mm", lambda t, bi=bi, oj=oj, nb2=nb2: [
                        t.matmul(bk[nb2][0][:, :], wb_b[bi][:, ic, oj * 128:(oj + 1) * 128],
                                 lnx[:, ic, :], start=(ic == 0), stop=(ic == 7))
                        for ic in range(8)][-1],
                        deps=[ld2, bk[nb2][1]] + ln_evs, inc=True)
                    a1 = pl.op("act", lambda s, oc=oc, nb1=nb1: s.activation(
                        scr2[:, :], bk[nb1][0][:, :], AF.Silu, scale=s1c[:, oc:oc + 1]),
                        deps=[m1], inc=True)
                    d1 = pl.op("dve", lambda v, oc=oc, nb2=nb2: v.scalar_tensor_tensor(
                        hbuf[:, oc, :], bk[nb2][0][:, :], s2c[:, oc:oc + 1], scr2[:, :],
                        ALU.mult, ALU.mult), deps=[m2, a1], inc=True)
                    bk[nb1][1] = a1
                    bk[nb2][1] = d1
                    h_evs.append(d1)
                    wev_a[bi], wev_b[bi] = a1, d1

            if tile == 0 and "u0" in hooks:
                hooks["u0"](pl)
            st_h1 = stash["L0_ffn_h1_w"]

            def mk_gelu(dst, scol):
                def consume(pl_, oc, bankv, mm_ev, dst=dst, scol=scol):
                    return pl_.op("act", lambda s, oc=oc, bankv=bankv: s.activation(
                        dst[:, oc, :], bankv[:, :], AF.Gelu, scale=scol[:, oc:oc + 1]),
                        deps=[mm_ev], inc=True)
                return consume

            g_evs = emit_wmm(C, pl, st_h1, DH, DH, 128, wb_h, wev_h,
                             rhs_fn=lambda ic: hbuf[:, ic, :], rhs_deps=h_evs,
                             bk=bk, bank_ids=[0, 1],
                             consume=mk_gelu(gbuf, h1c), tw=TT)
            if tile == 0 and "h10" in hooks:
                hooks["h10"](pl)
            st_h2 = stash["L0_ffn_h2_w"]
            h2_evs = emit_wmm(C, pl, st_h2, DH, DH, 128, wb_h, wev_h,
                              rhs_fn=lambda ic: gbuf[:, ic, :], rhs_deps=g_evs,
                              bk=bk, bank_ids=[2, 3],
                              consume=mk_gelu(hbuf, h2c), tw=TT)

            if tile == 0 and "h20" in hooks:
                hooks["h20"](pl)
            st_w3 = stash["L0_ffn_w3_w"]

            def consume_w3(pl_, oc, bankv, mm_ev):
                return pl_.op("dve", lambda v, oc=oc, bankv=bankv, t0=t0:
                              v.scalar_tensor_tensor(
                                  stream[:, oc, t0:t0 + TT], bankv[:, :],
                                  s3h[:, oc:oc + 1], stream[:, oc, t0:t0 + TT],
                                  ALU.mult, ALU.add), deps=[mm_ev], inc=True)
            emit_wmm(C, pl, st_w3, DH, D, 128, wb_h, wev_h,
                     rhs_fn=lambda ic: hbuf[:, ic, :], rhs_deps=h2_evs,
                     bk=bk, bank_ids=[0, 1], consume=consume_w3, tw=TT)
        if "post" in hooks:
            hooks["post"](pl)
        pl.run()


def emit_spike_scan(C, pl, y_fn, out_fn, dc, deps, mem, spk, acc):
    e = pl.op("dve", lambda v: v.tensor_scalar(acc[:, :], y_fn(), 1.0, None, ALU.is_gt),
              deps=deps, inc=True)
    e = pl.op("dve", lambda v: v.scalar_tensor_tensor(
        mem[:, :], y_fn(), 1.9, acc[:, :], ALU.mult, ALU.subtract), deps=[e], inc=True)
    e = pl.op("dve", lambda v: v.tensor_scalar(spk[:, :], mem[:, :], 1.0, None, ALU.is_gt),
              deps=[e], inc=True)
    e = pl.op("dve", lambda v: v.tensor_tensor(acc[:, :], acc[:, :], spk[:, :], ALU.add),
              deps=[e], inc=True)
    for _ in range(3):
        e = pl.op("dve", lambda v: v.scalar_tensor_tensor(
            mem[:, :], mem[:, :], 0.9, y_fn(), ALU.mult, ALU.add), deps=[e], inc=True)
        e = pl.op("dve", lambda v: v.tensor_tensor(mem[:, :], mem[:, :], spk[:, :],
                                                   ALU.subtract), deps=[e], inc=True)
        e = pl.op("dve", lambda v: v.tensor_scalar(spk[:, :], mem[:, :], 1.0, None,
                                                   ALU.is_gt), deps=[e], inc=True)
        e = pl.op("dve", lambda v: v.tensor_tensor(acc[:, :], acc[:, :], spk[:, :],
                                                   ALU.add), deps=[e], inc=True)
    return out_fn(pl, e)


def emit_p_l1ffn(C, stash, tag, out_ext):
    with ExitStack() as sk:
        lnx = C.sb(sk, f"{tag}_lnx", [128, 8, TT], BF16)
        hb = C.sb(sk, f"{tag}_h", [128, 32, TT], BF16)
        yf = C.sb(sk, f"{tag}_y", [128, 8, TT], F32)
        rsc = C.sb(sk, f"{tag}_rsc", [128, TT], F32)
        mem = C.sb(sk, f"{tag}_mem", [128, TT], F32)
        spk = C.sb(sk, f"{tag}_spk", [128, TT], F32)
        acc = C.sb(sk, f"{tag}_acc", [128, TT], F32)
        wb_f = [C.sb(sk, f"{tag}_wf{i}", [128, 8, 512], BF16) for i in range(2)]
        wb_p = [C.sb(sk, f"{tag}_wp{i}", [128, 32, 256], BF16) for i in range(2)]
        P = {"st0": C.ps(sk, f"{tag}_st0", [128, 512]),
             "st1": C.ps(sk, f"{tag}_st1", [128, 512]),
             "bc0": C.ps(sk, f"{tag}_bc0", [128, 512]),
             "bc1": C.ps(sk, f"{tag}_bc1", [128, 512]),
             "st_free": None, "bc_free": None}
        bk = {i: [C.ps(sk, f"{tag}_mm{i}", [128, 512]), None] for i in range(2)}
        is_p6 = out_ext is not None
        if is_p6:
            stile = C.sb(sk, f"{tag}_stile", [128, 8, TT], F32)
            obuf = [C.sb(sk, f"{tag}_ob{i}", [128, 1024], F32) for i in range(2)]
            ps_tr = [C.ps(sk, f"{tag}_tr{i}", [128, 4, 128]) for i in range(2)]
        stream = C.t["STREAM"] if not is_p6 else None
        st_fc = stash["L1_ffn_W_fc"]
        st_pj = stash["L1_ffn_W_proj"]

        pl = Plan(C)
        wev_f = [None, None]
        wev_p = [None, None]
        out_dma_evs = []
        stile_ev = None
        for tile in range(NT):
            t0 = tile * TT
            if is_p6:
                stile_ev = pl.op("gdma", lambda g, t0=t0: g.dma_start(
                    stile[:, :, :],
                    C.t["STREAMD"].rearrange("d p t -> p d t")[:, :, t0:t0 + TT]),
                    deps=[stile_ev, C.t.get("_p5_done"),
                          out_dma_evs[-1] if out_dma_evs else None])
                src = lambda dc: stile[:, dc, :]
                ln_deps = (stile_ev,)
            else:
                src = lambda dc, t0=t0: stream[:, dc, t0:t0 + TT]
                ln_deps = ()
            ln_evs = emit_ln(C, pl, P, src_fn=src, n_dc=8, tw=TT,
                             dst_fn=lambda dc: lnx[:, dc, :], src_is_f32=True,
                             deps=ln_deps)

            def consume_fc(pl_, oc, bankv, mm_ev):
                a = pl_.op("act", lambda s, bankv=bankv: s.activation(
                    rsc[:, :], bankv[:, :], AF.Relu), deps=[mm_ev], inc=True)
                return pl_.op("dve", lambda v, oc=oc: v.tensor_tensor(
                    hb[:, oc, :], rsc[:, :], rsc[:, :], ALU.mult), deps=[a], inc=True)
            h_evs = emit_wmm(C, pl, st_fc, D, DH, 512, wb_f, wev_f,
                             rhs_fn=lambda ic: lnx[:, ic, :], rhs_deps=ln_evs,
                             bk=bk, bank_ids=[0, 1], consume=consume_fc, tw=TT)

            def consume_pj(pl_, oc, bankv, mm_ev):
                return pl_.op("act", lambda s, oc=oc, bankv=bankv: s.activation(
                    yf[:, oc, :], bankv[:, :], AF.Copy), deps=[mm_ev], inc=True)
            y_evs = emit_wmm(C, pl, st_pj, DH, D, 256, wb_p, wev_p,
                             rhs_fn=lambda ic: hb[:, ic, :], rhs_deps=h_evs,
                             bk=bk, bank_ids=[0, 1], consume=consume_pj, tw=TT)

            fin_evs = []
            for dc in range(8):
                if is_p6:
                    def out_fn(pl_, e, dc=dc):
                        return pl_.op("dve", lambda v, dc=dc: v.scalar_tensor_tensor(
                            stile[:, dc, :], acc[:, :], 0.1, stile[:, dc, :],
                            ALU.mult, ALU.add), deps=[e], inc=True)
                else:
                    def out_fn(pl_, e, dc=dc, t0=t0):
                        return pl_.op("dve", lambda v, dc=dc, t0=t0:
                                      v.scalar_tensor_tensor(
                                          stream[:, dc, t0:t0 + TT], acc[:, :], 0.1,
                                          stream[:, dc, t0:t0 + TT], ALU.mult, ALU.add),
                                      deps=[e], inc=True)
                fin_evs.append(emit_spike_scan(
                    C, pl, y_fn=lambda dc=dc: yf[:, dc, :], out_fn=out_fn, dc=dc,
                    deps=y_evs, mem=mem, spk=spk, acc=acc))
            if is_p6:
                ev_evac = None
                for tb in range(4):
                    bi = tb % 2
                    evac_evs = []
                    for dc in range(8):
                        slot = dc % 4
                        pb = ps_tr[dc // 4 % 2]
                        tr = pl.op("mm", lambda t, tb=tb, dc=dc, slot=slot, pb=pb:
                                   t.transpose(pb[:, slot, :],
                                               stile[:, dc, tb * 128:(tb + 1) * 128],
                                               C.t["ident_f"][:, :]),
                                   deps=fin_evs + [ev_evac, out_dma_evs[-2] if
                                                   len(out_dma_evs) >= 2 else None],
                                   inc=True)
                        ev_evac = pl.op("dve", lambda v, bi=bi, dc=dc, slot=slot, pb=pb:
                                        v.tensor_copy(obuf[bi][:, dc * 128:(dc + 1) * 128],
                                                      pb[:, slot, :]),
                                        deps=[tr], inc=True)
                        evac_evs.append(ev_evac)
                    out_dma_evs.append(pl.op("gdma", lambda g, t0=t0, tb=tb, bi=bi:
                                             g.dma_start(out_ext[t0 + tb * 128:
                                                                 t0 + (tb + 1) * 128, :],
                                                         obuf[bi][:, :]),
                                             deps=evac_evs))
        if is_p6:
            pl.op("pool", lambda g: g.nop(), deps=out_dma_evs)
        pl.run()


def emit_pair_signal(C, pl, k, data_evs):
    def fn(g, k=k):
        with g.register(_rn("foff")) as foff:
            g.reg_load(foff, C.t["auxi"][0:1, 6:7])
            g.reg_add(foff, foff, k)
            return g.dma_start(bass.AP(C.t["SH_FLAG"], foff, [[1, 1], [1, 1]]),
                               C.t["syncv"][0:1, k:k + 1])
    return pl.op("gdma", fn, deps=list(data_evs) + [C.t["_consts_ev"]])


def emit_pair_wait(C, pl, k, deps=()):
    def fn(g, k=k):
        fl = C.t["FLAG_SB"]
        with (g.register(_rn("pf")) as pf, g.register(_rn("want")) as want,
              g.register(_rn("got")) as got, g.register(_rn("nd")) as nd,
              g.register(_rn("cnt")) as cnt):
            g.reg_load(pf, C.t["auxi"][0:1, 7:8])
            g.reg_add(pf, pf, k)
            g.reg_load(want, C.t["syncv"][0:1, k:k + 1])
            g.reg_mov(nd, 1)
            g.reg_mov(cnt, C.counts["pdma"])
            with g.While(nd):
                g.reg_add(cnt, cnt, 16)
                g.dma_start(fl[0:1, 0:1],
                            bass.AP(C.t["SH_FLAG"], pf, [[1, 1], [1, 1]])
                            ).then_inc(C.sems["pdma"], 16)
                g.wait_ge(C.sems["pdma"], cnt)
                g.reg_load(got, fl[0:1, 0:1])
                g.reg_mov(nd, 1)
                with g.If_eq(got, want):
                    g.reg_mov(nd, 0)
        return g.sem_inc(C.sems["pool"], 1)
    return pl.op("pool", fn, deps=tuple(deps) + (C.t["_consts_ev"],),
                 inc=True, manual=True)


def emit_p2(C, stash):
    nc = C.nc
    with ExitStack() as sk:
        lnx1 = C.sb(sk, "p2_lnx1", [128, 8, TT], BF16)
        lnx2 = C.sb(sk, "p2_lnx2", [128, 8, TT], BF16)
        xsp = C.sb(sk, "p2_xsp", [128, 16, TT + 3], BF16)
        xs = C.sb(sk, "p2_xs", [128, 16, TT], BF16)
        gatb = C.sb(sk, "p2_gat", [128, 16, TT], BF16)
        spf = C.sb(sk, "p2_spf", [128, TT], F32)
        spp = C.sb(sk, "p2_spp", [128, 2, TT], BF16)
        convs = C.sb(sk, "p2_convs", [128, TT], F32)
        wb_in = [C.sb(sk, f"p2_wi{i}", [128, 8, 256], BF16) for i in range(2)]
        wb_d = [C.sb(sk, f"p2_wd{i}", [128, 16, 128], BF16) for i in range(2)]
        wb_o = [C.sb(sk, f"p2_wo{i}", [128, 16, 128], BF16) for i in range(2)]
        dat = C.sb(sk, "p2_dat", [16, TT], F32)
        bxs = C.sb(sk, "p2_bx", [16, TT], F32)
        bxd = C.sb(sk, "p2_bxd", [16, TT], F32)
        z16 = C.sb(sk, "p2_z16", [16, TT], F32)
        hs = C.sb(sk, "p2_hs", [16, TPC], F32)
        acum = C.sb(sk, "p2_ac", [16, TPC], F32)
        h0e = C.sb(sk, "p2_h0e", [16, 1], F32)
        h0r = C.sb(sk, "p2_h0r", [16, 1], F32)
        hscb = C.sb(sk, "p2_hscb", [16, TT], BF16)
        halo3 = C.sb(sk, "p2_halo3", [128, 16, 3], BF16)
        hxs = C.sb(sk, "p2_hxs", [128, 16, 3], BF16)
        hln1 = C.sb(sk, "p2_hln1", [128, 8, 3], BF16)
        hln2 = C.sb(sk, "p2_hln2", [128, 8, 3], BF16)
        P = {"st0": C.ps(sk, "p2_st0", [128, 512]),
             "st1": C.ps(sk, "p2_st1", [128, 512]),
             "bc0": C.ps(sk, "p2_bc0", [128, 512]),
             "bc1": C.ps(sk, "p2_bc1", [128, 512]),
             "st_free": None, "bc_free": None}
        bk = {i: [C.ps(sk, f"p2_mm{i}", [128, 512]), None] for i in range(2)}
        deltarow = C.sb(sk, "p2_delta", [1, TT], F32)
        ms0 = C.ps(sk, "p2_ms0", [16, 512])
        ms1 = C.ps(sk, "p2_ms1", [16, 512])

        stream = C.t["STREAM"]
        st_in = stash["L0_attn_W_in"]
        st_wd = stash["L0_attn_Wd"]
        st_wo = stash["L0_attn_W_out"]
        rv_in = st_in[0].rearrange("(ic p) o -> p ic o", p=128)
        lng = C.t["lng_c"]
        lnb = C.t["lnb_c"]
        cw = C.t["convw_c"]
        cb = C.t["convb_c"]
        dvc = C.t["dvec_c"]
        bdc = C.t["bd_c"]
        bmat = C.t["BMAT_BF"]
        ctbf = C.t["CT_BF"]
        auxf = C.t["auxf"]

        if DEBUG_P2:
            C.t["DBG_YG"] = C.nc.dram_tensor("dbg_yg", [NT, 128, 16 * TT], BF16,
                                             kind="ExternalOutput")
        pl = Plan(C)
        z16_ev = pl.op("dve", lambda v: v.memset(z16[:, :], 0.0), inc=True)

        hl1 = emit_ln(C, pl, P, src_fn=lambda dc: stream[:, dc, TPC - 3:TPC],
                      n_dc=8, tw=3, dst_fn=lambda dc: hln1[:, dc, :], src_is_f32=True)
        hl2 = emit_ln(C, pl, P, src_fn=lambda dc: hln1[:, dc, :],
                      n_dc=8, tw=3, dst_fn=lambda dc: hln2[:, dc, :], src_is_f32=False,
                      gamma=lambda dc: lng[:, dc:dc + 1], beta=lambda dc: lnb[:, dc:dc + 1])
        wev_in = [None, None]
        hxs_evs = []
        for gi in range(8):
            bi = gi % 2
            ld = pl.op("sdma", lambda e, gi=gi, bi=bi: e.dma_start(
                wb_in[bi][:, :, :], rv_in[:, :, gi * 256:(gi + 1) * 256]),
                deps=[wev_in[bi]] + list(st_in[1]))
            for oj in range(2):
                oc = gi * 2 + oj
                nb = oc % 2
                mm_ev = pl.op("mm", lambda t, bi=bi, oj=oj, nb=nb: [
                    t.matmul(bk[nb][0][:, 0:3], wb_in[bi][:, ic, oj * 128:(oj + 1) * 128],
                             hln2[:, ic, :], start=(ic == 0), stop=(ic == 7))
                    for ic in range(8)][-1],
                    deps=[ld, bk[nb][1]] + hl2, inc=True)
                cev = pl.op("dve", lambda v, oc=oc, nb=nb: v.tensor_copy(
                    hxs[:, oc, :], bk[nb][0][:, 0:3]), deps=[mm_ev], inc=True)
                bk[nb][1] = cev
                hxs_evs.append(cev)
                wev_in[bi] = cev
        def fn_send_halo(g):
            with g.register(_rn("hoff")) as hoff:
                g.reg_load(hoff, C.t["auxi"][0:1, 2:3])
                return g.dma_start(bass.AP(C.t["SH_HALO"], hoff, [[48, 128], [1, 48]]),
                                   hxs.rearrange("p a b -> p (a b)")[:, :])
        e_send = pl.op("gdma", fn_send_halo, deps=hxs_evs)
        emit_pair_signal(C, pl, 0, [e_send])
        w0 = emit_pair_wait(C, pl, 0)

        def fn_recv_halo(g):
            with g.register(_rn("hoff")) as hoff:
                g.reg_load(hoff, C.t["auxi"][0:1, 3:4])
                return g.dma_start(halo3.rearrange("p a b -> p (a b)")[:, :],
                                   bass.AP(C.t["SH_HALO"], hoff, [[48, 128], [1, 48]]))
        e_recv = pl.op("gdma", fn_recv_halo, deps=[w0])
        e_mask = pl.op("dve", lambda v: v.tensor_scalar(
            halo3.rearrange("p a b -> p (a b)")[:, :],
            halo3.rearrange("p a b -> p (a b)")[:, :],
            auxf[:, 0:1], None, ALU.mult), deps=[e_recv], inc=True)

        wev_d = [None, None]
        hs_last = None
        ac_last = None
        scan_evs = []
        xsp_tail_ev = None
        spill_evs = []
        sp_l2 = sp_gate = sp_xs = None
        for tile in range(NT):
            t0 = tile * TT
            l1 = emit_ln(C, pl, P, src_fn=lambda dc, t0=t0: stream[:, dc, t0:t0 + TT],
                         n_dc=8, tw=TT, dst_fn=lambda dc: lnx1[:, dc, :], src_is_f32=True,
                         dst_war=[sp_l2])
            l2 = emit_ln(C, pl, P, src_fn=lambda dc: lnx1[:, dc, :],
                         n_dc=8, tw=TT, dst_fn=lambda dc: lnx2[:, dc, :], src_is_f32=False,
                         gamma=lambda dc: lng[:, dc:dc + 1],
                         beta=lambda dc: lnb[:, dc:dc + 1])
            sp_l2 = pl.op("gdma", lambda g, tile=tile: g.dma_start(
                C.t["SPILL_LNX2"][tile, :, :],
                lnx1.rearrange("p a b -> p (a b)")[:, :]),
                deps=l2 + l1)

            if tile == 0:
                e_h = pl.op("dve", lambda v: v.tensor_copy(
                    xsp[:, :, 0:3], halo3[:, :, :]),
                    deps=[e_mask, xsp_tail_ev], inc=True)
            else:
                e_h = pl.op("dve", lambda v: v.tensor_copy(
                    xsp[:, :, 0:3], xsp[:, :, TT:TT + 3]),
                    deps=[xsp_tail_ev], inc=True)

            xs_evs = []
            gate_evs = []
            for gi in range(16):
                bi = gi % 2
                ld = pl.op("sdma", lambda e, gi=gi, bi=bi: e.dma_start(
                    wb_in[bi][:, :, :], rv_in[:, :, gi * 256:(gi + 1) * 256]),
                    deps=[wev_in[bi]] + list(st_in[1]))
                for oj in range(2):
                    oc = gi * 2 + oj
                    nb = oc % 2
                    mm_ev = pl.op("mm", lambda t, bi=bi, oj=oj, nb=nb: [
                        t.matmul(bk[nb][0][:, :], wb_in[bi][:, ic, oj * 128:(oj + 1) * 128],
                                 lnx2[:, ic, :], start=(ic == 0), stop=(ic == 7))
                        for ic in range(8)][-1],
                        deps=[ld, bk[nb][1]] + l2, inc=True)
                    if oc < 16:
                        cev = pl.op("act", lambda s, oc=oc, nb=nb: s.activation(
                            xsp[:, oc, 3:3 + TT], bk[nb][0][:, :], AF.Copy),
                            deps=[mm_ev, e_h], inc=True)
                        xs_evs.append(cev)
                    else:
                        cev = pl.op("act", lambda s, oc=oc, nb=nb: s.activation(
                            gatb[:, oc - 16, :], bk[nb][0][:, :], AF.Copy),
                            deps=[mm_ev, sp_gate], inc=True)
                        gate_evs.append(cev)
                    bk[nb][1] = cev
                    wev_in[bi] = cev
            sp_gate = pl.op("gdma", lambda g, tile=tile: g.dma_start(
                C.t["SPILL_GATE"][tile, :, :],
                gatb.rearrange("p a b -> p (a b)")[:, :]), deps=gate_evs)

            conv_evs = []
            prev_cc = None
            for cc in range(16):
                e = pl.op("dve", lambda v, cc=cc: v.tensor_scalar(
                    convs[:, :], xsp[:, cc, 0:TT], cw[:, cc * 4:cc * 4 + 1], None,
                    ALU.mult), deps=xs_evs + [e_h, prev_cc], inc=True)
                for k in (1, 2, 3):
                    e = pl.op("dve", lambda v, cc=cc, k=k: v.scalar_tensor_tensor(
                        convs[:, :], xsp[:, cc, k:k + TT], cw[:, cc * 4 + k:cc * 4 + k + 1],
                        convs[:, :], ALU.mult, ALU.add), deps=[e], inc=True)
                e = pl.op("act", lambda s, cc=cc: s.activation(
                    xs[:, cc, :], convs[:, :], AF.Silu, bias=cb[:, cc:cc + 1]),
                    deps=[e, sp_xs], inc=True)
                prev_cc = e
                conv_evs.append(e)
            xsp_tail_ev = conv_evs[-1]
            sp_xs = pl.op("gdma", lambda g, tile=tile: g.dma_start(
                C.t["SPILL_XS"][tile, :, :],
                xs.rearrange("p a b -> p (a b)")[:, :]), deps=conv_evs)
            spill_evs += [sp_l2, sp_gate, sp_xs]

            rv_d = st_wd[0].rearrange("(ic p) o -> p ic o", p=128)
            dsum_ev = None
            sp_prev = [None, None]
            for gi in range(16):
                bi = gi % 2
                ld = pl.op("sdma", lambda e, gi=gi, bi=bi: e.dma_start(
                    wb_d[bi][:, :, :], rv_d[:, :, gi * 128:(gi + 1) * 128]),
                    deps=[wev_d[bi]] + list(st_wd[1]))
                for oj in range(1):
                    oc = gi
                    nb = oc % 2
                    mm_ev = pl.op("mm", lambda t, bi=bi, oj=oj, nb=nb: [
                        t.matmul(bk[nb][0][:, :], wb_d[bi][:, ic, oj * 128:(oj + 1) * 128],
                                 xs[:, ic, :], start=(ic == 0), stop=(ic == 15))
                        for ic in range(16)][-1],
                        deps=[ld, bk[nb][1]] + conv_evs, inc=True)
                    e1 = pl.op("act", lambda s, oc=oc, nb=nb: s.activation(
                        spf[:, :], bk[nb][0][:, :], AF.Exp, bias=bdc[:, oc:oc + 1]),
                        deps=[mm_ev], inc=True)
                    si = oc % 2
                    e2 = pl.op("act", lambda s, si=si: s.activation(
                        spp[:, si, :], spf[:, :], AF.Ln, bias=1.0),
                        deps=[e1, sp_prev[si]], inc=True)
                    dsum_ev = pl.op("mm", lambda t, si=si, oc=oc: t.matmul(
                        P["st0"][0:1, :], C.t["ONES_BF"][:, 0:1], spp[:, si, :],
                        start=(oc == 0), stop=(oc == 15)),
                        deps=[e2, P["st_free"] if oc == 0 else None], inc=True)
                    sp_prev[si] = dsum_ev
                    bk[nb][1] = e1
                    wev_d[bi] = e1
            delta = deltarow[0:1, :]
            d_ev = pl.op("dve", lambda v: v.tensor_scalar(
                delta, P["st0"][0:1, :], 1.0 / DI, None, ALU.mult),
                deps=[dsum_ev], inc=True)
            P["st_free"] = d_ev

            m_da = pl.op("mm", lambda t: t.matmul(ms0[:, :], C.t["A_ROW"][0:1, :], delta),
                         deps=[d_ev, C.t.get("_ms0_free")], inc=True)
            e_da = pl.op("act", lambda s: s.activation(dat[:, :], ms0[:, :], AF.Exp),
                         deps=[m_da], inc=True)
            m_db = pl.op("mm", lambda t: t.matmul(ms1[:, :], C.t["ONE1_F"][0:1, 0:16],
                                                  delta),
                         deps=[d_ev, C.t.get("_ms1_free")], inc=True)
            m_bx = pl.op("mm", lambda t: [
                t.matmul(ms0[:, :], bmat[:, cc * 16:(cc + 1) * 16], xs[:, cc, :],
                         start=(cc == 0), stop=(cc == 15)) for cc in range(16)][-1],
                deps=conv_evs + [e_da], inc=True)
            e_bx = pl.op("act", lambda s: s.activation(bxs[:, :], ms0[:, :], AF.Copy),
                         deps=[m_bx], inc=True)
            C.t["_ms0_free"] = e_bx
            e_bxd = pl.op("dve", lambda v: v.tensor_tensor(bxd[:, :], bxs[:, :],
                                                           ms1[:, :], ALU.mult),
                          deps=[e_bx, m_db], inc=True)
            C.t["_ms1_free"] = e_bxd

            hs_last = pl.op("dve", lambda v, t0=t0, tile=tile: v.tensor_tensor_scan(
                hs[:, t0:t0 + TT], dat[:, :], bxd[:, :],
                (0.0 if tile == 0 else hs[:, t0 - 1:t0]), ALU.mult, ALU.add),
                deps=[e_da, e_bxd, hs_last], inc=True)
            ac_last = pl.op("dve", lambda v, t0=t0, tile=tile: v.tensor_tensor_scan(
                acum[:, t0:t0 + TT], dat[:, :], z16[:, :],
                (1.0 if tile == 0 else acum[:, t0 - 1:t0]), ALU.mult, ALU.add),
                deps=[e_da, z16_ev, ac_last], inc=True)
            scan_evs += [hs_last, ac_last]

        def fn_send_h(g):
            with g.register(_rn("ho")) as ho:
                g.reg_load(ho, C.t["auxi"][0:1, 4:5])
                return g.dma_start(bass.AP(C.t["SH_HEND"], ho, [[1, 16], [1, 1]]),
                                   hs[:, TPC - 1:TPC])
        e_sh = pl.op("gdma", fn_send_h, deps=[hs_last])
        emit_pair_signal(C, pl, 1, [e_sh])
        w1 = emit_pair_wait(C, pl, 1)

        def fn_recv_h(g):
            with g.register(_rn("ho")) as ho:
                g.reg_load(ho, C.t["auxi"][0:1, 5:6])
                return g.dma_start(h0r[:, :],
                                   bass.AP(C.t["SH_HEND"], ho, [[1, 16], [1, 1]]))
        e_rh = pl.op("gdma", fn_recv_h, deps=[w1])
        e_h0 = pl.op("dve", lambda v: v.tensor_tensor(
            h0e[:, :], h0r[:, :], auxf[0:16, 0:1], ALU.mult), deps=[e_rh], inc=True)

        wev_o = [None, None]
        rv_o = st_wo[0].rearrange("(ic p) o -> p ic o", p=128)
        tile_done = None
        for tile in range(NT):
            t0 = tile * TT
            rl_xs = pl.op("gdma", lambda g, tile=tile: g.dma_start(
                xs.rearrange("p a b -> p (a b)")[:, :],
                C.t["SPILL_XS"][tile, :, :]), deps=spill_evs + [tile_done])
            rl_gat = pl.op("gdma", lambda g, tile=tile: g.dma_start(
                gatb.rearrange("p a b -> p (a b)")[:, :],
                C.t["SPILL_GATE"][tile, :, :]), deps=spill_evs + [tile_done])
            rl_ln = pl.op("gdma", lambda g, tile=tile: g.dma_start(
                lnx2.rearrange("p a b -> p (a b)")[:, :],
                C.t["SPILL_LNX2"][tile, :, :]), deps=spill_evs + [tile_done])
            e_hsc = pl.op("dve", lambda v, t0=t0: v.scalar_tensor_tensor(
                hscb[:, :], acum[:, t0:t0 + TT], h0e[:, 0:1], hs[:, t0:t0 + TT],
                ALU.mult, ALU.add), deps=[e_h0] + scan_evs, inc=True)

            yg_evs = []
            prev_y = None
            for cc in range(16):
                nb = cc % 2
                m = pl.op("mm", lambda t, cc=cc, nb=nb: t.matmul(
                    bk[nb][0][:, :], ctbf[:, cc * 128:(cc + 1) * 128], hscb[:, :]),
                    deps=[e_hsc, bk[nb][1], C.t["_consts_ev"]], inc=True)
                e_y = pl.op("dve", lambda v, cc=cc, nb=nb: v.scalar_tensor_tensor(
                    convs[:, :], xs[:, cc, :], dvc[:, cc:cc + 1], bk[nb][0][:, :],
                    ALU.mult, ALU.add), deps=[m, rl_xs, prev_y], inc=True)
                e_sg = pl.op("act", lambda s, cc=cc: s.activation(
                    spf[:, :], gatb[:, cc, :], AF.Silu), deps=[rl_gat, prev_y], inc=True)
                e_yg = pl.op("dve", lambda v, cc=cc: v.tensor_tensor(
                    gatb[:, cc, :], convs[:, :], spf[:, :], ALU.mult),
                    deps=[e_y, e_sg], inc=True)
                bk[nb][1] = e_y
                prev_y = e_yg
                yg_evs.append(e_yg)

            if DEBUG_P2:
                pl.op("gdma", lambda g, tile=tile: g.dma_start(
                    C.t["DBG_YG"][tile, :, :],
                    gatb.rearrange("p a b -> p (a b)")[:, :]), deps=yg_evs)
            for gi in range(8):
                bi = gi % 2
                ld = pl.op("sdma", lambda e, gi=gi, bi=bi: e.dma_start(
                    wb_o[bi][:, :, :], rv_o[:, :, gi * 128:(gi + 1) * 128]),
                    deps=[wev_o[bi]] + list(st_wo[1]))
                for oj in range(1):
                    oc = gi
                    nb = oc % 2
                    m = pl.op("mm", lambda t, bi=bi, oj=oj, nb=nb: [
                        t.matmul(bk[nb][0][:, :], wb_o[bi][:, ic, oj * 128:(oj + 1) * 128],
                                 gatb[:, ic, :], start=(ic == 0), stop=(ic == 15))
                        for ic in range(16)][-1],
                        deps=[ld, bk[nb][1]] + yg_evs, inc=True)
                    e1 = pl.op("dve", lambda v, oc=oc, nb=nb: v.tensor_tensor(
                        convs[:, :], bk[nb][0][:, :], lnx2[:, oc, :], ALU.add),
                        deps=[m, rl_ln], inc=True)
                    e2 = pl.op("dve", lambda v, oc=oc, t0=t0: v.tensor_tensor(
                        stream[:, oc, t0:t0 + TT], stream[:, oc, t0:t0 + TT],
                        convs[:, :], ALU.add), deps=[e1], inc=True)
                    bk[nb][1] = e1
                    wev_o[bi] = e2
                    tile_done = e2
        if DEBUG_P2:
            nc = C.nc
            d_hs = nc.dram_tensor("dbg_hs", [16, TPC], F32, kind="ExternalOutput")
            d_ac = nc.dram_tensor("dbg_ac", [16, TPC], F32, kind="ExternalOutput")
            d_xs = nc.dram_tensor("dbg_xs", [NT, 128, 16 * TT], BF16, kind="ExternalOutput")
            d_gt = nc.dram_tensor("dbg_gt", [NT, 128, 16 * TT], BF16, kind="ExternalOutput")
            d_ln = nc.dram_tensor("dbg_ln", [NT, 128, 8 * TT], BF16, kind="ExternalOutput")
            d_h0 = nc.dram_tensor("dbg_h0", [16, 1], F32, kind="ExternalOutput")
            pl.op("gdma", lambda g: g.dma_start(d_hs[:, :], hs[:, :]), deps=scan_evs)
            pl.op("gdma", lambda g: g.dma_start(d_ac[:, :], acum[:, :]), deps=scan_evs)
            pl.op("gdma", lambda g: g.dma_start(d_xs[:, :, :], C.t["SPILL_XS"][:, :, :]),
                  deps=spill_evs)
            pl.op("gdma", lambda g: g.dma_start(d_gt[:, :, :], C.t["SPILL_GATE"][:, :, :]),
                  deps=spill_evs)
            pl.op("gdma", lambda g: g.dma_start(d_ln[:, :, :], C.t["SPILL_LNX2"][:, :, :]),
                  deps=spill_evs)
            pl.op("gdma", lambda g: g.dma_start(d_h0[:, :], h0e[:, :]), deps=[e_h0])
        pl.run()


def emit_p5_prep(C):
    with ExitStack() as sk:
        lnk = C.sb(sk, "p5_lnk", [128, 8, TT], BF16)
        tmb = [C.sb(sk, f"p5_tmb{i}", [128, 1024], BF16) for i in range(2)]
        P = {"st0": C.ps(sk, "p5p_st0", [128, 512]),
             "st1": C.ps(sk, "p5p_st1", [128, 512]),
             "bc0": C.ps(sk, "p5p_bc0", [128, 512]),
             "bc1": C.ps(sk, "p5p_bc1", [128, 512]),
             "st_free": None, "bc_free": None}
        ps_tr = [C.ps(sk, f"p5p_tr{i}", [128, 4, 128], BF16) for i in range(2)]
        stream = C.t["STREAM"]
        kstage = C.nc.dram_tensor("KSTAGE", [TPC, D], BF16)

        pl = Plan(C)
        key_evs = []
        for tile in range(NT):
            t0 = tile * TT
            ln_evs = emit_ln(C, pl, P,
                             src_fn=lambda dc, t0=t0: stream[:, dc, t0:t0 + TT],
                             n_dc=8, tw=TT, dst_fn=lambda dc: lnk[:, dc, :],
                             src_is_f32=True)
            for tb in range(4):
                bi = tb % 2
                evac = []
                ev_e = None
                for dc in range(8):
                    pb = ps_tr[dc % 2]
                    slot = (dc // 2) % 4
                    tr = pl.op("mm", lambda t, tb=tb, dc=dc, pb=pb, slot=slot:
                               t.transpose(pb[:, slot, :],
                                           lnk[:, dc, tb * 128:(tb + 1) * 128],
                                           C.t["ident_bf"][:, :]),
                               deps=ln_evs + [ev_e, key_evs[-2] if len(key_evs) >= 2
                                              else None], inc=True)
                    ev_e = pl.op("dve", lambda v, bi=bi, dc=dc, pb=pb, slot=slot:
                                 v.tensor_copy(tmb[bi][:, dc * 128:(dc + 1) * 128],
                                               pb[:, slot, :]), deps=[tr], inc=True)
                    evac.append(ev_e)

                key_evs.append(pl.op("gdma", lambda g, t0=t0, tb=tb, bi=bi:
                                     g.dma_start(kstage[t0 + tb * 128:
                                                        t0 + (tb + 1) * 128, :],
                                                 tmb[bi][:, :]), deps=evac))
        def fn_pub(g):
            with g.register(_rn("ko")) as ko:
                g.reg_load(ko, C.t["auxi"][0:1, 0:1])
                return g.dma_start(bass.AP(C.t["SH_KEYS"], ko, [[D, TPC], [1, D]]),
                                   kstage[:, :])
        key_evs = [pl.op("gdma", fn_pub, deps=key_evs)]
        sp_evs = []
        for dc in range(8):
            sp_evs.append(pl.op("gdma", lambda g, dc=dc: g.dma_start(
                C.t["STREAMD"][dc, :, :], stream[:, dc, :]), inc=True))
        C.t["_keys_sig"] = emit_pair_signal(C, pl, 2, key_evs + sp_evs)
        C.t["_keys_evs"] = tuple(key_evs) + tuple(sp_evs)

        pl.run()


def emit_p5_main(C):
    with ExitStack() as sk:
        kfm = C.sb(sk, "p5_kfm", [128, 8, 2 * TPC], BF16)
        ktm = C.sb(sk, "p5_ktm", [128, 32, 1024], BF16)
        ebuf = C.sb(sk, "p5_e", [128, 4096], F32)
        ek = C.sb(sk, "p5_ek", [128, 4096], BF16)
        ekt = C.sb(sk, "p5_ekt", [128, 32, 128], BF16)
        att = C.sb(sk, "p5_att", [128, 1024], F32)
        stile = C.sb(sk, "p5_stile", [128, 8, 128], F32)
        zsp = C.sb(sk, "p5_zsp", [128, 8], F32)
        rc = C.sb(sk, "p5_rc", [128, 8], F32)
        wrows = C.sb(sk, "p5_wrows", [128, 2, 1024], F32)
        wrsc = C.sb(sk, "p5_wrsc", [128, 8], F32)
        wrs8 = C.sb(sk, "p5_wrs8", [8, 128], F32)
        wrow = C.sb(sk, "p5_wrow", [1, 1024], F32)
        wrsbc = C.sb(sk, "p5_wrsbc", [128, 1024], BF16)
        sc = [C.ps(sk, f"p5_sc{i}", [128, 512]) for i in range(2)]
        at = [C.ps(sk, f"p5_at{i}", [128, 512]) for i in range(2)]
        trb = [C.ps(sk, f"p5_trb{i}", [128, 4, 128], BF16) for i in range(2)]
        tra = [C.ps(sk, f"p5_tra{i}", [128, 4, 128]) for i in range(2)]

        wrsd = C.nc.dram_tensor("WRS_D", [8, 128], F32)
        pl = Plan(C)
        rs_evs = []
        lw = None
        for half in range(4):
            lw = pl.op("gdma", lambda g, half=half: g.dma_start(
                wrows[:, :, :],
                C.t["ext_L1_attn_W"].rearrange("(a p) b -> p a b", p=128)
                [:, half * 2:half * 2 + 2, :]), deps=[rs_evs[-1] if rs_evs else None])
            for j in range(2):
                rs_evs.append(pl.op("dve", lambda v, half=half, j=j: v.reduce_sum(
                    wrsc[:, half * 2 + j:half * 2 + j + 1], wrows[:, j, :], axis=AX.X),
                    deps=[lw], inc=True))
        trw = pl.op("mm", lambda t: t.transpose(sc[0][0:8, 0:128], wrsc[:, :],
                                                C.t["ident_f"][:, :]),
                    deps=rs_evs, inc=True)
        evw = pl.op("dve", lambda v: v.tensor_copy(wrs8[:, :], sc[0][0:8, 0:128]),
                    deps=[trw], inc=True)
        stw = pl.op("gdma", lambda g: g.dma_start(wrsd[:, :], wrs8[:, :]), deps=[evw])
        ldw = pl.op("gdma", lambda g: g.dma_start(
            wrow[0:1, :], wrsd.rearrange("a b -> (a b)")[None, :]), deps=[stw])
        bcw = pl.op("mm", lambda t: [
            t.matmul(sc[0][:, 0:512], C.t["ONE1_F"][0:1, :], wrow[0:1, 0:512]),
            t.matmul(sc[1][:, 0:512], C.t["ONE1_F"][0:1, :], wrow[0:1, 512:1024])][-1],
            deps=[ldw, evw], inc=True)
        wv1 = pl.op("dve", lambda v: v.tensor_copy(wrsbc[:, 0:512], sc[0][:, 0:512]),
                    deps=[bcw], inc=True)
        wv2 = pl.op("dve", lambda v: v.tensor_copy(wrsbc[:, 512:1024], sc[1][:, 0:512]),
                    deps=[bcw], inc=True)
        C.t["_wrs_ev"] = wv2
        C.t["_wrs_ev1"] = wv1
        w2 = emit_pair_wait(C, pl, 2)

        kfm_evs = []
        ktm_evs = []
        for half in range(2):
            acol = 0 if half == 0 else 1
            for dc in range(8):
                def fn_fm(s, half=half, dc=dc, acol=acol):
                    with s.register(_rn("ko")) as ko:
                        s.reg_load(ko, C.t["auxi"][0:1, acol:acol + 1])
                        s.reg_add(ko, ko, dc * 128)
                        return s.dma_start(
                            kfm[:, dc, half * TPC:(half + 1) * TPC],
                            bass.AP(C.t["SH_KEYS"], ko, [[D, TPC], [1, 128]]),
                            transpose=True)
                kfm_evs.append(pl.op("sdma", fn_fm,
                                      deps=[w2] + list(C.t["_keys_evs"])))

            def fn_tm(g, half=half, acol=acol):
                with g.register(_rn("ko")) as ko:
                    g.reg_load(ko, C.t["auxi"][0:1, acol:acol + 1])
                    return g.dma_start(
                        ktm[:, half * 16:(half + 1) * 16, :],
                        bass.AP(C.t["SH_KEYS"], ko, [[D, 128], [128 * D, 16], [1, D]]))
            ktm_evs.append(pl.op("gdma", fn_tm,
                                  deps=[w2] + list(C.t["_keys_evs"])))

        st_prev = [None] * 4
        all_stores = []
        for qt in range(16):
            q0 = qt * 128
            e_evs = []
            for ks in range(8):
                nb = ks % 2
                m = pl.op("mm", lambda t, q0=q0, ks=ks, nb=nb: [
                    t.matmul(sc[nb][:, :], kfm[:, dc, q0:q0 + 128],
                             kfm[:, dc, ks * 512:(ks + 1) * 512],
                             start=(dc == 0), stop=(dc == 7)) for dc in range(8)][-1],
                    deps=kfm_evs + [e_evs[-2] if len(e_evs) >= 2 else None,
                                    C.t["_wrs_ev"], C.t["_wrs_ev1"]], inc=True)
                e_evs.append(pl.op("act", lambda s, ks=ks, nb=nb: s.activation(
                    ebuf[:, ks * 512:(ks + 1) * 512], sc[nb][:, :], AF.Exp,
                    scale=1.0 / 32.0, accum_out=zsp[:, ks:ks + 1]), deps=[m], inc=True))
            rz = pl.op("dve", lambda v: v.reduce_sum(rc[:, 0:1], zsp[:, :], axis=AX.X),
                       deps=e_evs, inc=True)
            rt = pl.op("dve", lambda v: v.tensor_scalar(rc[:, 1:2], rc[:, 0:1], 0.01,
                                                        None, ALU.mult),
                       deps=[rz], inc=True)
            rek = pl.op("dve", lambda v: v.scalar_tensor_tensor(
                ek[:, :], ebuf[:, :], rc[:, 1:2], ebuf[:, :], ALU.is_ge, ALU.mult,
                accum_out=rc[:, 2:3]), deps=[rt] + e_evs, inc=True)
            r1 = pl.op("dve", lambda v: v.tensor_scalar(rc[:, 3:4], rc[:, 0:1], 1e-9,
                                                        None, ALU.mult), deps=[rz], inc=True)
            r2 = pl.op("dve", lambda v: v.tensor_tensor(rc[:, 3:4], rc[:, 3:4],
                                                        rc[:, 2:3], ALU.add),
                       deps=[r1, rek], inc=True)
            r3 = pl.op("dve", lambda v: v.reciprocal(rc[:, 4:5], rc[:, 3:4]),
                       deps=[r2], inc=True)
            r4 = pl.op("dve", lambda v: v.tensor_tensor(rc[:, 5:6], rc[:, 0:1],
                                                        rc[:, 2:3], ALU.subtract),
                       deps=[rek, rz], inc=True)
            r5 = pl.op("dve", lambda v: v.reciprocal(rc[:, 6:7], rc[:, 0:1]),
                       deps=[rz], inc=True)
            r6 = pl.op("dve", lambda v: v.tensor_tensor(rc[:, 5:6], rc[:, 5:6],
                                                        rc[:, 6:7], ALU.mult),
                       deps=[r4, r5], inc=True)
            r7 = pl.op("dve", lambda v: v.tensor_scalar(rc[:, 5:6], rc[:, 5:6], 0.01,
                                                        None, ALU.mult), deps=[r6], inc=True)

            ekt_evs = []
            ev_e = None
            for kg in range(8):
                pb = trb[kg % 2]
                for j in range(4):
                    kc = kg * 4 + j
                    tr = pl.op("mm", lambda t, kc=kc, j=j, pb=pb: t.transpose(
                        pb[:, j, :], ek[:, kc * 128:(kc + 1) * 128],
                        C.t["ident_bf"][:, :]),
                        deps=[rek, ekt_evs[-2] if len(ekt_evs) >= 2 else None], inc=True)
                ev_e = pl.op("dve", lambda v, kg=kg, pb=pb: v.tensor_copy(
                    ekt[:, kg * 4:(kg + 1) * 4, :].rearrange("p a b -> p (a b)"),
                    pb.rearrange("p a b -> p (a b)")[:, :]), deps=[tr], inc=True)
                ekt_evs.append(ev_e)

            att_evs = []
            for ds in range(2):
                m = pl.op("mm", lambda t, ds=ds: [
                    t.matmul(at[ds][:, :], ekt[:, kc, :],
                             ktm[:, kc, ds * 512:(ds + 1) * 512],
                             start=(kc == 0), stop=(kc == 31)) for kc in range(32)][-1],
                    deps=ekt_evs + ktm_evs + [st_prev[2 + ds]], inc=True)
                att_evs.append(pl.op("act", lambda s, ds=ds: s.activation(
                    att[:, ds * 512:(ds + 1) * 512], at[ds][:, :], AF.Copy,
                    scale=rc[:, 4:5]), deps=[m, r3], inc=True))
                st_prev[2 + ds] = att_evs[-1]
            cas = pl.op("dve", lambda v: v.scalar_tensor_tensor(
                att[:, :], wrsbc[:, :], rc[:, 5:6], att[:, :],
                ALU.mult, ALU.add),
                deps=att_evs + [r7, C.t["_wrs_ev"], C.t["_wrs_ev1"]], inc=True)

            ld_st = pl.op("gdma", lambda g, q0=q0: g.dma_start(
                stile.rearrange("p a b -> p (a b)")[:, :],
                C.t["STREAMD"].rearrange("d p t -> p d t")[:, :, q0:q0 + 128]),
                deps=[st_prev[0]] + list(C.t["_keys_evs"]))
            add_evs = []
            ev_e2 = None
            for dc in range(8):
                pb = tra[dc % 2]
                slot = (dc // 2) % 4
                tr = pl.op("mm", lambda t, dc=dc, pb=pb, slot=slot: t.transpose(
                    pb[:, slot, :], att[:, dc * 128:(dc + 1) * 128],
                    C.t["ident_f"][:, :]),
                    deps=[cas, ev_e2, add_evs[-2] if len(add_evs) >= 2 else None],
                    inc=True)
                ev_e2 = pl.op("dve", lambda v, dc=dc, pb=pb, slot=slot: v.tensor_tensor(
                    stile[:, dc, :], stile[:, dc, :], pb[:, slot, :], ALU.add),
                    deps=[tr, ld_st], inc=True)
                add_evs.append(ev_e2)
            st_st = pl.op("gdma", lambda g, q0=q0: g.dma_start(
                C.t["STREAMD"].rearrange("d p t -> p d t")[:, :, q0:q0 + 128],
                stile.rearrange("p a b -> p (a b)")[:, :]), deps=add_evs)
            st_prev[0] = st_st
            all_stores.append(st_st)
        C.t["_p5_done"] = pl.op("pool", lambda g: g.nop(), deps=all_stores, inc=True)
        pl.run()


COL_INPUTS = [
    ("lng_c", [128, 8]), ("lnb_c", [128, 8]), ("convw_c", [128, 64]),
    ("convb_c", [128, 16]), ("dvec_c", [128, 16]), ("bd_c", [128, 16]),
    ("w1s_c", [128, 32]), ("w2s_c", [128, 32]), ("h1s_c", [128, 32]),
    ("h2s_c", [128, 32]), ("w3s_c", [128, 8]), ("bmat_c", [128, 256]),
    ("auxf", [128, 2]),
]


def emit_consts(C, pl):
    evs = []
    for name, shape in COL_INPUTS:
        evs.append(pl.op("gdma", lambda g, n=name: g.dma_start(
            C.t[n][:, :], C.t["ext_" + n][:, :])))
    for name in ["ident_f", "cmat_t", "alog_r", "auxi", "syncv", "fconst"]:
        evs.append(pl.op("gdma", lambda g, n=name: g.dma_start(
            C.t[n][:, :], C.t["ext_" + n][:, :])))
    e1 = pl.op("dve", lambda v: v.memset(C.t["ONES_BF"][:, :], 1.0), inc=True)
    e2 = pl.op("dve", lambda v: v.memset(C.t["ONE1_F"][:, :], 1.0), inc=True)
    e3 = pl.op("dve", lambda v: v.tensor_copy(C.t["ident_bf"][:, :],
                                              C.t["ident_f"][:, :]),
               deps=[evs[-6]], inc=True)
    e4 = pl.op("dve", lambda v: v.tensor_scalar(C.t["w3sh_c"][:, :], C.t["w3s_c"][:, :],
                                                0.5, None, ALU.mult),
               deps=[evs[10]], inc=True)
    e5 = pl.op("act", lambda s: s.activation(C.t["A_ROW"][0:1, :], C.t["alog_r"][0:1, :],
                                             AF.Exp), deps=[evs[-4], evs[-1]], inc=True)
    e6 = pl.op("dve", lambda v: v.tensor_scalar(C.t["A_ROW"][0:1, :],
                                                C.t["A_ROW"][0:1, :], -1.0, None,
                                                ALU.mult), deps=[e5], inc=True)
    e7 = pl.op("dve", lambda v: v.tensor_copy(C.t["CT_BF"][:, :], C.t["cmat_t"][:, :]),
               deps=[evs[-5]], inc=True)
    e8 = pl.op("dve", lambda v: v.tensor_copy(C.t["BMAT_BF"][:, :], C.t["bmat_c"][:, :]),
               deps=[evs[11]], inc=True)
    allc = evs + [e1, e2, e3, e4, e6, e7, e8]
    C.t["_consts_ev"] = pl.op("pool", lambda g: g.nop(), deps=allc, inc=True)
    return allc


def emit_load_stream(C, pl, ps_a, ps_b, src_ext, deps=()):
    xt = C.t["XT"]
    nb = xt.shape[1]
    stream = C.t["STREAM"]
    last = []
    ld_prev = [None, None]
    ev_evac = None
    for ti in range(16):
        bi = ti % nb
        ld = pl.op("gdma", lambda g, ti=ti, bi=bi: g.dma_start(
            xt[:, bi, :], src_ext[ti * 128:(ti + 1) * 128, :]),
            deps=tuple(deps) + (ld_prev[bi],))
        for dc in range(8):
            pb = ps_a if dc % 2 == 0 else ps_b
            tr = pl.op("mm", lambda t, ti=ti, bi=bi, dc=dc, pb=pb: t.transpose(
                pb[:, :], xt[:, bi, dc * 128:(dc + 1) * 128], C.t["ident_f"][:, :]),
                deps=[ld, ev_evac], inc=True)
            ev_evac = pl.op("dve", lambda v, ti=ti, dc=dc, pb=pb: v.tensor_copy(
                stream[:, dc, ti * 128:(ti + 1) * 128], pb[:, :]),
                deps=[tr], inc=True)
            ld_prev[bi] = ev_evac
        last.append(ev_evac)
    return last


def emit_store_stream(C, pl, ps_a, ps_b, dst_ext, deps=()):
    ob = C.t["OBUF"]
    stream = C.t["STREAM"]
    st_prev = [None, None]
    ev_evac = None
    outs = []
    for ti in range(16):
        bi = ti % 2
        evs = []
        for dc in range(8):
            pb = ps_a if dc % 2 == 0 else ps_b
            tr = pl.op("mm", lambda t, ti=ti, dc=dc, pb=pb: t.transpose(
                pb[:, :], stream[:, dc, ti * 128:(ti + 1) * 128], C.t["ident_f"][:, :]),
                deps=tuple(deps) + (ev_evac, st_prev[bi]), inc=True)
            ev_evac = pl.op("dve", lambda v, ti=ti, bi=bi, dc=dc, pb=pb: v.tensor_copy(
                ob[:, bi, dc * 128:(dc + 1) * 128], pb[:, :]), deps=[tr], inc=True)
            evs.append(ev_evac)
        st = pl.op("gdma", lambda g, ti=ti, bi=bi: g.dma_start(
            dst_ext[ti * 128:(ti + 1) * 128, :], ob[:, bi, :]), deps=evs)
        st_prev[bi] = st
        outs.append(st)
    pl.op("pool", lambda g: g.nop(), deps=outs)
    return outs


def _cols(v):
    v = np.asarray(v, np.float32).reshape(-1)
    n = v.size // 128
    return np.ascontiguousarray(v.reshape(n, 128).T)


def stage_inputs(inputs):
    nonce = int(np.random.randint(1, 1 << 30))
    x = np.asarray(inputs["L0_attn_x"], np.float32)
    conv_w = np.asarray(inputs["L0_attn_conv_w"], np.float32)
    convw_c = np.ascontiguousarray(
        conv_w[:, 0, :].reshape(16, 128, 4).transpose(1, 0, 2).reshape(128, 64))
    common = {
        "ext_lng_c": _cols(inputs["L0_attn_ln_gamma"]),
        "ext_lnb_c": _cols(inputs["L0_attn_ln_beta"]),
        "ext_convw_c": convw_c,
        "ext_convb_c": _cols(inputs["L0_attn_conv_b"]),
        "ext_dvec_c": _cols(inputs["L0_attn_D_vec"]),
        "ext_bd_c": _cols(inputs["L0_attn_bd"]),
        "ext_w1s_c": _cols(inputs["L0_ffn_w1_s"]),
        "ext_w2s_c": _cols(inputs["L0_ffn_w2_s"]),
        "ext_h1s_c": _cols(inputs["L0_ffn_h1_s"]),
        "ext_h2s_c": _cols(inputs["L0_ffn_h2_s"]),
        "ext_w3s_c": _cols(inputs["L0_ffn_w3_s"]),
        "ext_bmat_c": np.ascontiguousarray(
            np.asarray(inputs["L0_attn_B_mat"], np.float32).T
            .reshape(16, 128, 16).transpose(1, 0, 2).reshape(128, 256)),
        "ext_cmat_t": np.ascontiguousarray(np.asarray(inputs["L0_attn_C_mat"],
                                                      np.float32).T),
        "ext_alog_r": np.asarray(inputs["L0_attn_A_log"], np.float32).reshape(1, 16),
        "ext_ident_f": np.eye(128, dtype=np.float32),
        "ext_syncv": (nonce + np.arange(8, dtype=np.int64)).astype(np.int32).reshape(1, 8),
        "ext_fconst": np.broadcast_to(np.asarray(FCONSTS, np.float32), (128, 16)).copy(),
    }
    for name in list(W_IMAJOR) + list(W_OMAJOR) + ["L1_attn_W"]:
        common["ext_" + name] = np.asarray(inputs[name], np.float32)
    in_maps = []
    for c in range(8):
        b, h = c >> 1, c & 1
        m = dict(common)
        m["xin"] = np.ascontiguousarray(x[b, h * TPC:(h + 1) * TPC, :])
        m["ext_auxf"] = np.ascontiguousarray(
            np.stack([np.full(128, float(h), np.float32),
                      np.zeros(128, np.float32)], axis=1))
        auxi = np.zeros((1, 16), np.int32)
        auxi[0, 0] = h * TPC * D
        auxi[0, 1] = (1 - h) * TPC * D
        auxi[0, 2] = h * 128 * 48
        auxi[0, 3] = (1 - h) * 128 * 48
        auxi[0, 4] = h * 16
        auxi[0, 5] = (1 - h) * 16
        auxi[0, 6] = h * 32
        auxi[0, 7] = (1 - h) * 32
        m["ext_auxi"] = auxi
        in_maps.append(m)
    return in_maps


def build_program(phases):
    nc = bass.Bass("TRN2", target_bir_lowering=False,
                   detect_race_conditions=RACE_DETECT)
    first, last = phases[0], phases[-1]
    xin = nc.dram_tensor("xin", [TPC, D], F32, kind="ExternalInput")
    xout = nc.dram_tensor("xout", [TPC, D], F32, kind="ExternalOutput")

    wnames = sorted({w for p in phases for w in PHASE_WEIGHTS[p]})
    if "p6" in phases and "p4" not in phases:
        wnames = sorted(set(wnames) | set(PHASE_WEIGHTS["p4"]))

    with ExitStack() as prog:
        C = Ctx(nc, prog)
        C.t["ext_xin"] = xin
        C.t["ext_xout"] = xout
        for name, shape in COL_INPUTS:
            C.t["ext_" + name] = nc.dram_tensor("ext_" + name, shape, F32,
                                                kind="ExternalInput")
            C.sb(prog, name, shape, F32)
        C.t["ext_ident_f"] = nc.dram_tensor("ext_ident_f", [128, 128], F32,
                                            kind="ExternalInput")
        C.sb(prog, "ident_f", [128, 128], F32)
        C.sb(prog, "ident_bf", [128, 128], BF16)
        C.t["ext_cmat_t"] = nc.dram_tensor("ext_cmat_t", [16, 2048], F32,
                                           kind="ExternalInput")
        C.sb(prog, "CT_BF", [16, 2048], BF16)
        C.sb(prog, "BMAT_BF", [128, 256], BF16)
        C.t["ext_alog_r"] = nc.dram_tensor("ext_alog_r", [1, 16], F32,
                                           kind="ExternalInput")
        C.sb(prog, "alog_r", [1, 16], F32)
        C.sb(prog, "A_ROW", [1, 16], F32)
        C.t["ext_auxi"] = nc.dram_tensor("ext_auxi", [1, 16], I32, kind="ExternalInput")
        C.sb(prog, "auxi", [1, 16], I32)
        C.t["ext_syncv"] = nc.dram_tensor("ext_syncv", [1, 8], I32, kind="ExternalInput")
        C.sb(prog, "syncv", [1, 8], I32)
        C.t["ext_fconst"] = nc.dram_tensor("ext_fconst", [128, 16], F32, kind="ExternalInput")
        fc = C.sb(prog, "fconst", [128, 16], F32)
        for i, v in enumerate(FCONSTS):
            nc.const_aps.register(F32, v, fc[:, i:i + 1])
        C.sb(prog, "FLAG_SB", [1, 4], I32)
        C.sb(prog, "ONES_BF", [128, 1], BF16)
        C.sb(prog, "ONE1_F", [1, 128], F32)
        C.sb(prog, "XB", [128, 2, TT], BF16)
        C.sb(prog, "XSQ", [128, 2, TT], BF16)
        C.sb(prog, "ROWS", [1, 4 * TT], F32)
        C.sb(prog, "SCR", [128, TT], F32)
        C.sb(prog, "w3sh_c", [128, 8], F32)
        for name in wnames:
            shape = (W_IMAJOR.get(name) or W_OMAJOR.get(name))
            C.t["ext_" + name] = nc.dram_tensor("ext_" + name, list(shape), F32,
                                                kind="ExternalInput")
        if "p5" in phases:
            C.t["ext_L1_attn_W"] = nc.dram_tensor("ext_L1_attn_W", [D, D], F32,
                                                  kind="ExternalInput")
        C.t["SH_FLAG"] = nc.dram_tensor("SH_FLAG", [2, 32], I32, addr_space="Shared")
        if "p2" in phases:
            C.t["SH_HALO"] = nc.dram_tensor("SH_HALO", [2, 128, 48], BF16,
                                            addr_space="Shared")
            C.t["SH_HEND"] = nc.dram_tensor("SH_HEND", [2, 16], F32,
                                            addr_space="Shared")
            C.t["SPILL_XS"] = nc.dram_tensor("SPILL_XS", [NT, 128, 16 * TT], BF16)
            C.t["SPILL_GATE"] = nc.dram_tensor("SPILL_GATE", [NT, 128, 16 * TT], BF16)
            C.t["SPILL_LNX2"] = nc.dram_tensor("SPILL_LNX2", [NT, 128, 8 * TT], BF16)
        if "p5" in phases:
            C.t["SH_KEYS"] = nc.dram_tensor("SH_KEYS", [2, TPC, D], BF16,
                                            addr_space="Shared")
        if "p5" in phases or "p6" in phases:
            C.t["STREAMD"] = nc.dram_tensor("STREAMD", [8, 128, TPC], F32)

        stash = {}
        sbuf_stream_needed = any(p in phases for p in ("p1", "p2", "p3", "p4", "p5"))
        fast_p1 = (first == "p1")
        with ExitStack() as smid:
            if sbuf_stream_needed or first == "p6":
                C.sb(smid, "STREAM", [128, 8, TPC], F32)
            with ExitStack() as s0:
                C.sb(s0, "cmat_t", [16, 2048], F32)
                if not fast_p1:
                    tscr = [C.sb(s0, f"tscr{i}", [128, 4096], BF16) for i in range(2)]
                if not fast_p1 and (sbuf_stream_needed or first == "p6"):
                    C.sb(s0, "XT", [128, 2, 1024], F32)
                    ps_a = C.ps(s0, "ps_la", [128, 128])
                    ps_b = C.ps(s0, "ps_lb", [128, 128])
                pl0 = Plan(C)
                emit_consts(C, pl0)
                if not fast_p1:
                    stash = emit_prepass(C, pl0, wnames, tscr, [None, None])
                    if sbuf_stream_needed or first == "p6":
                        ld_evs = emit_load_stream(C, pl0, ps_a, ps_b, xin)
                        if first == "p6":
                            sde = []
                            for dc in range(8):
                                sde.append(pl0.op("gdma", lambda g, dc=dc: g.dma_start(
                                    C.t["STREAMD"][dc, :, :], C.t["STREAM"][:, dc, :]),
                                    deps=ld_evs))
                            C.t["_p5_done"] = sde[-1]
                pl0.run()

            def later_names(idx):
                seen = set(stash)
                out = []
                for p in phases[idx + 1:]:
                    for w in PHASE_WEIGHTS[p]:
                        if w not in seen and w in wnames:
                            seen.add(w)
                            out.append(w)
                return out

            for pi, p in enumerate(phases):
                if p in ("p1", "p3"):
                    hooks = {}
                    if fast_p1 and p == "p1":
                        tev = [None, None]

                        def pre(pl, sk, P, bk, tev=tev):
                            C.t["_tscr"] = [C.sb(sk, f"f_tscr{i}", [128, 2048], BF16)
                                            for i in range(2)]
                            C.sb(sk, "XT", [128, 1, 1024], F32)
                            stash.update(emit_prepass(
                                C, pl, ["L0_ffn_w1_w", "L0_ffn_w2_w"],
                                C.t["_tscr"], tev))
                            evs = emit_load_stream(C, pl, P["bc0"][:, 0:128],
                                                   P["bc1"][:, 0:128], xin)
                            P["bc_free"] = evs[-1]

                        def u0(pl, tev=tev):
                            stash.update(emit_prepass(C, pl, ["L0_ffn_h1_w"],
                                                      C.t["_tscr"], tev))

                        def h10(pl, tev=tev):
                            stash.update(emit_prepass(C, pl, ["L0_ffn_h2_w"],
                                                      C.t["_tscr"], tev))

                        def h20(pl, tev=tev):
                            stash.update(emit_prepass(C, pl, ["L0_ffn_w3_w"],
                                                      C.t["_tscr"], tev))

                        def post(pl, pi=pi, tev=tev):
                            names = [n for n in later_names(pi) if n in W_IMAJOR]
                            stash.update(emit_prepass(C, pl, names,
                                                      C.t["_tscr"], tev))
                        hooks = {"pre": pre, "u0": u0, "h10": h10, "h20": h20,
                                 "post": post}
                    if fast_p1 and p == "p3":
                        def pre3(pl, sk, P, bk):
                            C.t["_tscr3"] = [C.sb(sk, f"f3_tscr{i}", [128, 2048], BF16)
                                             for i in range(2)]

                        def post3(pl, pi=pi):
                            names = later_names(pi)
                            if names:
                                new = emit_prepass(C, pl, names, C.t["_tscr3"],
                                                   [None, None])
                                stash.update(new)
                                evs = [e for (_, ready) in new.values()
                                       for e in ready]
                                pl.op("pool", lambda g: g.nop(), deps=evs)
                        hooks = {"pre": pre3, "post": post3}
                    emit_p_l0ffn(C, stash, p, hooks=hooks)
                elif p == "p2":
                    emit_p2(C, stash)
                elif p == "p4":
                    emit_p_l1ffn(C, stash, "p4", None)
                elif p == "p5":
                    emit_p5_prep(C)
                elif p == "p6":
                    break

            if last in ("p1", "p2", "p3", "p4"):
                with ExitStack() as s9:
                    C.sb(s9, "OBUF", [128, 2, 1024], F32)
                    ps_a = C.ps(s9, "ps_sa", [128, 128])
                    ps_b = C.ps(s9, "ps_sb", [128, 128])
                    pl9 = Plan(C)
                    emit_store_stream(C, pl9, ps_a, ps_b, xout)
                    pl9.run()

        if "p5" in phases:
            emit_p5_main(C)
        if "p6" in phases:
            emit_p_l1ffn(C, stash, "p6", xout)
        if last == "p5":
            with ExitStack() as sl:
                C.t["STREAM"] = C.sb(sl, "STREAM5", [128, 8, TPC], F32)
                C.t["OBUF"] = C.sb(sl, "OBUF5", [128, 2, 1024], F32)
                ps_a = C.ps(sl, "ps_s5a", [128, 128])
                ps_b = C.ps(sl, "ps_s5b", [128, 128])
                pl = Plan(C)
                evs = []
                for dc in range(8):
                    evs.append(pl.op("gdma", lambda g, dc=dc: g.dma_start(
                        C.t["STREAM"][:, dc, :], C.t["STREAMD"][dc, :, :]),
                        deps=[C.t.get("_p5_done")]))
                emit_store_stream(C, pl, ps_a, ps_b, xout, deps=evs)
                pl.run()
    return nc


def kernel(**inputs):
    in_maps = stage_inputs(inputs)
    cur = [m["xin"] for m in in_maps]
    for phases in GROUPS:
        nc = build_program(phases)
        maps = []
        for c in range(8):
            m = dict(in_maps[c])
            m["xin"] = np.ascontiguousarray(cur[c])
            maps.append(m)
        res = run_bass_kernel_spmd(nc, maps, core_ids=list(range(8)))
        cur = [res.results[c]["xout"] for c in range(8)]
    out = np.empty((B, T, D), np.float32)
    for c in range(8):
        b, h = c >> 1, c & 1
        out[b, h * TPC:(h + 1) * TPC, :] = cur[c]
    return out


# version 03
# speedup vs baseline: 1.6986x; 1.0737x over previous
import numpy as np
import concourse.bass as bass
import concourse.mybir as mybir
from concourse.bass_utils import run_bass_kernel_spmd
from contextlib import ExitStack

F32 = mybir.dt.float32
BF16 = mybir.dt.bfloat16
I32 = mybir.dt.int32
AF = mybir.ActivationFunctionType
AX = mybir.AxisListType
ALU = mybir.AluOpType

B, T, D = 4, 4096, 1024
FCONSTS = [1e-5, 1.0, -0.5, -1.0, 0.5, 0.1, 0.9, 1.9, 1.0 / 32.0, 0.01, 1e-9,
           1.0 / 1024.0, 1.0 / 2048.0, 0.0, 2.0, 3.0]
TPC = 2048
NT = 4
TT = 512
DI = 2048
DH = 4096
NS = 16
EPS = 1e-5

ALL_PHASES = ["p1", "p2", "p3", "p4", "p5", "p6"]
DEBUG_P2 = False
STRICT_SYNC = True
RACE_DETECT = True
GROUPS = [ALL_PHASES]

NDMASEM = 8
ENG_OF_KIND = {"mm": "tensor", "dve": "vector", "act": "scalar", "pool": "gpsimd",
               "gdma": "gpsimd", "sdma": "sync", "adma": "scalar", "pdma": "gpsimd"}
DMA_KINDS = ("gdma", "sdma", "adma")


class Ctx:
    def __init__(self, nc, stack):
        self.nc = nc
        self.stack = stack
        self.sems = {}
        self.counts = {}
        self.rr = {k: 0 for k in DMA_KINDS}
        self.highwater = {}
        for k in ("mm", "dve", "act", "pool", "pdma"):
            self.sems[k] = stack.enter_context(nc.semaphore("S_" + k))
            self.counts[k] = 0
        for k in DMA_KINDS:
            for j in range(NDMASEM):
                n = f"{k}{j}"
                self.sems[n] = stack.enter_context(nc.semaphore("S_" + n))
                self.counts[n] = 0
        self.t = {}

    def sb(self, stack, name, shape, dtype):
        h = stack.enter_context(self.nc.sbuf_tensor(name, shape, dtype))
        self.t[name] = h
        return h

    def ps(self, stack, name, shape, dtype=F32):
        h = stack.enter_context(self.nc.psum_tensor(name, shape, dtype))
        self.t[name] = h
        return h


_REGN = [0]


def _rn(base):
    _REGN[0] += 1
    return f"{base}{_REGN[0]}"


def eng_of(sem_name):
    for k, e in ENG_OF_KIND.items():
        if sem_name.startswith(k):
            return e
    raise KeyError(sem_name)


class Plan:

    def __init__(self, C):
        self.C = C
        self.ops = []

    def op(self, kind, fn, deps=(), inc=False, manual=False):
        C = self.C
        if kind in DMA_KINDS:
            j = C.rr[kind]
            C.rr[kind] = (j + 1) % NDMASEM
            sem_name = f"{kind}{j}"
            amt = 16
            inc = True
            prev = (sem_name, C.counts[sem_name]) if C.counts[sem_name] else None
            deps = tuple(deps) + (prev,)
        else:
            sem_name = kind
            amt = 1
        ev = None
        if inc:
            C.counts[sem_name] += amt
            ev = (sem_name, C.counts[sem_name])
        self.ops.append((sem_name, fn, tuple(d for d in deps if d is not None), ev,
                         0 if (not inc or manual) else amt))
        return ev

    def run(self):
        C = self.C
        by_eng = {}
        for rec in self.ops:
            by_eng.setdefault(eng_of(rec[0]), []).append(rec)
        with C.nc.Block() as block:
            for eng_name in ("gpsimd", "sync", "tensor", "scalar", "vector"):
                recs = by_eng.get(eng_name)
                if not recs:
                    continue

                def body(eng, recs=recs, eng_name=eng_name):
                    hw = C.highwater.setdefault(eng_name, {})
                    for (sem_name, fn, deps, ev, amt) in recs:
                        for (dk, dv) in deps:
                            is_dma = any(dk.startswith(p) for p in DMA_KINDS) or dk == "pdma"
                            same = eng_of(dk) == eng_name and not is_dma
                            if same and (not STRICT_SYNC or eng_name == "tensor"):
                                continue
                            if hw.get(dk, 0) >= dv:
                                continue
                            eng.wait_ge(C.sems[dk], dv)
                            hw[dk] = dv
                        inst = fn(eng)
                        if amt:
                            inst.then_inc(C.sems[sem_name], amt)
                getattr(block, eng_name)(body)


W_IMAJOR = {
    "L0_attn_W_in": (1024, 4096),
    "L0_attn_Wd": (2048, 2048),
    "L0_attn_W_out": (2048, 1024),
}
W_OMAJOR = {
    "L0_ffn_w1_w": (4096, 1024),
    "L0_ffn_w2_w": (4096, 1024),
    "L0_ffn_h1_w": (4096, 4096),
    "L0_ffn_h2_w": (4096, 4096),
    "L0_ffn_w3_w": (1024, 4096),
    "L1_ffn_W_fc": (4096, 1024),
    "L1_ffn_W_proj": (1024, 4096),
}
PHASE_WEIGHTS = {
    "p1": ["L0_ffn_w1_w", "L0_ffn_w2_w", "L0_ffn_h1_w", "L0_ffn_h2_w", "L0_ffn_w3_w"],
    "p2": ["L0_attn_W_in", "L0_attn_Wd", "L0_attn_W_out"],
    "p3": [],
    "p4": ["L1_ffn_W_fc", "L1_ffn_W_proj"],
    "p5": [],
    "p6": [],
}


def emit_prepass(C, pl, names, tscr, tev, tq="sdma"):
    nc = C.nc
    out = {}
    for name in names:
        ext = C.t["ext_" + name]
        if name in W_IMAJOR:
            A, Bb = W_IMAJOR[name]
            stash = nc.dram_tensor("st_" + name, [A, Bb], BF16)
            step = max(1, min(A, (8 << 20) // (Bb * 4)))
            ev = None
            r = 0
            while r < A:
                rr = min(step, A - r)
                ev = pl.op("gdma", lambda g, s=stash, e=ext, r=r, rr=rr:
                           g.dma_start(s[r:r + rr, :], e[r:r + rr, :]), deps=[ev])
                r += rr
            out[name] = (stash, (ev,))
        else:
            O, I = W_OMAJOR[name]
            stash_n = nc.dram_tensor("sn_" + name, [O, I], BF16)
            stash_t = nc.dram_tensor("st_" + name, [I, O], BF16)
            step = max(1, min(O, (8 << 20) // (I * 4)))
            cast_ev = None
            r = 0
            while r < O:
                rr = min(step, O - r)
                cast_ev = pl.op("gdma", lambda g, s=stash_n, e=ext, r=r, rr=rr:
                                g.dma_start(s[r:r + rr, :], e[r:r + rr, :]),
                                deps=[cast_ev])
                r += rr
            last = None
            nh = max(1, O // 2048)
            oh = O // nh
            for ic in range(I // 128):
                for hh in range(nh):
                    bi = (ic * nh + hh) % 2
                    ld = pl.op(tq, lambda s, sn=stash_n, ic=ic, bi=bi, hh=hh, oh=oh:
                               s.dma_start(tscr[bi][:, 0:oh],
                                           sn[hh * oh:(hh + 1) * oh,
                                              ic * 128:(ic + 1) * 128],
                                           transpose=True),
                               deps=[cast_ev, tev[bi]])
                    st = pl.op("gdma", lambda g, stt=stash_t, ic=ic, bi=bi, hh=hh, oh=oh:
                               g.dma_start(stt[ic * 128:(ic + 1) * 128,
                                               hh * oh:(hh + 1) * oh],
                                           tscr[bi][:, 0:oh]),
                               deps=[ld])
                    tev[bi] = st
                    last = st
            out[name] = (stash_t, (tev[0], tev[1]))
    return out


def emit_ln(C, pl, P, src_fn, n_dc, tw, dst_fn, src_is_f32, deps=(),
            gamma=None, beta=None, dst_war=()):
    xb = C.t["XB"]
    sq = C.t["XSQ"]
    rows = C.t["ROWS"]
    mu = rows[0:1, 0 * TT:0 * TT + tw]
    m2 = rows[0:1, 1 * TT:1 * TT + tw]
    var = m2
    rstd = rows[0:1, 2 * TT:2 * TT + tw]
    bro = rows[0:1, 3 * TT:3 * TT + tw]

    deps = tuple(deps)
    mm_ev = None
    mms = []
    for dc in range(n_dc):
        bi = dc % 2
        war = (mms[-2],) if len(mms) >= 2 else ()
        if src_is_f32:
            e1 = pl.op("dve", lambda v, dc=dc, bi=bi: v.tensor_copy(
                xb[:, bi, 0:tw], src_fn(dc)), deps=deps + war, inc=True)
            sb_fn = (lambda bi=bi: xb[:, bi, 0:tw])
        else:
            e1 = None
            sb_fn = (lambda dc=dc: src_fn(dc))
        e2 = pl.op("dve", lambda v, bi=bi, f=sb_fn: v.tensor_tensor(
            sq[:, bi, 0:tw], f(), f(), ALU.mult),
            deps=deps + war + ((e1,) if e1 else ()), inc=True)
        mm_ev = pl.op("mm", lambda t, dc=dc, bi=bi, f=sb_fn, n=n_dc, tw=tw: (
            t.matmul(P["st0"][0:1, 0:tw], C.t["ONES_BF"][:, 0:1], f(),
                     start=(dc == 0), stop=(dc == n - 1)),
            t.matmul(P["st1"][0:1, 0:tw], C.t["ONES_BF"][:, 0:1], sq[:, bi, 0:tw],
                     start=(dc == 0), stop=(dc == n - 1)))[-1],
            deps=((e1,) if e1 else ()) + (e2, P.get("st_free")), inc=True)
        mms.append(mm_ev)

    r1 = pl.op("dve", lambda v: v.tensor_scalar(mu, P["st0"][0:1, 0:tw], 1.0 / D, None,
                                                ALU.mult), deps=[mm_ev], inc=True)
    r2 = pl.op("dve", lambda v: v.tensor_scalar(m2, P["st1"][0:1, 0:tw], 1.0 / D, None,
                                                ALU.mult), deps=[mm_ev], inc=True)
    r3 = pl.op("dve", lambda v: v.tensor_tensor(bro, mu, mu, ALU.mult), deps=[r1], inc=True)
    r4 = pl.op("dve", lambda v: v.tensor_tensor(var, m2, bro, ALU.subtract),
               deps=[r2, r3], inc=True)
    r5 = pl.op("act", lambda s: s.activation(var, var, AF.Ln, bias=EPS), deps=[r4], inc=True)
    r6 = pl.op("act", lambda s: s.activation(rstd, var, AF.Exp, scale=-0.5),
               deps=[r5], inc=True)
    r7 = pl.op("dve", lambda v: v.tensor_tensor(bro, mu, rstd, ALU.mult), deps=[r6], inc=True)
    if gamma is None:
        r7 = pl.op("dve", lambda v: v.tensor_scalar(bro, bro, -1.0, None, ALU.mult),
                   deps=[r7], inc=True)
    b1 = pl.op("mm", lambda t: t.matmul(P["bc0"][:, 0:tw], C.t["ONE1_F"][0:1, :], rstd),
               deps=[r6, P.get("bc_free")], inc=True)
    b2 = pl.op("mm", lambda t: t.matmul(P["bc1"][:, 0:tw], C.t["ONE1_F"][0:1, :], bro),
               deps=[r7, P.get("bc_free")], inc=True)

    scr = C.t["SCR"]
    apply_evs = []
    prev = None
    for dc in range(n_dc):
        a1 = pl.op("dve", lambda v, dc=dc: v.tensor_tensor(
            scr[:, 0:tw], src_fn(dc), P["bc0"][:, 0:tw], ALU.mult),
            deps=[b1, prev], inc=True)
        if gamma is None:
            a2 = pl.op("dve", lambda v, dc=dc: v.tensor_tensor(
                dst_fn(dc), scr[:, 0:tw], P["bc1"][:, 0:tw], ALU.add),
                deps=[a1, b2] + list(dst_war), inc=True)
        else:
            a15 = pl.op("dve", lambda v, dc=dc: v.tensor_tensor(
                scr[:, 0:tw], scr[:, 0:tw], P["bc1"][:, 0:tw], ALU.subtract),
                deps=[a1, b2], inc=True)
            a2 = pl.op("dve", lambda v, dc=dc: v.tensor_scalar(
                dst_fn(dc), scr[:, 0:tw], gamma(dc), beta(dc), ALU.mult, ALU.add),
                deps=[a15] + list(dst_war), inc=True)
        prev = a2
        apply_evs.append(a2)
    P["st_free"] = r2
    P["bc_free"] = apply_evs[-1]
    return apply_evs


def emit_wmm(C, pl, stashT_ev, I, O, og, wbuf, wev, rhs_fn, rhs_deps, bk,
             bank_ids, consume, tw, oc_range=None, dma_kind="sdma"):
    stashT, stash_ready = stashT_ev
    ICn = I // 128
    oc0, oc1 = (0, O // 128) if oc_range is None else oc_range
    og_oc = og // 128
    rview = stashT.rearrange("(ic p) o -> p ic o", p=128)
    evs = []
    for gi in range(oc0 // og_oc, (oc1 + og_oc - 1) // og_oc):
        bi = gi % 2
        ld = pl.op(dma_kind, lambda e, gi=gi, bi=bi: e.dma_start(
            wbuf[bi][:, :, 0:og], rview[:, :, gi * og:(gi + 1) * og]),
            deps=[wev[bi]] + list(stash_ready))
        last = None
        for oj in range(og_oc):
            oc = gi * og_oc + oj
            if oc < oc0 or oc >= oc1:
                continue
            nb = bank_ids[(oc - oc0) % len(bank_ids)]
            bview, bfree = bk[nb]
            mm_ev = pl.op("mm", lambda t, bi=bi, oj=oj, bview=bview: [
                t.matmul(bview[:, 0:tw], wbuf[bi][:, ic, oj * 128:(oj + 1) * 128],
                         rhs_fn(ic), start=(ic == 0), stop=(ic == ICn - 1))
                for ic in range(ICn)][-1],
                deps=tuple(rhs_deps) + (ld, bfree), inc=True)
            cev = consume(pl, oc, bview, mm_ev)
            bk[nb][1] = cev
            evs.append(cev)
            last = cev
        if last is not None:
            wev[bi] = last
    return evs


def emit_p_l0ffn(C, stash, tag, hooks=None):
    hooks = hooks or {}
    with ExitStack() as sk:
        lnx = C.sb(sk, f"{tag}_lnx", [128, 8, TT], BF16)
        hbuf = C.sb(sk, f"{tag}_h", [128, 32, TT], BF16)
        gbuf = C.sb(sk, f"{tag}_g", [128, 32, TT], BF16)
        scr2 = C.sb(sk, f"{tag}_scr2", [128, TT], F32)
        wb_a = [C.sb(sk, f"{tag}_wa{i}", [128, 8, 256], BF16) for i in range(2)]
        wb_b = [C.sb(sk, f"{tag}_wb{i}", [128, 8, 256], BF16) for i in range(2)]
        wb_h = [C.sb(sk, f"{tag}_wh{i}", [128, 32, 128], BF16) for i in range(2)]
        P = {"st0": C.ps(sk, f"{tag}_st0", [128, 512]),
             "st1": C.ps(sk, f"{tag}_st1", [128, 512]),
             "bc0": C.ps(sk, f"{tag}_bc0", [128, 512]),
             "bc1": C.ps(sk, f"{tag}_bc1", [128, 512]),
             "st_free": None, "bc_free": None}
        bk = {i: [C.ps(sk, f"{tag}_mm{i}", [128, 512]), None] for i in range(4)}

        stream = C.t["STREAM"]
        s1c, s2c = C.t["w1s_c"], C.t["w2s_c"]
        h1c, h2c = C.t["h1s_c"], C.t["h2s_c"]
        s3h = C.t["w3sh_c"]

        pl = Plan(C)
        if "pre" in hooks:
            hooks["pre"](pl, sk, P, bk)
        st_w1 = stash["L0_ffn_w1_w"]
        st_w2 = stash["L0_ffn_w2_w"]
        wev_a = [None, None]
        wev_b = [None, None]
        wev_h = [None, None]
        rv1 = st_w1[0].rearrange("(ic p) o -> p ic o", p=128)
        rv2 = st_w2[0].rearrange("(ic p) o -> p ic o", p=128)

        for tile in range(NT):
            t0 = tile * TT
            ln_evs = emit_ln(C, pl, P,
                             src_fn=lambda dc, t0=t0: stream[:, dc, t0:t0 + TT],
                             n_dc=8, tw=TT,
                             dst_fn=lambda dc: lnx[:, dc, :], src_is_f32=True)

            h_evs = []
            for gi in range(16):
                bi = gi % 2
                ld1 = pl.op("sdma", lambda e, gi=gi, bi=bi: e.dma_start(
                    wb_a[bi][:, :, :], rv1[:, :, gi * 256:(gi + 1) * 256]),
                    deps=[wev_a[bi]] + list(st_w1[1]))
                ld2 = pl.op("sdma", lambda e, gi=gi, bi=bi: e.dma_start(
                    wb_b[bi][:, :, :], rv2[:, :, gi * 256:(gi + 1) * 256]),
                    deps=[wev_b[bi]] + list(st_w2[1]))
                for oj in range(2):
                    oc = gi * 2 + oj
                    nb1, nb2 = (oc % 2), 2 + (oc % 2)
                    m1 = pl.op("mm", lambda t, bi=bi, oj=oj, nb1=nb1: [
                        t.matmul(bk[nb1][0][:, :], wb_a[bi][:, ic, oj * 128:(oj + 1) * 128],
                                 lnx[:, ic, :], start=(ic == 0), stop=(ic == 7))
                        for ic in range(8)][-1],
                        deps=[ld1, bk[nb1][1]] + ln_evs, inc=True)
                    m2 = pl.op("mm", lambda t, bi=bi, oj=oj, nb2=nb2: [
                        t.matmul(bk[nb2][0][:, :], wb_b[bi][:, ic, oj * 128:(oj + 1) * 128],
                                 lnx[:, ic, :], start=(ic == 0), stop=(ic == 7))
                        for ic in range(8)][-1],
                        deps=[ld2, bk[nb2][1]] + ln_evs, inc=True)
                    a1 = pl.op("act", lambda s, oc=oc, nb1=nb1: s.activation(
                        scr2[:, :], bk[nb1][0][:, :], AF.Silu, scale=s1c[:, oc:oc + 1]),
                        deps=[m1], inc=True)
                    d1 = pl.op("dve", lambda v, oc=oc, nb2=nb2: v.scalar_tensor_tensor(
                        hbuf[:, oc, :], bk[nb2][0][:, :], s2c[:, oc:oc + 1], scr2[:, :],
                        ALU.mult, ALU.mult), deps=[m2, a1], inc=True)
                    bk[nb1][1] = a1
                    bk[nb2][1] = d1
                    h_evs.append(d1)
                    wev_a[bi], wev_b[bi] = a1, d1

            if tile == 0 and "u0" in hooks:
                hooks["u0"](pl)
            st_h1 = stash["L0_ffn_h1_w"]

            def mk_gelu(dst, scol):
                def consume(pl_, oc, bankv, mm_ev, dst=dst, scol=scol):
                    return pl_.op("act", lambda s, oc=oc, bankv=bankv: s.activation(
                        dst[:, oc, :], bankv[:, :], AF.Gelu, scale=scol[:, oc:oc + 1]),
                        deps=[mm_ev], inc=True)
                return consume

            g_evs = emit_wmm(C, pl, st_h1, DH, DH, 128, wb_h, wev_h,
                             rhs_fn=lambda ic: hbuf[:, ic, :], rhs_deps=h_evs,
                             bk=bk, bank_ids=[0, 1, 2],
                             consume=mk_gelu(gbuf, h1c), tw=TT)
            if tile == 0 and "h10" in hooks:
                hooks["h10"](pl)
            st_h2 = stash["L0_ffn_h2_w"]
            h2_evs = emit_wmm(C, pl, st_h2, DH, DH, 128, wb_h, wev_h,
                              rhs_fn=lambda ic: gbuf[:, ic, :], rhs_deps=g_evs,
                              bk=bk, bank_ids=[3, 0, 1],
                              consume=mk_gelu(hbuf, h2c), tw=TT)

            if tile == 0 and "h20" in hooks:
                hooks["h20"](pl)
            st_w3 = stash["L0_ffn_w3_w"]

            def consume_w3(pl_, oc, bankv, mm_ev):
                return pl_.op("dve", lambda v, oc=oc, bankv=bankv, t0=t0:
                              v.scalar_tensor_tensor(
                                  stream[:, oc, t0:t0 + TT], bankv[:, :],
                                  s3h[:, oc:oc + 1], stream[:, oc, t0:t0 + TT],
                                  ALU.mult, ALU.add), deps=[mm_ev], inc=True)
            emit_wmm(C, pl, st_w3, DH, D, 128, wb_h, wev_h,
                     rhs_fn=lambda ic: hbuf[:, ic, :], rhs_deps=h2_evs,
                     bk=bk, bank_ids=[2, 3, 0], consume=consume_w3, tw=TT)
        if "post" in hooks:
            hooks["post"](pl)
        pl.run()


def emit_spike_scan(C, pl, y_fn, out_fn, dc, deps, mem, spk, acc, eng="dve"):
    stt_ok = (eng == "dve")
    e = pl.op(eng, lambda v: v.tensor_scalar(acc[:, :], y_fn(), 1.0, None, ALU.is_gt),
              deps=deps, inc=True)
    if stt_ok:
        e = pl.op(eng, lambda v: v.scalar_tensor_tensor(
            mem[:, :], y_fn(), 1.9, acc[:, :], ALU.mult, ALU.subtract),
            deps=[e], inc=True)
    else:
        e = pl.op(eng, lambda v: v.tensor_scalar(mem[:, :], y_fn(), 1.9, None,
                                                 ALU.mult), deps=[e], inc=True)
        e = pl.op(eng, lambda v: v.tensor_tensor(mem[:, :], mem[:, :], acc[:, :],
                                                 ALU.subtract), deps=[e], inc=True)
    e = pl.op(eng, lambda v: v.tensor_scalar(spk[:, :], mem[:, :], 1.0, None, ALU.is_gt),
              deps=[e], inc=True)
    e = pl.op(eng, lambda v: v.tensor_tensor(acc[:, :], acc[:, :], spk[:, :], ALU.add),
              deps=[e], inc=True)
    for _ in range(3):
        if stt_ok:
            e = pl.op(eng, lambda v: v.scalar_tensor_tensor(
                mem[:, :], mem[:, :], 0.9, y_fn(), ALU.mult, ALU.add),
                deps=[e], inc=True)
        else:
            e = pl.op(eng, lambda v: v.tensor_scalar(mem[:, :], mem[:, :], 0.9, None,
                                                     ALU.mult), deps=[e], inc=True)
            e = pl.op(eng, lambda v: v.tensor_tensor(mem[:, :], mem[:, :], y_fn(),
                                                     ALU.add), deps=[e], inc=True)
        e = pl.op(eng, lambda v: v.tensor_tensor(mem[:, :], mem[:, :], spk[:, :],
                                                 ALU.subtract), deps=[e], inc=True)
        e = pl.op(eng, lambda v: v.tensor_scalar(spk[:, :], mem[:, :], 1.0, None,
                                                 ALU.is_gt), deps=[e], inc=True)
        e = pl.op(eng, lambda v: v.tensor_tensor(acc[:, :], acc[:, :], spk[:, :],
                                                 ALU.add), deps=[e], inc=True)
    return out_fn(pl, e)


def emit_p_l1ffn(C, stash, tag, out_ext):
    with ExitStack() as sk:
        lnx = C.sb(sk, f"{tag}_lnx", [128, 8, TT], BF16)
        hb = C.sb(sk, f"{tag}_h", [128, 32, TT], BF16)
        yf = C.sb(sk, f"{tag}_y", [128, 8, TT], F32)
        rsc = C.sb(sk, f"{tag}_rsc", [128, TT], F32)
        mem = C.sb(sk, f"{tag}_mem", [128, TT], F32)
        spk = C.sb(sk, f"{tag}_spk", [128, TT], F32)
        acc = C.sb(sk, f"{tag}_acc", [128, TT], F32)
        mem2 = C.sb(sk, f"{tag}_mem2", [128, TT], F32)
        spk2 = C.sb(sk, f"{tag}_spk2", [128, TT], F32)
        acc2 = C.sb(sk, f"{tag}_acc2", [128, TT], F32)
        wb_f = [C.sb(sk, f"{tag}_wf{i}", [128, 8, 512], BF16) for i in range(2)]
        wb_p = [C.sb(sk, f"{tag}_wp{i}", [128, 32, 256], BF16) for i in range(2)]
        P = {"st0": C.ps(sk, f"{tag}_st0", [128, 512]),
             "st1": C.ps(sk, f"{tag}_st1", [128, 512]),
             "bc0": C.ps(sk, f"{tag}_bc0", [128, 512]),
             "bc1": C.ps(sk, f"{tag}_bc1", [128, 512]),
             "st_free": None, "bc_free": None}
        bk = {i: [C.ps(sk, f"{tag}_mm{i}", [128, 512]), None] for i in range(2)}
        is_p6 = out_ext is not None
        if is_p6:
            stile = C.sb(sk, f"{tag}_stile", [128, 8, TT], F32)
            obuf = [C.sb(sk, f"{tag}_ob{i}", [128, 1024], F32) for i in range(2)]
            ps_tr = [C.ps(sk, f"{tag}_tr{i}", [128, 4, 128]) for i in range(2)]
        stream = C.t["STREAM"] if not is_p6 else None
        st_fc = stash["L1_ffn_W_fc"]
        st_pj = stash["L1_ffn_W_proj"]

        pl = Plan(C)
        wev_f = [None, None]
        wev_p = [None, None]
        out_dma_evs = []
        stile_ev = None
        prev_fin = []
        for tile in range(NT):
            t0 = tile * TT
            if is_p6:
                stile_ev = pl.op("gdma", lambda g, t0=t0: g.dma_start(
                    stile[:, :, :],
                    C.t["STREAMD"].rearrange("d p t -> p d t")[:, :, t0:t0 + TT]),
                    deps=[stile_ev, C.t.get("_p5_done"),
                          out_dma_evs[-1] if out_dma_evs else None])
                src = lambda dc: stile[:, dc, :]
                ln_deps = (stile_ev,)
            else:
                src = lambda dc, t0=t0: stream[:, dc, t0:t0 + TT]
                ln_deps = ()
            ln_evs = emit_ln(C, pl, P, src_fn=src, n_dc=8, tw=TT,
                             dst_fn=lambda dc: lnx[:, dc, :], src_is_f32=True,
                             deps=ln_deps)

            def consume_fc(pl_, oc, bankv, mm_ev):
                a = pl_.op("act", lambda s, bankv=bankv: s.activation(
                    rsc[:, :], bankv[:, :], AF.Relu), deps=[mm_ev], inc=True)
                return pl_.op("dve", lambda v, oc=oc: v.tensor_tensor(
                    hb[:, oc, :], rsc[:, :], rsc[:, :], ALU.mult), deps=[a], inc=True)
            h_evs = emit_wmm(C, pl, st_fc, D, DH, 512, wb_f, wev_f,
                             rhs_fn=lambda ic: lnx[:, ic, :], rhs_deps=ln_evs,
                             bk=bk, bank_ids=[0, 1], consume=consume_fc, tw=TT)

            def consume_pj(pl_, oc, bankv, mm_ev):
                return pl_.op("act", lambda s, oc=oc, bankv=bankv: s.activation(
                    yf[:, oc, :], bankv[:, :], AF.Copy),
                    deps=[mm_ev] + prev_fin, inc=True)
            y_evs = emit_wmm(C, pl, st_pj, DH, D, 256, wb_p, wev_p,
                             rhs_fn=lambda ic: hb[:, ic, :], rhs_deps=h_evs,
                             bk=bk, bank_ids=[0, 1], consume=consume_pj, tw=TT)

            fin_evs = []
            prev_pool_out = None
            for dc in range(8):
                if is_p6:
                    def out_fn(pl_, e, dc=dc):
                        return pl_.op("dve", lambda v, dc=dc: v.scalar_tensor_tensor(
                            stile[:, dc, :], acc[:, :], 0.1, stile[:, dc, :],
                            ALU.mult, ALU.add), deps=[e], inc=True)
                else:
                    def out_fn(pl_, e, dc=dc, t0=t0):
                        return pl_.op("dve", lambda v, dc=dc, t0=t0:
                                      v.scalar_tensor_tensor(
                                          stream[:, dc, t0:t0 + TT], acc[:, :], 0.1,
                                          stream[:, dc, t0:t0 + TT], ALU.mult, ALU.add),
                                      deps=[e], inc=True)
                on_pool = False
                fe = emit_spike_scan(
                    C, pl, y_fn=lambda dc=dc: yf[:, dc, :], out_fn=out_fn, dc=dc,
                    deps=list(y_evs) + ([prev_pool_out] if on_pool else []),
                    mem=mem2 if on_pool else mem,
                    spk=spk2 if on_pool else spk,
                    acc=acc2 if on_pool else acc,
                    eng="pool" if on_pool else "dve")
                if on_pool:
                    prev_pool_out = fe
                fin_evs.append(fe)
            prev_fin = list(fin_evs)
            if is_p6:
                ev_evac = None
                for tb in range(4):
                    bi = tb % 2
                    evac_evs = []
                    for dc in range(8):
                        slot = dc % 4
                        pb = ps_tr[dc // 4 % 2]
                        tr = pl.op("mm", lambda t, tb=tb, dc=dc, slot=slot, pb=pb:
                                   t.transpose(pb[:, slot, :],
                                               stile[:, dc, tb * 128:(tb + 1) * 128],
                                               C.t["ident_f"][:, :]),
                                   deps=fin_evs + [ev_evac, out_dma_evs[-2] if
                                                   len(out_dma_evs) >= 2 else None],
                                   inc=True)
                        ev_evac = pl.op("dve", lambda v, bi=bi, dc=dc, slot=slot, pb=pb:
                                        v.tensor_copy(obuf[bi][:, dc * 128:(dc + 1) * 128],
                                                      pb[:, slot, :]),
                                        deps=[tr], inc=True)
                        evac_evs.append(ev_evac)
                    out_dma_evs.append(pl.op("gdma", lambda g, t0=t0, tb=tb, bi=bi:
                                             g.dma_start(out_ext[t0 + tb * 128:
                                                                 t0 + (tb + 1) * 128, :],
                                                         obuf[bi][:, :]),
                                             deps=evac_evs))
        if is_p6:
            pl.op("pool", lambda g: g.nop(), deps=out_dma_evs)
        pl.run()


def emit_pair_signal(C, pl, k, data_evs):
    def fn(g, k=k):
        with g.register(_rn("foff")) as foff:
            g.reg_load(foff, C.t["auxi"][0:1, 6:7])
            g.reg_add(foff, foff, k)
            return g.dma_start(bass.AP(C.t["SH_FLAG"], foff, [[1, 1], [1, 1]]),
                               C.t["syncv"][0:1, k:k + 1])
    return pl.op("gdma", fn, deps=list(data_evs) + [C.t["_consts_ev"]])


def emit_pair_wait(C, pl, k, deps=()):
    def fn(g, k=k):
        fl = C.t["FLAG_SB"]
        with (g.register(_rn("pf")) as pf, g.register(_rn("want")) as want,
              g.register(_rn("got")) as got, g.register(_rn("nd")) as nd,
              g.register(_rn("cnt")) as cnt):
            g.reg_load(pf, C.t["auxi"][0:1, 7:8])
            g.reg_add(pf, pf, k)
            g.reg_load(want, C.t["syncv"][0:1, k:k + 1])
            g.reg_mov(nd, 1)
            g.reg_mov(cnt, C.counts["pdma"])
            with g.While(nd):
                g.reg_add(cnt, cnt, 16)
                g.dma_start(fl[0:1, 0:1],
                            bass.AP(C.t["SH_FLAG"], pf, [[1, 1], [1, 1]])
                            ).then_inc(C.sems["pdma"], 16)
                g.wait_ge(C.sems["pdma"], cnt)
                g.reg_load(got, fl[0:1, 0:1])
                g.reg_mov(nd, 1)
                with g.If_eq(got, want):
                    g.reg_mov(nd, 0)
        return g.sem_inc(C.sems["pool"], 1)
    return pl.op("pool", fn, deps=tuple(deps) + (C.t["_consts_ev"],),
                 inc=True, manual=True)


def emit_p2(C, stash):
    nc = C.nc
    with ExitStack() as sk:
        lnx1 = C.sb(sk, "p2_lnx1", [128, 8, TT], BF16)
        lnx2 = C.sb(sk, "p2_lnx2", [128, 8, TT], BF16)
        xsp = C.sb(sk, "p2_xsp", [128, 16, TT + 3], BF16)
        xs = C.sb(sk, "p2_xs", [128, 16, TT], BF16)
        gatb = C.sb(sk, "p2_gat", [128, 16, TT], BF16)
        spf = C.sb(sk, "p2_spf", [128, TT], F32)
        spp = C.sb(sk, "p2_spp", [128, 2, TT], BF16)
        convs = C.sb(sk, "p2_convs", [128, TT], F32)
        wb_in = [C.sb(sk, f"p2_wi{i}", [128, 8, 256], BF16) for i in range(2)]
        wb_d = [C.sb(sk, f"p2_wd{i}", [128, 16, 128], BF16) for i in range(2)]
        wb_o = [C.sb(sk, f"p2_wo{i}", [128, 16, 128], BF16) for i in range(2)]
        dat = C.sb(sk, "p2_dat", [16, TT], F32)
        bxs = C.sb(sk, "p2_bx", [16, TT], F32)
        bxd = C.sb(sk, "p2_bxd", [16, TT], F32)
        z16 = C.sb(sk, "p2_z16", [16, TT], F32)
        hs = C.sb(sk, "p2_hs", [16, TPC], F32)
        acum = C.sb(sk, "p2_ac", [16, TPC], F32)
        h0e = C.sb(sk, "p2_h0e", [16, 1], F32)
        h0r = C.sb(sk, "p2_h0r", [16, 1], F32)
        hscb = C.sb(sk, "p2_hscb", [16, TT], BF16)
        halo3 = C.sb(sk, "p2_halo3", [128, 16, 3], BF16)
        hxs = C.sb(sk, "p2_hxs", [128, 16, 3], BF16)
        hln1 = C.sb(sk, "p2_hln1", [128, 8, 3], BF16)
        hln2 = C.sb(sk, "p2_hln2", [128, 8, 3], BF16)
        P = {"st0": C.ps(sk, "p2_st0", [128, 512]),
             "st1": C.ps(sk, "p2_st1", [128, 512]),
             "bc0": C.ps(sk, "p2_bc0", [128, 512]),
             "bc1": C.ps(sk, "p2_bc1", [128, 512]),
             "st_free": None, "bc_free": None}
        bk = {i: [C.ps(sk, f"p2_mm{i}", [128, 512]), None] for i in range(2)}
        deltarow = C.sb(sk, "p2_delta", [1, TT], F32)
        ms0 = C.ps(sk, "p2_ms0", [16, 512])
        ms1 = C.ps(sk, "p2_ms1", [16, 512])

        stream = C.t["STREAM"]
        st_in = stash["L0_attn_W_in"]
        st_wd = stash["L0_attn_Wd"]
        st_wo = stash["L0_attn_W_out"]
        rv_in = st_in[0].rearrange("(ic p) o -> p ic o", p=128)
        lng = C.t["lng_c"]
        lnb = C.t["lnb_c"]
        cw = C.t["convw_c"]
        cb = C.t["convb_c"]
        dvc = C.t["dvec_c"]
        bdc = C.t["bd_c"]
        bmat = C.t["BMAT_BF"]
        ctbf = C.t["CT_BF"]
        auxf = C.t["auxf"]

        if DEBUG_P2:
            C.t["DBG_YG"] = C.nc.dram_tensor("dbg_yg", [NT, 128, 16 * TT], BF16,
                                             kind="ExternalOutput")
        pl = Plan(C)
        z16_ev = pl.op("dve", lambda v: v.memset(z16[:, :], 0.0), inc=True)

        hl1 = emit_ln(C, pl, P, src_fn=lambda dc: stream[:, dc, TPC - 3:TPC],
                      n_dc=8, tw=3, dst_fn=lambda dc: hln1[:, dc, :], src_is_f32=True)
        hl2 = emit_ln(C, pl, P, src_fn=lambda dc: hln1[:, dc, :],
                      n_dc=8, tw=3, dst_fn=lambda dc: hln2[:, dc, :], src_is_f32=False,
                      gamma=lambda dc: lng[:, dc:dc + 1], beta=lambda dc: lnb[:, dc:dc + 1])
        wev_in = [None, None]
        hxs_evs = []
        for gi in range(8):
            bi = gi % 2
            ld = pl.op("sdma", lambda e, gi=gi, bi=bi: e.dma_start(
                wb_in[bi][:, :, :], rv_in[:, :, gi * 256:(gi + 1) * 256]),
                deps=[wev_in[bi]] + list(st_in[1]))
            for oj in range(2):
                oc = gi * 2 + oj
                nb = oc % 2
                mm_ev = pl.op("mm", lambda t, bi=bi, oj=oj, nb=nb: [
                    t.matmul(bk[nb][0][:, 0:3], wb_in[bi][:, ic, oj * 128:(oj + 1) * 128],
                             hln2[:, ic, :], start=(ic == 0), stop=(ic == 7))
                    for ic in range(8)][-1],
                    deps=[ld, bk[nb][1]] + hl2, inc=True)
                cev = pl.op("dve", lambda v, oc=oc, nb=nb: v.tensor_copy(
                    hxs[:, oc, :], bk[nb][0][:, 0:3]), deps=[mm_ev], inc=True)
                bk[nb][1] = cev
                hxs_evs.append(cev)
                wev_in[bi] = cev
        def fn_send_halo(g):
            with g.register(_rn("hoff")) as hoff:
                g.reg_load(hoff, C.t["auxi"][0:1, 2:3])
                return g.dma_start(bass.AP(C.t["SH_HALO"], hoff, [[48, 128], [1, 48]]),
                                   hxs.rearrange("p a b -> p (a b)")[:, :])
        e_send = pl.op("gdma", fn_send_halo, deps=hxs_evs)
        emit_pair_signal(C, pl, 0, [e_send])
        w0 = emit_pair_wait(C, pl, 0)

        def fn_recv_halo(g):
            with g.register(_rn("hoff")) as hoff:
                g.reg_load(hoff, C.t["auxi"][0:1, 3:4])
                return g.dma_start(halo3.rearrange("p a b -> p (a b)")[:, :],
                                   bass.AP(C.t["SH_HALO"], hoff, [[48, 128], [1, 48]]))
        e_recv = pl.op("gdma", fn_recv_halo, deps=[w0])
        e_mask = pl.op("dve", lambda v: v.tensor_scalar(
            halo3.rearrange("p a b -> p (a b)")[:, :],
            halo3.rearrange("p a b -> p (a b)")[:, :],
            auxf[:, 0:1], None, ALU.mult), deps=[e_recv], inc=True)

        wev_d = [None, None]
        hs_last = None
        ac_last = None
        scan_evs = []
        xsp_tail_ev = None
        spill_evs = []
        sp_l2 = sp_gate = sp_xs = None
        for tile in range(NT):
            t0 = tile * TT
            l1 = emit_ln(C, pl, P, src_fn=lambda dc, t0=t0: stream[:, dc, t0:t0 + TT],
                         n_dc=8, tw=TT, dst_fn=lambda dc: lnx1[:, dc, :], src_is_f32=True,
                         dst_war=[sp_l2])
            l2 = emit_ln(C, pl, P, src_fn=lambda dc: lnx1[:, dc, :],
                         n_dc=8, tw=TT, dst_fn=lambda dc: lnx2[:, dc, :], src_is_f32=False,
                         gamma=lambda dc: lng[:, dc:dc + 1],
                         beta=lambda dc: lnb[:, dc:dc + 1])
            sp_l2 = pl.op("gdma", lambda g, tile=tile: g.dma_start(
                C.t["SPILL_LNX2"][tile, :, :],
                lnx1.rearrange("p a b -> p (a b)")[:, :]),
                deps=l2 + l1)

            if tile == 0:
                e_h = pl.op("dve", lambda v: v.tensor_copy(
                    xsp[:, :, 0:3], halo3[:, :, :]),
                    deps=[e_mask, xsp_tail_ev], inc=True)
            else:
                e_h = pl.op("dve", lambda v: v.tensor_copy(
                    xsp[:, :, 0:3], xsp[:, :, TT:TT + 3]),
                    deps=[xsp_tail_ev], inc=True)

            xs_evs = []
            gate_evs = []
            for gi in range(16):
                bi = gi % 2
                ld = pl.op("sdma", lambda e, gi=gi, bi=bi: e.dma_start(
                    wb_in[bi][:, :, :], rv_in[:, :, gi * 256:(gi + 1) * 256]),
                    deps=[wev_in[bi]] + list(st_in[1]))
                for oj in range(2):
                    oc = gi * 2 + oj
                    nb = oc % 2
                    mm_ev = pl.op("mm", lambda t, bi=bi, oj=oj, nb=nb: [
                        t.matmul(bk[nb][0][:, :], wb_in[bi][:, ic, oj * 128:(oj + 1) * 128],
                                 lnx2[:, ic, :], start=(ic == 0), stop=(ic == 7))
                        for ic in range(8)][-1],
                        deps=[ld, bk[nb][1]] + l2, inc=True)
                    if oc < 16:
                        cev = pl.op("act", lambda s, oc=oc, nb=nb: s.activation(
                            xsp[:, oc, 3:3 + TT], bk[nb][0][:, :], AF.Copy),
                            deps=[mm_ev, e_h], inc=True)
                        xs_evs.append(cev)
                    else:
                        cev = pl.op("act", lambda s, oc=oc, nb=nb: s.activation(
                            gatb[:, oc - 16, :], bk[nb][0][:, :], AF.Copy),
                            deps=[mm_ev, sp_gate], inc=True)
                        gate_evs.append(cev)
                    bk[nb][1] = cev
                    wev_in[bi] = cev
            sp_gate = pl.op("gdma", lambda g, tile=tile: g.dma_start(
                C.t["SPILL_GATE"][tile, :, :],
                gatb.rearrange("p a b -> p (a b)")[:, :]), deps=gate_evs)

            conv_evs = []
            prev_cc = None
            for cc in range(16):
                e = pl.op("dve", lambda v, cc=cc: v.tensor_scalar(
                    convs[:, :], xsp[:, cc, 0:TT], cw[:, cc * 4:cc * 4 + 1], None,
                    ALU.mult), deps=xs_evs + [e_h, prev_cc], inc=True)
                for k in (1, 2, 3):
                    e = pl.op("dve", lambda v, cc=cc, k=k: v.scalar_tensor_tensor(
                        convs[:, :], xsp[:, cc, k:k + TT], cw[:, cc * 4 + k:cc * 4 + k + 1],
                        convs[:, :], ALU.mult, ALU.add), deps=[e], inc=True)
                e = pl.op("act", lambda s, cc=cc: s.activation(
                    xs[:, cc, :], convs[:, :], AF.Silu, bias=cb[:, cc:cc + 1]),
                    deps=[e, sp_xs], inc=True)
                prev_cc = e
                conv_evs.append(e)
            xsp_tail_ev = conv_evs[-1]
            sp_xs = pl.op("gdma", lambda g, tile=tile: g.dma_start(
                C.t["SPILL_XS"][tile, :, :],
                xs.rearrange("p a b -> p (a b)")[:, :]), deps=conv_evs)
            spill_evs += [sp_l2, sp_gate, sp_xs]

            rv_d = st_wd[0].rearrange("(ic p) o -> p ic o", p=128)
            dsum_ev = None
            sp_prev = [None, None]
            for gi in range(16):
                bi = gi % 2
                ld = pl.op("sdma", lambda e, gi=gi, bi=bi: e.dma_start(
                    wb_d[bi][:, :, :], rv_d[:, :, gi * 128:(gi + 1) * 128]),
                    deps=[wev_d[bi]] + list(st_wd[1]))
                for oj in range(1):
                    oc = gi
                    nb = oc % 2
                    mm_ev = pl.op("mm", lambda t, bi=bi, oj=oj, nb=nb: [
                        t.matmul(bk[nb][0][:, :], wb_d[bi][:, ic, oj * 128:(oj + 1) * 128],
                                 xs[:, ic, :], start=(ic == 0), stop=(ic == 15))
                        for ic in range(16)][-1],
                        deps=[ld, bk[nb][1]] + conv_evs, inc=True)
                    e1 = pl.op("act", lambda s, oc=oc, nb=nb: s.activation(
                        spf[:, :], bk[nb][0][:, :], AF.Exp, bias=bdc[:, oc:oc + 1]),
                        deps=[mm_ev], inc=True)
                    si = oc % 2
                    e2 = pl.op("act", lambda s, si=si: s.activation(
                        spp[:, si, :], spf[:, :], AF.Ln, bias=1.0),
                        deps=[e1, sp_prev[si]], inc=True)
                    dsum_ev = pl.op("mm", lambda t, si=si, oc=oc: t.matmul(
                        P["st0"][0:1, :], C.t["ONES_BF"][:, 0:1], spp[:, si, :],
                        start=(oc == 0), stop=(oc == 15)),
                        deps=[e2, P["st_free"] if oc == 0 else None], inc=True)
                    sp_prev[si] = dsum_ev
                    bk[nb][1] = e1
                    wev_d[bi] = e1
            delta = deltarow[0:1, :]
            d_ev = pl.op("dve", lambda v: v.tensor_scalar(
                delta, P["st0"][0:1, :], 1.0 / DI, None, ALU.mult),
                deps=[dsum_ev], inc=True)
            P["st_free"] = d_ev

            m_da = pl.op("mm", lambda t: t.matmul(ms0[:, :], C.t["A_ROW"][0:1, :], delta),
                         deps=[d_ev, C.t.get("_ms0_free")], inc=True)
            e_da = pl.op("act", lambda s: s.activation(dat[:, :], ms0[:, :], AF.Exp),
                         deps=[m_da], inc=True)
            m_db = pl.op("mm", lambda t: t.matmul(ms1[:, :], C.t["ONE1_F"][0:1, 0:16],
                                                  delta),
                         deps=[d_ev, C.t.get("_ms1_free")], inc=True)
            m_bx = pl.op("mm", lambda t: [
                t.matmul(ms0[:, :], bmat[:, cc * 16:(cc + 1) * 16], xs[:, cc, :],
                         start=(cc == 0), stop=(cc == 15)) for cc in range(16)][-1],
                deps=conv_evs + [e_da], inc=True)
            e_bx = pl.op("act", lambda s: s.activation(bxs[:, :], ms0[:, :], AF.Copy),
                         deps=[m_bx], inc=True)
            C.t["_ms0_free"] = e_bx
            e_bxd = pl.op("dve", lambda v: v.tensor_tensor(bxd[:, :], bxs[:, :],
                                                           ms1[:, :], ALU.mult),
                          deps=[e_bx, m_db], inc=True)
            C.t["_ms1_free"] = e_bxd

            hs_last = pl.op("dve", lambda v, t0=t0, tile=tile: v.tensor_tensor_scan(
                hs[:, t0:t0 + TT], dat[:, :], bxd[:, :],
                (0.0 if tile == 0 else hs[:, t0 - 1:t0]), ALU.mult, ALU.add),
                deps=[e_da, e_bxd, hs_last], inc=True)
            ac_last = pl.op("dve", lambda v, t0=t0, tile=tile: v.tensor_tensor_scan(
                acum[:, t0:t0 + TT], dat[:, :], z16[:, :],
                (1.0 if tile == 0 else acum[:, t0 - 1:t0]), ALU.mult, ALU.add),
                deps=[e_da, z16_ev, ac_last], inc=True)
            scan_evs += [hs_last, ac_last]

        def fn_send_h(g):
            with g.register(_rn("ho")) as ho:
                g.reg_load(ho, C.t["auxi"][0:1, 4:5])
                return g.dma_start(bass.AP(C.t["SH_HEND"], ho, [[1, 16], [1, 1]]),
                                   hs[:, TPC - 1:TPC])
        e_sh = pl.op("gdma", fn_send_h, deps=[hs_last])
        emit_pair_signal(C, pl, 1, [e_sh])
        w1 = emit_pair_wait(C, pl, 1)

        def fn_recv_h(g):
            with g.register(_rn("ho")) as ho:
                g.reg_load(ho, C.t["auxi"][0:1, 5:6])
                return g.dma_start(h0r[:, :],
                                   bass.AP(C.t["SH_HEND"], ho, [[1, 16], [1, 1]]))
        e_rh = pl.op("gdma", fn_recv_h, deps=[w1])
        e_h0 = pl.op("dve", lambda v: v.tensor_tensor(
            h0e[:, :], h0r[:, :], auxf[0:16, 0:1], ALU.mult), deps=[e_rh], inc=True)

        wev_o = [None, None]
        rv_o = st_wo[0].rearrange("(ic p) o -> p ic o", p=128)
        tile_done = None
        for tile in range(NT):
            t0 = tile * TT
            rl_xs = pl.op("gdma", lambda g, tile=tile: g.dma_start(
                xs.rearrange("p a b -> p (a b)")[:, :],
                C.t["SPILL_XS"][tile, :, :]), deps=spill_evs + [tile_done])
            rl_gat = pl.op("gdma", lambda g, tile=tile: g.dma_start(
                gatb.rearrange("p a b -> p (a b)")[:, :],
                C.t["SPILL_GATE"][tile, :, :]), deps=spill_evs + [tile_done])
            rl_ln = pl.op("gdma", lambda g, tile=tile: g.dma_start(
                lnx2.rearrange("p a b -> p (a b)")[:, :],
                C.t["SPILL_LNX2"][tile, :, :]), deps=spill_evs + [tile_done])
            e_hsc = pl.op("dve", lambda v, t0=t0: v.scalar_tensor_tensor(
                hscb[:, :], acum[:, t0:t0 + TT], h0e[:, 0:1], hs[:, t0:t0 + TT],
                ALU.mult, ALU.add), deps=[e_h0] + scan_evs, inc=True)

            yg_evs = []
            prev_y = None
            for cc in range(16):
                nb = cc % 2
                m = pl.op("mm", lambda t, cc=cc, nb=nb: t.matmul(
                    bk[nb][0][:, :], ctbf[:, cc * 128:(cc + 1) * 128], hscb[:, :]),
                    deps=[e_hsc, bk[nb][1], C.t["_consts_ev"]], inc=True)
                e_y = pl.op("dve", lambda v, cc=cc, nb=nb: v.scalar_tensor_tensor(
                    convs[:, :], xs[:, cc, :], dvc[:, cc:cc + 1], bk[nb][0][:, :],
                    ALU.mult, ALU.add), deps=[m, rl_xs, prev_y], inc=True)
                e_sg = pl.op("act", lambda s, cc=cc: s.activation(
                    spf[:, :], gatb[:, cc, :], AF.Silu), deps=[rl_gat, prev_y], inc=True)
                e_yg = pl.op("dve", lambda v, cc=cc: v.tensor_tensor(
                    gatb[:, cc, :], convs[:, :], spf[:, :], ALU.mult),
                    deps=[e_y, e_sg], inc=True)
                bk[nb][1] = e_y
                prev_y = e_yg
                yg_evs.append(e_yg)

            if DEBUG_P2:
                pl.op("gdma", lambda g, tile=tile: g.dma_start(
                    C.t["DBG_YG"][tile, :, :],
                    gatb.rearrange("p a b -> p (a b)")[:, :]), deps=yg_evs)
            for gi in range(8):
                bi = gi % 2
                ld = pl.op("sdma", lambda e, gi=gi, bi=bi: e.dma_start(
                    wb_o[bi][:, :, :], rv_o[:, :, gi * 128:(gi + 1) * 128]),
                    deps=[wev_o[bi]] + list(st_wo[1]))
                for oj in range(1):
                    oc = gi
                    nb = oc % 2
                    m = pl.op("mm", lambda t, bi=bi, oj=oj, nb=nb: [
                        t.matmul(bk[nb][0][:, :], wb_o[bi][:, ic, oj * 128:(oj + 1) * 128],
                                 gatb[:, ic, :], start=(ic == 0), stop=(ic == 15))
                        for ic in range(16)][-1],
                        deps=[ld, bk[nb][1]] + yg_evs, inc=True)
                    e1 = pl.op("dve", lambda v, oc=oc, nb=nb: v.tensor_tensor(
                        convs[:, :], bk[nb][0][:, :], lnx2[:, oc, :], ALU.add),
                        deps=[m, rl_ln], inc=True)
                    e2 = pl.op("dve", lambda v, oc=oc, t0=t0: v.tensor_tensor(
                        stream[:, oc, t0:t0 + TT], stream[:, oc, t0:t0 + TT],
                        convs[:, :], ALU.add), deps=[e1], inc=True)
                    bk[nb][1] = e1
                    wev_o[bi] = e2
                    tile_done = e2
        if DEBUG_P2:
            nc = C.nc
            d_hs = nc.dram_tensor("dbg_hs", [16, TPC], F32, kind="ExternalOutput")
            d_ac = nc.dram_tensor("dbg_ac", [16, TPC], F32, kind="ExternalOutput")
            d_xs = nc.dram_tensor("dbg_xs", [NT, 128, 16 * TT], BF16, kind="ExternalOutput")
            d_gt = nc.dram_tensor("dbg_gt", [NT, 128, 16 * TT], BF16, kind="ExternalOutput")
            d_ln = nc.dram_tensor("dbg_ln", [NT, 128, 8 * TT], BF16, kind="ExternalOutput")
            d_h0 = nc.dram_tensor("dbg_h0", [16, 1], F32, kind="ExternalOutput")
            pl.op("gdma", lambda g: g.dma_start(d_hs[:, :], hs[:, :]), deps=scan_evs)
            pl.op("gdma", lambda g: g.dma_start(d_ac[:, :], acum[:, :]), deps=scan_evs)
            pl.op("gdma", lambda g: g.dma_start(d_xs[:, :, :], C.t["SPILL_XS"][:, :, :]),
                  deps=spill_evs)
            pl.op("gdma", lambda g: g.dma_start(d_gt[:, :, :], C.t["SPILL_GATE"][:, :, :]),
                  deps=spill_evs)
            pl.op("gdma", lambda g: g.dma_start(d_ln[:, :, :], C.t["SPILL_LNX2"][:, :, :]),
                  deps=spill_evs)
            pl.op("gdma", lambda g: g.dma_start(d_h0[:, :], h0e[:, :]), deps=[e_h0])
        pl.run()


def emit_p5_prep(C):
    with ExitStack() as sk:
        lnk = C.sb(sk, "p5_lnk", [128, 8, TT], BF16)
        tmb = [C.sb(sk, f"p5_tmb{i}", [128, 1024], BF16) for i in range(2)]
        P = {"st0": C.ps(sk, "p5p_st0", [128, 512]),
             "st1": C.ps(sk, "p5p_st1", [128, 512]),
             "bc0": C.ps(sk, "p5p_bc0", [128, 512]),
             "bc1": C.ps(sk, "p5p_bc1", [128, 512]),
             "st_free": None, "bc_free": None}
        ps_tr = [C.ps(sk, f"p5p_tr{i}", [128, 4, 128], BF16) for i in range(2)]
        stream = C.t["STREAM"]
        kstage = C.nc.dram_tensor("KSTAGE", [TPC, D], BF16)

        pl = Plan(C)
        key_evs = []
        for tile in range(NT):
            t0 = tile * TT
            ln_evs = emit_ln(C, pl, P,
                             src_fn=lambda dc, t0=t0: stream[:, dc, t0:t0 + TT],
                             n_dc=8, tw=TT, dst_fn=lambda dc: lnk[:, dc, :],
                             src_is_f32=True)
            for tb in range(4):
                bi = tb % 2
                evac = []
                ev_e = None
                for dc in range(8):
                    pb = ps_tr[dc % 2]
                    slot = (dc // 2) % 4
                    tr = pl.op("mm", lambda t, tb=tb, dc=dc, pb=pb, slot=slot:
                               t.transpose(pb[:, slot, :],
                                           lnk[:, dc, tb * 128:(tb + 1) * 128],
                                           C.t["ident_bf"][:, :]),
                               deps=ln_evs + [ev_e, key_evs[-2] if len(key_evs) >= 2
                                              else None], inc=True)
                    ev_e = pl.op("dve", lambda v, bi=bi, dc=dc, pb=pb, slot=slot:
                                 v.tensor_copy(tmb[bi][:, dc * 128:(dc + 1) * 128],
                                               pb[:, slot, :]), deps=[tr], inc=True)
                    evac.append(ev_e)

                key_evs.append(pl.op("gdma", lambda g, t0=t0, tb=tb, bi=bi:
                                     g.dma_start(kstage[t0 + tb * 128:
                                                        t0 + (tb + 1) * 128, :],
                                                 tmb[bi][:, :]), deps=evac))
        def fn_pub(g):
            with g.register(_rn("ko")) as ko:
                g.reg_load(ko, C.t["auxi"][0:1, 0:1])
                return g.dma_start(bass.AP(C.t["SH_KEYS"], ko, [[D, TPC], [1, D]]),
                                   kstage[:, :])
        key_evs = [pl.op("gdma", fn_pub, deps=key_evs)]
        sp_evs = []
        for dc in range(8):
            sp_evs.append(pl.op("gdma", lambda g, dc=dc: g.dma_start(
                C.t["STREAMD"][dc, :, :], stream[:, dc, :]), inc=True))
        C.t["_keys_sig"] = emit_pair_signal(C, pl, 2, key_evs + sp_evs)
        C.t["_keys_evs"] = tuple(key_evs) + tuple(sp_evs)

        pl.run()


def emit_p5_main(C):
    with ExitStack() as sk:
        kfm = C.sb(sk, "p5_kfm", [128, 8, 2 * TPC], BF16)
        ktm = C.sb(sk, "p5_ktm", [128, 32, 1024], BF16)
        ebuf = C.sb(sk, "p5_e", [128, 4096], F32)
        ek = C.sb(sk, "p5_ek", [128, 4096], BF16)
        ekt = C.sb(sk, "p5_ekt", [128, 32, 128], BF16)
        att = C.sb(sk, "p5_att", [128, 1024], F32)
        stile = C.sb(sk, "p5_stile", [128, 8, 128], F32)
        zsp = C.sb(sk, "p5_zsp", [128, 8], F32)
        rc = C.sb(sk, "p5_rc", [128, 8], F32)
        wrows = C.sb(sk, "p5_wrows", [128, 2, 1024], F32)
        wrsc = C.sb(sk, "p5_wrsc", [128, 8], F32)
        wrs8 = C.sb(sk, "p5_wrs8", [8, 128], F32)
        wrow = C.sb(sk, "p5_wrow", [1, 1024], F32)
        wrsbc = C.sb(sk, "p5_wrsbc", [128, 1024], BF16)
        sc = [C.ps(sk, f"p5_sc{i}", [128, 512]) for i in range(2)]
        at = [C.ps(sk, f"p5_at{i}", [128, 512]) for i in range(2)]
        trb = [C.ps(sk, f"p5_trb{i}", [128, 4, 128], BF16) for i in range(2)]
        tra = [C.ps(sk, f"p5_tra{i}", [128, 4, 128]) for i in range(2)]

        wrsd = C.nc.dram_tensor("WRS_D", [8, 128], F32)
        pl = Plan(C)
        rs_evs = []
        lw = None
        for half in range(4):
            lw = pl.op("gdma", lambda g, half=half: g.dma_start(
                wrows[:, :, :],
                C.t["ext_L1_attn_W"].rearrange("(a p) b -> p a b", p=128)
                [:, half * 2:half * 2 + 2, :]), deps=[rs_evs[-1] if rs_evs else None])
            for j in range(2):
                rs_evs.append(pl.op("dve", lambda v, half=half, j=j: v.reduce_sum(
                    wrsc[:, half * 2 + j:half * 2 + j + 1], wrows[:, j, :], axis=AX.X),
                    deps=[lw], inc=True))
        trw = pl.op("mm", lambda t: t.transpose(sc[0][0:8, 0:128], wrsc[:, :],
                                                C.t["ident_f"][:, :]),
                    deps=rs_evs, inc=True)
        evw = pl.op("dve", lambda v: v.tensor_copy(wrs8[:, :], sc[0][0:8, 0:128]),
                    deps=[trw], inc=True)
        stw = pl.op("gdma", lambda g: g.dma_start(wrsd[:, :], wrs8[:, :]), deps=[evw])
        ldw = pl.op("gdma", lambda g: g.dma_start(
            wrow[0:1, :], wrsd.rearrange("a b -> (a b)")[None, :]), deps=[stw])
        bcw = pl.op("mm", lambda t: [
            t.matmul(sc[0][:, 0:512], C.t["ONE1_F"][0:1, :], wrow[0:1, 0:512]),
            t.matmul(sc[1][:, 0:512], C.t["ONE1_F"][0:1, :], wrow[0:1, 512:1024])][-1],
            deps=[ldw, evw], inc=True)
        wv1 = pl.op("dve", lambda v: v.tensor_copy(wrsbc[:, 0:512], sc[0][:, 0:512]),
                    deps=[bcw], inc=True)
        wv2 = pl.op("dve", lambda v: v.tensor_copy(wrsbc[:, 512:1024], sc[1][:, 0:512]),
                    deps=[bcw], inc=True)
        C.t["_wrs_ev"] = wv2
        C.t["_wrs_ev1"] = wv1
        w2 = emit_pair_wait(C, pl, 2)

        kfm_evs = []
        ktm_evs = []
        for half in range(2):
            acol = 0 if half == 0 else 1
            for dc in range(8):
                def fn_fm(s, half=half, dc=dc, acol=acol):
                    with s.register(_rn("ko")) as ko:
                        s.reg_load(ko, C.t["auxi"][0:1, acol:acol + 1])
                        s.reg_add(ko, ko, dc * 128)
                        return s.dma_start(
                            kfm[:, dc, half * TPC:(half + 1) * TPC],
                            bass.AP(C.t["SH_KEYS"], ko, [[D, TPC], [1, 128]]),
                            transpose=True)
                kfm_evs.append(pl.op("sdma", fn_fm,
                                      deps=[w2] + list(C.t["_keys_evs"])))

            def fn_tm(g, half=half, acol=acol):
                with g.register(_rn("ko")) as ko:
                    g.reg_load(ko, C.t["auxi"][0:1, acol:acol + 1])
                    return g.dma_start(
                        ktm[:, half * 16:(half + 1) * 16, :],
                        bass.AP(C.t["SH_KEYS"], ko, [[D, 128], [128 * D, 16], [1, D]]))
            ktm_evs.append(pl.op("gdma", fn_tm,
                                  deps=[w2] + list(C.t["_keys_evs"])))

        st_prev = [None] * 4
        all_stores = []
        for qt in range(16):
            q0 = qt * 128
            e_evs = []
            for ks in range(8):
                nb = ks % 2
                m = pl.op("mm", lambda t, q0=q0, ks=ks, nb=nb: [
                    t.matmul(sc[nb][:, :], kfm[:, dc, q0:q0 + 128],
                             kfm[:, dc, ks * 512:(ks + 1) * 512],
                             start=(dc == 0), stop=(dc == 7)) for dc in range(8)][-1],
                    deps=kfm_evs + [e_evs[-2] if len(e_evs) >= 2 else None,
                                    C.t["_wrs_ev"], C.t["_wrs_ev1"]], inc=True)
                e_evs.append(pl.op("act", lambda s, ks=ks, nb=nb: s.activation(
                    ebuf[:, ks * 512:(ks + 1) * 512], sc[nb][:, :], AF.Exp,
                    scale=1.0 / 32.0, accum_out=zsp[:, ks:ks + 1]), deps=[m], inc=True))
            rz = pl.op("dve", lambda v: v.reduce_sum(rc[:, 0:1], zsp[:, :], axis=AX.X),
                       deps=e_evs, inc=True)
            rt = pl.op("dve", lambda v: v.tensor_scalar(rc[:, 1:2], rc[:, 0:1], 0.01,
                                                        None, ALU.mult),
                       deps=[rz], inc=True)
            rek = pl.op("dve", lambda v: v.scalar_tensor_tensor(
                ek[:, :], ebuf[:, :], rc[:, 1:2], ebuf[:, :], ALU.is_ge, ALU.mult,
                accum_out=rc[:, 2:3]), deps=[rt] + e_evs, inc=True)
            r1 = pl.op("dve", lambda v: v.tensor_scalar(rc[:, 3:4], rc[:, 0:1], 1e-9,
                                                        None, ALU.mult), deps=[rz], inc=True)
            r2 = pl.op("dve", lambda v: v.tensor_tensor(rc[:, 3:4], rc[:, 3:4],
                                                        rc[:, 2:3], ALU.add),
                       deps=[r1, rek], inc=True)
            r3 = pl.op("dve", lambda v: v.reciprocal(rc[:, 4:5], rc[:, 3:4]),
                       deps=[r2], inc=True)
            r4 = pl.op("dve", lambda v: v.tensor_tensor(rc[:, 5:6], rc[:, 0:1],
                                                        rc[:, 2:3], ALU.subtract),
                       deps=[rek, rz], inc=True)
            r5 = pl.op("dve", lambda v: v.reciprocal(rc[:, 6:7], rc[:, 0:1]),
                       deps=[rz], inc=True)
            r6 = pl.op("dve", lambda v: v.tensor_tensor(rc[:, 5:6], rc[:, 5:6],
                                                        rc[:, 6:7], ALU.mult),
                       deps=[r4, r5], inc=True)
            r7 = pl.op("dve", lambda v: v.tensor_scalar(rc[:, 5:6], rc[:, 5:6], 0.01,
                                                        None, ALU.mult), deps=[r6], inc=True)

            ekt_evs = []
            ev_e = None
            for kg in range(8):
                pb = trb[kg % 2]
                for j in range(4):
                    kc = kg * 4 + j
                    tr = pl.op("mm", lambda t, kc=kc, j=j, pb=pb: t.transpose(
                        pb[:, j, :], ek[:, kc * 128:(kc + 1) * 128],
                        C.t["ident_bf"][:, :]),
                        deps=[rek, ekt_evs[-2] if len(ekt_evs) >= 2 else None], inc=True)
                ev_e = pl.op("dve", lambda v, kg=kg, pb=pb: v.tensor_copy(
                    ekt[:, kg * 4:(kg + 1) * 4, :].rearrange("p a b -> p (a b)"),
                    pb.rearrange("p a b -> p (a b)")[:, :]), deps=[tr], inc=True)
                ekt_evs.append(ev_e)

            att_evs = []
            for ds in range(2):
                m = pl.op("mm", lambda t, ds=ds: [
                    t.matmul(at[ds][:, :], ekt[:, kc, :],
                             ktm[:, kc, ds * 512:(ds + 1) * 512],
                             start=(kc == 0), stop=(kc == 31)) for kc in range(32)][-1],
                    deps=ekt_evs + ktm_evs + [st_prev[2 + ds]], inc=True)
                att_evs.append(pl.op("act", lambda s, ds=ds: s.activation(
                    att[:, ds * 512:(ds + 1) * 512], at[ds][:, :], AF.Copy,
                    scale=rc[:, 4:5]), deps=[m, r3], inc=True))
                st_prev[2 + ds] = att_evs[-1]
            cas = pl.op("dve", lambda v: v.scalar_tensor_tensor(
                att[:, :], wrsbc[:, :], rc[:, 5:6], att[:, :],
                ALU.mult, ALU.add),
                deps=att_evs + [r7, C.t["_wrs_ev"], C.t["_wrs_ev1"]], inc=True)

            ld_st = pl.op("gdma", lambda g, q0=q0: g.dma_start(
                stile.rearrange("p a b -> p (a b)")[:, :],
                C.t["STREAMD"].rearrange("d p t -> p d t")[:, :, q0:q0 + 128]),
                deps=[st_prev[0]] + list(C.t["_keys_evs"]))
            add_evs = []
            ev_e2 = None
            for dc in range(8):
                pb = tra[dc % 2]
                slot = (dc // 2) % 4
                tr = pl.op("mm", lambda t, dc=dc, pb=pb, slot=slot: t.transpose(
                    pb[:, slot, :], att[:, dc * 128:(dc + 1) * 128],
                    C.t["ident_f"][:, :]),
                    deps=[cas, ev_e2, add_evs[-2] if len(add_evs) >= 2 else None],
                    inc=True)
                ev_e2 = pl.op("dve", lambda v, dc=dc, pb=pb, slot=slot: v.tensor_tensor(
                    stile[:, dc, :], stile[:, dc, :], pb[:, slot, :], ALU.add),
                    deps=[tr, ld_st], inc=True)
                add_evs.append(ev_e2)
            st_st = pl.op("gdma", lambda g, q0=q0: g.dma_start(
                C.t["STREAMD"].rearrange("d p t -> p d t")[:, :, q0:q0 + 128],
                stile.rearrange("p a b -> p (a b)")[:, :]), deps=add_evs)
            st_prev[0] = st_st
            all_stores.append(st_st)
        C.t["_p5_done"] = pl.op("pool", lambda g: g.nop(), deps=all_stores, inc=True)
        pl.run()


COL_INPUTS = [
    ("lng_c", [128, 8]), ("lnb_c", [128, 8]), ("convw_c", [128, 64]),
    ("convb_c", [128, 16]), ("dvec_c", [128, 16]), ("bd_c", [128, 16]),
    ("w1s_c", [128, 32]), ("w2s_c", [128, 32]), ("h1s_c", [128, 32]),
    ("h2s_c", [128, 32]), ("w3s_c", [128, 8]), ("bmat_c", [128, 256]),
    ("auxf", [128, 2]),
]


def emit_consts(C, pl):
    evs = []
    for name, shape in COL_INPUTS:
        evs.append(pl.op("gdma", lambda g, n=name: g.dma_start(
            C.t[n][:, :], C.t["ext_" + n][:, :])))
    for name in ["ident_f", "cmat_t", "alog_r", "auxi", "syncv", "fconst"]:
        evs.append(pl.op("gdma", lambda g, n=name: g.dma_start(
            C.t[n][:, :], C.t["ext_" + n][:, :])))
    e1 = pl.op("dve", lambda v: v.memset(C.t["ONES_BF"][:, :], 1.0), inc=True)
    e2 = pl.op("dve", lambda v: v.memset(C.t["ONE1_F"][:, :], 1.0), inc=True)
    e3 = pl.op("dve", lambda v: v.tensor_copy(C.t["ident_bf"][:, :],
                                              C.t["ident_f"][:, :]),
               deps=[evs[-6]], inc=True)
    e4 = pl.op("dve", lambda v: v.tensor_scalar(C.t["w3sh_c"][:, :], C.t["w3s_c"][:, :],
                                                0.5, None, ALU.mult),
               deps=[evs[10]], inc=True)
    e5 = pl.op("act", lambda s: s.activation(C.t["A_ROW"][0:1, :], C.t["alog_r"][0:1, :],
                                             AF.Exp), deps=[evs[-4], evs[-1]], inc=True)
    e6 = pl.op("dve", lambda v: v.tensor_scalar(C.t["A_ROW"][0:1, :],
                                                C.t["A_ROW"][0:1, :], -1.0, None,
                                                ALU.mult), deps=[e5], inc=True)
    e7 = pl.op("dve", lambda v: v.tensor_copy(C.t["CT_BF"][:, :], C.t["cmat_t"][:, :]),
               deps=[evs[-5]], inc=True)
    e8 = pl.op("dve", lambda v: v.tensor_copy(C.t["BMAT_BF"][:, :], C.t["bmat_c"][:, :]),
               deps=[evs[11]], inc=True)
    allc = evs + [e1, e2, e3, e4, e6, e7, e8]
    C.t["_consts_ev"] = pl.op("pool", lambda g: g.nop(), deps=allc, inc=True)
    return allc


def emit_load_stream(C, pl, ps_a, ps_b, src_ext, deps=(), ld_kind="gdma"):
    xt = C.t["XT"]
    nb = xt.shape[1]
    stream = C.t["STREAM"]
    last = []
    ld_prev = [None, None]
    ev_evac = None
    for ti in range(16):
        bi = ti % nb
        ld = pl.op(ld_kind, lambda g, ti=ti, bi=bi: g.dma_start(
            xt[:, bi, :], src_ext[ti * 128:(ti + 1) * 128, :]),
            deps=tuple(deps) + (ld_prev[bi],))
        for dc in range(8):
            pb = ps_a if dc % 2 == 0 else ps_b
            tr = pl.op("mm", lambda t, ti=ti, bi=bi, dc=dc, pb=pb: t.transpose(
                pb[:, :], xt[:, bi, dc * 128:(dc + 1) * 128], C.t["ident_f"][:, :]),
                deps=[ld, ev_evac], inc=True)
            ev_evac = pl.op("dve", lambda v, ti=ti, dc=dc, pb=pb: v.tensor_copy(
                stream[:, dc, ti * 128:(ti + 1) * 128], pb[:, :]),
                deps=[tr], inc=True)
            ld_prev[bi] = ev_evac
        last.append(ev_evac)
    return last


def emit_store_stream(C, pl, ps_a, ps_b, dst_ext, deps=()):
    ob = C.t["OBUF"]
    stream = C.t["STREAM"]
    st_prev = [None, None]
    ev_evac = None
    outs = []
    for ti in range(16):
        bi = ti % 2
        evs = []
        for dc in range(8):
            pb = ps_a if dc % 2 == 0 else ps_b
            tr = pl.op("mm", lambda t, ti=ti, dc=dc, pb=pb: t.transpose(
                pb[:, :], stream[:, dc, ti * 128:(ti + 1) * 128], C.t["ident_f"][:, :]),
                deps=tuple(deps) + (ev_evac, st_prev[bi]), inc=True)
            ev_evac = pl.op("dve", lambda v, ti=ti, bi=bi, dc=dc, pb=pb: v.tensor_copy(
                ob[:, bi, dc * 128:(dc + 1) * 128], pb[:, :]), deps=[tr], inc=True)
            evs.append(ev_evac)
        st = pl.op("gdma", lambda g, ti=ti, bi=bi: g.dma_start(
            dst_ext[ti * 128:(ti + 1) * 128, :], ob[:, bi, :]), deps=evs)
        st_prev[bi] = st
        outs.append(st)
    pl.op("pool", lambda g: g.nop(), deps=outs)
    return outs


def _cols(v):
    v = np.asarray(v, np.float32).reshape(-1)
    n = v.size // 128
    return np.ascontiguousarray(v.reshape(n, 128).T)


def stage_inputs(inputs):
    nonce = int(np.random.randint(1, 1 << 30))
    x = np.asarray(inputs["L0_attn_x"], np.float32)
    conv_w = np.asarray(inputs["L0_attn_conv_w"], np.float32)
    convw_c = np.ascontiguousarray(
        conv_w[:, 0, :].reshape(16, 128, 4).transpose(1, 0, 2).reshape(128, 64))
    common = {
        "ext_lng_c": _cols(inputs["L0_attn_ln_gamma"]),
        "ext_lnb_c": _cols(inputs["L0_attn_ln_beta"]),
        "ext_convw_c": convw_c,
        "ext_convb_c": _cols(inputs["L0_attn_conv_b"]),
        "ext_dvec_c": _cols(inputs["L0_attn_D_vec"]),
        "ext_bd_c": _cols(inputs["L0_attn_bd"]),
        "ext_w1s_c": _cols(inputs["L0_ffn_w1_s"]),
        "ext_w2s_c": _cols(inputs["L0_ffn_w2_s"]),
        "ext_h1s_c": _cols(inputs["L0_ffn_h1_s"]),
        "ext_h2s_c": _cols(inputs["L0_ffn_h2_s"]),
        "ext_w3s_c": _cols(inputs["L0_ffn_w3_s"]),
        "ext_bmat_c": np.ascontiguousarray(
            np.asarray(inputs["L0_attn_B_mat"], np.float32).T
            .reshape(16, 128, 16).transpose(1, 0, 2).reshape(128, 256)),
        "ext_cmat_t": np.ascontiguousarray(np.asarray(inputs["L0_attn_C_mat"],
                                                      np.float32).T),
        "ext_alog_r": np.asarray(inputs["L0_attn_A_log"], np.float32).reshape(1, 16),
        "ext_ident_f": np.eye(128, dtype=np.float32),
        "ext_syncv": (nonce + np.arange(8, dtype=np.int64)).astype(np.int32).reshape(1, 8),
        "ext_fconst": np.broadcast_to(np.asarray(FCONSTS, np.float32), (128, 16)).copy(),
    }
    for name in list(W_IMAJOR) + list(W_OMAJOR) + ["L1_attn_W"]:
        common["ext_" + name] = np.asarray(inputs[name], np.float32)
    in_maps = []
    for c in range(8):
        b, h = c >> 1, c & 1
        m = dict(common)
        m["xin"] = np.ascontiguousarray(x[b, h * TPC:(h + 1) * TPC, :])
        m["ext_auxf"] = np.ascontiguousarray(
            np.stack([np.full(128, float(h), np.float32),
                      np.zeros(128, np.float32)], axis=1))
        auxi = np.zeros((1, 16), np.int32)
        auxi[0, 0] = h * TPC * D
        auxi[0, 1] = (1 - h) * TPC * D
        auxi[0, 2] = h * 128 * 48
        auxi[0, 3] = (1 - h) * 128 * 48
        auxi[0, 4] = h * 16
        auxi[0, 5] = (1 - h) * 16
        auxi[0, 6] = h * 32
        auxi[0, 7] = (1 - h) * 32
        m["ext_auxi"] = auxi
        in_maps.append(m)
    return in_maps


def build_program(phases):
    nc = bass.Bass("TRN2", target_bir_lowering=False,
                   detect_race_conditions=RACE_DETECT)
    first, last = phases[0], phases[-1]
    xin = nc.dram_tensor("xin", [TPC, D], F32, kind="ExternalInput")
    xout = nc.dram_tensor("xout", [TPC, D], F32, kind="ExternalOutput")

    wnames = sorted({w for p in phases for w in PHASE_WEIGHTS[p]})
    if "p6" in phases and "p4" not in phases:
        wnames = sorted(set(wnames) | set(PHASE_WEIGHTS["p4"]))

    with ExitStack() as prog:
        C = Ctx(nc, prog)
        C.t["ext_xin"] = xin
        C.t["ext_xout"] = xout
        for name, shape in COL_INPUTS:
            C.t["ext_" + name] = nc.dram_tensor("ext_" + name, shape, F32,
                                                kind="ExternalInput")
            C.sb(prog, name, shape, F32)
        C.t["ext_ident_f"] = nc.dram_tensor("ext_ident_f", [128, 128], F32,
                                            kind="ExternalInput")
        C.sb(prog, "ident_f", [128, 128], F32)
        C.sb(prog, "ident_bf", [128, 128], BF16)
        C.t["ext_cmat_t"] = nc.dram_tensor("ext_cmat_t", [16, 2048], F32,
                                           kind="ExternalInput")
        C.sb(prog, "CT_BF", [16, 2048], BF16)
        C.sb(prog, "BMAT_BF", [128, 256], BF16)
        C.t["ext_alog_r"] = nc.dram_tensor("ext_alog_r", [1, 16], F32,
                                           kind="ExternalInput")
        C.sb(prog, "alog_r", [1, 16], F32)
        C.sb(prog, "A_ROW", [1, 16], F32)
        C.t["ext_auxi"] = nc.dram_tensor("ext_auxi", [1, 16], I32, kind="ExternalInput")
        C.sb(prog, "auxi", [1, 16], I32)
        C.t["ext_syncv"] = nc.dram_tensor("ext_syncv", [1, 8], I32, kind="ExternalInput")
        C.sb(prog, "syncv", [1, 8], I32)
        C.t["ext_fconst"] = nc.dram_tensor("ext_fconst", [128, 16], F32, kind="ExternalInput")
        fc = C.sb(prog, "fconst", [128, 16], F32)
        for i, v in enumerate(FCONSTS):
            nc.const_aps.register(F32, v, fc[:, i:i + 1])
        C.sb(prog, "FLAG_SB", [1, 4], I32)
        C.sb(prog, "ONES_BF", [128, 1], BF16)
        C.sb(prog, "ONE1_F", [1, 128], F32)
        C.sb(prog, "XB", [128, 2, TT], BF16)
        C.sb(prog, "XSQ", [128, 2, TT], BF16)
        C.sb(prog, "ROWS", [1, 4 * TT], F32)
        C.sb(prog, "SCR", [128, TT], F32)
        C.sb(prog, "w3sh_c", [128, 8], F32)
        for name in wnames:
            shape = (W_IMAJOR.get(name) or W_OMAJOR.get(name))
            C.t["ext_" + name] = nc.dram_tensor("ext_" + name, list(shape), F32,
                                                kind="ExternalInput")
        if "p5" in phases:
            C.t["ext_L1_attn_W"] = nc.dram_tensor("ext_L1_attn_W", [D, D], F32,
                                                  kind="ExternalInput")
        C.t["SH_FLAG"] = nc.dram_tensor("SH_FLAG", [2, 32], I32, addr_space="Shared")
        if "p2" in phases:
            C.t["SH_HALO"] = nc.dram_tensor("SH_HALO", [2, 128, 48], BF16,
                                            addr_space="Shared")
            C.t["SH_HEND"] = nc.dram_tensor("SH_HEND", [2, 16], F32,
                                            addr_space="Shared")
            C.t["SPILL_XS"] = nc.dram_tensor("SPILL_XS", [NT, 128, 16 * TT], BF16)
            C.t["SPILL_GATE"] = nc.dram_tensor("SPILL_GATE", [NT, 128, 16 * TT], BF16)
            C.t["SPILL_LNX2"] = nc.dram_tensor("SPILL_LNX2", [NT, 128, 8 * TT], BF16)
        if "p5" in phases:
            C.t["SH_KEYS"] = nc.dram_tensor("SH_KEYS", [2, TPC, D], BF16,
                                            addr_space="Shared")
        if "p5" in phases or "p6" in phases:
            C.t["STREAMD"] = nc.dram_tensor("STREAMD", [8, 128, TPC], F32)

        stash = {}
        sbuf_stream_needed = any(p in phases for p in ("p1", "p2", "p3", "p4", "p5"))
        fast_p1 = (first == "p1")
        with ExitStack() as smid:
            if sbuf_stream_needed or first == "p6":
                C.sb(smid, "STREAM", [128, 8, TPC], F32)
            with ExitStack() as s0:
                C.sb(s0, "cmat_t", [16, 2048], F32)
                if not fast_p1:
                    tscr = [C.sb(s0, f"tscr{i}", [128, 4096], BF16) for i in range(2)]
                if not fast_p1 and (sbuf_stream_needed or first == "p6"):
                    C.sb(s0, "XT", [128, 2, 1024], F32)
                    ps_a = C.ps(s0, "ps_la", [128, 128])
                    ps_b = C.ps(s0, "ps_lb", [128, 128])
                pl0 = Plan(C)
                emit_consts(C, pl0)
                if not fast_p1:
                    stash = emit_prepass(C, pl0, wnames, tscr, [None, None])
                    if sbuf_stream_needed or first == "p6":
                        ld_evs = emit_load_stream(C, pl0, ps_a, ps_b, xin)
                        if first == "p6":
                            sde = []
                            for dc in range(8):
                                sde.append(pl0.op("gdma", lambda g, dc=dc: g.dma_start(
                                    C.t["STREAMD"][dc, :, :], C.t["STREAM"][:, dc, :]),
                                    deps=ld_evs))
                            C.t["_p5_done"] = sde[-1]
                pl0.run()

            def later_names(idx):
                seen = set(stash)
                out = []
                for p in phases[idx + 1:]:
                    for w in PHASE_WEIGHTS[p]:
                        if w not in seen and w in wnames:
                            seen.add(w)
                            out.append(w)
                return out

            for pi, p in enumerate(phases):
                if p in ("p1", "p3"):
                    hooks = {}
                    if fast_p1 and p == "p1":
                        tev = [None, None]

                        def pre(pl, sk, P, bk, tev=tev):
                            C.t["_tscr"] = [C.sb(sk, f"f_tscr{i}", [128, 2048], BF16)
                                            for i in range(2)]
                            C.sb(sk, "XT", [128, 1, 1024], F32)
                            stash.update(emit_prepass(
                                C, pl, ["L0_ffn_w1_w", "L0_ffn_w2_w"],
                                C.t["_tscr"], tev, tq="adma"))
                            evs = emit_load_stream(C, pl, P["bc0"][:, 0:128],
                                                   P["bc1"][:, 0:128], xin,
                                                   ld_kind="sdma")
                            P["bc_free"] = evs[-1]
                            stash.update(emit_prepass(C, pl, ["L0_ffn_h1_w"],
                                                      C.t["_tscr"], tev, tq="adma"))

                        def u0(pl, tev=tev):
                            stash.update(emit_prepass(C, pl, ["L0_ffn_h2_w"],
                                                      C.t["_tscr"], tev, tq="adma"))

                        def h10(pl, tev=tev):
                            stash.update(emit_prepass(C, pl, ["L0_ffn_w3_w"],
                                                      C.t["_tscr"], tev, tq="adma"))

                        def h20(pl, tev=tev):
                            pass

                        def post(pl, pi=pi, tev=tev):
                            names = [n for n in later_names(pi) if n in W_IMAJOR]
                            stash.update(emit_prepass(C, pl, names,
                                                      C.t["_tscr"], tev, tq="adma"))
                        hooks = {"pre": pre, "u0": u0, "h10": h10, "h20": h20,
                                 "post": post}
                    if fast_p1 and p == "p3":
                        def pre3(pl, sk, P, bk):
                            C.t["_tscr3"] = [C.sb(sk, f"f3_tscr{i}", [128, 2048], BF16)
                                             for i in range(2)]

                        def post3(pl, pi=pi):
                            names = later_names(pi)
                            if names:
                                new = emit_prepass(C, pl, names, C.t["_tscr3"],
                                                   [None, None], tq="adma")
                                stash.update(new)
                                evs = [e for (_, ready) in new.values()
                                       for e in ready]
                                pl.op("pool", lambda g: g.nop(), deps=evs)
                        hooks = {"pre": pre3, "post": post3}
                    emit_p_l0ffn(C, stash, p, hooks=hooks)
                elif p == "p2":
                    emit_p2(C, stash)
                elif p == "p4":
                    emit_p_l1ffn(C, stash, "p4", None)
                elif p == "p5":
                    emit_p5_prep(C)
                elif p == "p6":
                    break

            if last in ("p1", "p2", "p3", "p4"):
                with ExitStack() as s9:
                    C.sb(s9, "OBUF", [128, 2, 1024], F32)
                    ps_a = C.ps(s9, "ps_sa", [128, 128])
                    ps_b = C.ps(s9, "ps_sb", [128, 128])
                    pl9 = Plan(C)
                    emit_store_stream(C, pl9, ps_a, ps_b, xout)
                    pl9.run()

        if "p5" in phases:
            emit_p5_main(C)
        if "p6" in phases:
            emit_p_l1ffn(C, stash, "p6", xout)
        if last == "p5":
            with ExitStack() as sl:
                C.t["STREAM"] = C.sb(sl, "STREAM5", [128, 8, TPC], F32)
                C.t["OBUF"] = C.sb(sl, "OBUF5", [128, 2, 1024], F32)
                ps_a = C.ps(sl, "ps_s5a", [128, 128])
                ps_b = C.ps(sl, "ps_s5b", [128, 128])
                pl = Plan(C)
                evs = []
                for dc in range(8):
                    evs.append(pl.op("gdma", lambda g, dc=dc: g.dma_start(
                        C.t["STREAM"][:, dc, :], C.t["STREAMD"][dc, :, :]),
                        deps=[C.t.get("_p5_done")]))
                emit_store_stream(C, pl, ps_a, ps_b, xout, deps=evs)
                pl.run()
    return nc


def kernel(**inputs):
    in_maps = stage_inputs(inputs)
    cur = [m["xin"] for m in in_maps]
    for phases in GROUPS:
        nc = build_program(phases)
        maps = []
        for c in range(8):
            m = dict(in_maps[c])
            m["xin"] = np.ascontiguousarray(cur[c])
            maps.append(m)
        res = run_bass_kernel_spmd(nc, maps, core_ids=list(range(8)))
        cur = [res.results[c]["xout"] for c in range(8)]
    out = np.empty((B, T, D), np.float32)
    for c in range(8):
        b, h = c >> 1, c & 1
        out[b, h * TPC:(h + 1) * TPC, :] = cur[c]
    return out
